# version 01
# baseline (speedup 1.0000x reference)
import contextlib
import numpy as np
import concourse.bass as bass
import concourse.mybir as mybir
from concourse.bass_utils import run_bass_kernel_spmd

F32 = mybir.dt.float32
BF16 = mybir.dt.bfloat16
AF = mybir.ActivationFunctionType
OP = mybir.AluOpType
AX = mybir.AxisListType

T = 2048
D = 1024
NH = 16
DH = 64
NCORE = 8
NEG = -1.0e30
ATTN_DEBUG = None
L2_SKIP = set()
L1A_PARTS = {'v', 'proj', 'scores', 'av', 'recip', 'oproj'}
RAW_WIN = 16


ENG_LIST = ("sp", "pool", "pe", "dve", "act")


class Res:
    __slots__ = ("name", "last_writer", "readers")

    def __init__(self, name=""):
        self.name = name
        self.last_writer = None
        self.readers = []


class Op:
    __slots__ = ("eng", "fn", "deps", "dma", "need_inc", "inc_val",
                 "dma_idx", "waits", "tag", "selfsync", "raw_same", "eseq")

    def __init__(self, eng, fn, dma, tag):
        self.selfsync = False
        self.raw_same = None
        self.eseq = 0
        self.eng = eng
        self.fn = fn
        self.dma = dma
        self.deps = set()
        self.need_inc = False
        self.inc_val = 0
        self.dma_idx = 0
        self.waits = []
        self.tag = tag


class P:
    def __init__(self, nc):
        self.nc = nc
        self.ops = []
        self.csem = {}
        self.dsem = {}
        self._es = contextlib.ExitStack()

    def sbuf(self, name, shape, dtype):
        return self.nc.alloc_sbuf_tensor(name, shape, dtype)

    def psum(self, name, shape, dtype=F32):
        return self.nc.alloc_psum_tensor(name, shape, dtype)

    def emit(self, eng, fn, r=(), w=(), deps=(), dma=False, tag="",
             selfsync=False):
        op = Op(eng, fn, dma, tag)
        op.selfsync = selfsync
        dd = op.deps
        for res in r:
            lw = res.last_writer
            if lw is not None:
                dd.add(lw)
                if lw.eng == eng and not lw.dma and eng in ("dve", "act", "pool"):
                    if op.raw_same is None:
                        op.raw_same = set()
                    op.raw_same.add(lw)
        for res in w:
            if res.last_writer is not None:
                dd.add(res.last_writer)
            dd.update(res.readers)
        for d in deps:
            if d is not None:
                dd.add(d)
        dd.discard(op)
        for res in r:
            res.readers.append(op)
        for res in w:
            res.last_writer = op
            res.readers = []
        self.ops.append(op)
        return op

    def barrier(self, eng, deps, tag="barrier"):
        return self.emit(eng, None, deps=deps, tag=tag)

    def finalize_and_emit(self):
        nc = self.nc
        for e in ENG_LIST:
            self.csem[e] = self._es.enter_context(nc.semaphore(f"cs_{e}"))
            self.dsem[e] = self._es.enter_context(nc.semaphore(f"ds_{e}"))
        eseq = {e: 0 for e in ENG_LIST}
        for op in self.ops:
            eseq[op.eng] += 1
            op.eseq = eseq[op.eng]
        global RAW_WIN
        for op in self.ops:
            for d in op.deps:
                near_raw = (op.raw_same is not None and d in op.raw_same
                            and op.eseq - d.eseq <= RAW_WIN)
                if not d.dma and (d.eng != op.eng or op.selfsync or near_raw):
                    if d.fn is None:
                        raise RuntimeError(f"cross-engine dep on barrier {d.tag}")
                    d.need_inc = True
        cnt = {e: 0 for e in ENG_LIST}
        dcnt = {e: 0 for e in ENG_LIST}
        for op in self.ops:
            if op.dma:
                dcnt[op.eng] += 1
                op.dma_idx = dcnt[op.eng]
            elif op.need_inc:
                cnt[op.eng] += 1
            op.inc_val = cnt[op.eng]
        wm = {e: {} for e in ENG_LIST}
        for op in self.ops:
            mywm = wm[op.eng]
            wl = {}
            for d in op.deps:
                if d.dma:
                    sem, v = self.dsem[d.eng], 16 * d.dma_idx
                elif d.eng == op.eng:
                    near_raw = (op.raw_same is not None and d in op.raw_same
                                and op.eseq - d.eseq <= RAW_WIN)
                    if not (op.selfsync or near_raw):
                        continue
                    sem, v = self.csem[d.eng], d.inc_val
                else:
                    sem, v = self.csem[d.eng], d.inc_val
                key = id(sem)
                if mywm.get(key, 0) >= v:
                    continue
                if key not in wl or wl[key][1] < v:
                    wl[key] = (sem, v)
            op.waits = list(wl.values())
            for key, (sem, v) in wl.items():
                mywm[key] = v
        per = {e: [] for e in ENG_LIST}
        for op in self.ops:
            per[op.eng].append(op)
        blk = self._es.enter_context(nc.Block())
        regmap = {"sp": blk.sync, "pool": blk.gpsimd, "pe": blk.tensor,
                  "dve": blk.vector, "act": blk.scalar}

        def mk(e):
            ops = per[e]

            def runner(engine):
                for op in ops:
                    for sem, v in op.waits:
                        engine.wait_ge(sem, v)
                    if op.fn is None:
                        continue
                    ins = op.fn(engine)
                    if op.dma:
                        ins.then_inc(self.dsem[op.eng], 16)
                    elif op.need_inc:
                        ins.then_inc(self.csem[op.eng], 1)
            return runner
        for e in ENG_LIST:
            regmap[e](mk(e))
        self._es.close()


class Arena:

    def __init__(self, p, name, cols, dtype):
        self.sb = p.sbuf(name, [128, cols], dtype)
        self.cols = cols
        self.name = name
        self.cursor = 0
        self.carry = []
        self.live = []

    def reset(self):
        carry = []
        for r in self.live:
            if r.last_writer is not None:
                carry.append(r.last_writer)
            carry.extend(r.readers)
        self.carry = carry
        self.live = []
        self.cursor = 0

    def take(self, ncols, name):
        off = self.cursor
        self.cursor += ncols
        if self.cursor > self.cols:
            raise RuntimeError(f"arena {self.name} overflow at {name}: "
                               f"{self.cursor} > {self.cols}")
        r = Res(name)
        r.readers = list(self.carry)
        self.live.append(r)
        return off, r

    def sl(self, off, ncols):
        return self.sb[:, off:off + ncols]

    def ap(self, off, dims):
        return bass.AP(self.sb, off, [[self.cols, 128]] + dims)



class Registry:
    def __init__(self, p):
        self.p = p
        self.host = {}
        self.handles = {}
        self._const_cache = {}

    def dram_in(self, name, shape, fn):
        if name in self.handles:
            return self.handles[name]
        t = self.p.nc.dram_tensor(name, list(shape), F32, kind="ExternalInput")
        self.host[name] = fn
        self.handles[name] = t
        return t

    def const_sb(self, name, arr, dtype):
        key = (name, dtype)
        if key in self._const_cache:
            return self._const_cache[key]
        arr = np.asarray(arr, dtype=np.float32)
        if arr.ndim == 1:
            arr = arr[None, :]
        t = self.dram_in("c_" + name, arr.shape, arr)
        sfx = "_bf" if dtype == BF16 else "_f"
        sb = self.p.sbuf("sbc_" + name + sfx, list(arr.shape), dtype)
        res = Res("c_" + name)
        self.p.emit("pool",
                    lambda e, sb=sb, t=t: e.dma_start(out=sb[:], in_=t.ap()),
                    w=[res], dma=True, tag="const_" + name)
        out = (sb, res)
        self._const_cache[key] = out
        return out


class Ctx:
    def __init__(self, p, reg):
        self.p = p
        self.reg = reg
        self.xT = p.sbuf("xT", [128, 8 * T], F32)
        self.xT_r = [Res(f"xT{c}") for c in range(8)]
        self.xn = p.sbuf("xn", [128, 8 * T], BF16)
        self.xn_r = [Res(f"xn{c}") for c in range(8)]
        self.heads = p.sbuf("heads", [128, 8 * T], BF16)
        self.heads_r = [Res(f"hd{c}") for c in range(8)]
        self.pb = [p.psum(f"pb{i}", [128, 512], F32) for i in range(8)]
        self.pb_r = [Res(f"pb{i}") for i in range(8)]
        self.af = Arena(p, "arenaF", 6144, F32)
        self.ab = Arena(p, "arenaB", 22528, BF16)
        r = reg.const_sb
        self.id_f, self.id_f_r = r("ident", np.eye(128), F32)
        self.id_b, self.id_b_r = r("ident", np.eye(128), BF16)
        self.ones_col_f, self.ones_col_f_r = r("ones_col", np.ones((128, 1)), F32)
        self.ones_col_b, self.ones_col_b_r = r("ones_col", np.ones((128, 1)), BF16)
        self.ones_row_f, self.ones_row_f_r = r("ones_row", np.ones((1, 128)), F32)
        self.ones_row_b, self.ones_row_b_r = r("ones_row", np.ones((1, 128)), BF16)
        self.ones_512_b, self.ones_512_b_r = r("ones_512", np.ones((1, 512)), BF16)
        self.ones_tile_b, self.ones_tile_b_r = r("ones_tile", np.ones((128, 128)), BF16)
        a = np.arange(128)
        m_blta = np.where(a[None, :] < a[:, None], NEG, 0.0)
        self.mk_blta, self.mk_blta_r = r("mask_blta_T", m_blta.T, BF16)
        self.mk_bgta, self.mk_bgta_r = r("mask_bgta_T", m_blta, BF16)
        blk = (a[:, None] // 64) != (a[None, :] // 64)
        self.mk_l1, self.mk_l1_r = r("mask_l1_T", np.where(blk, NEG, 0.0), BF16)



def emit_load_transpose_in(ctx, x_ap):
    p = ctx.p
    o_tok, r_tok0 = ctx.af.take(1024, "xtok0")
    o_tok1, r_tok1 = ctx.af.take(1024, "xtok1")
    offs = [o_tok, o_tok1]
    rs = [r_tok0, r_tok1]
    for tt in range(16):
        bi = tt % 2
        sl = ctx.af.sl(offs[bi], 1024)
        p.emit("sp", lambda e, sl=sl, tt=tt: e.dma_start(
            out=sl, in_=x_ap[tt * 128:(tt + 1) * 128, :]),
            w=[rs[bi]], dma=True, tag=f"ldx{tt}")
        for g in range(2):
            pb, pr = ctx.pb[g], ctx.pb_r[g]
            for j in range(4):
                c = g * 4 + j
                p.emit("pe", lambda e, sl=sl, c=c, j=j, pb=pb: e.transpose(
                    pb[:, j * 128:(j + 1) * 128],
                    sl[:, c * 128:(c + 1) * 128], ctx.id_f[:]),
                    r=[rs[bi], ctx.id_f_r], w=[pr], tag=f"ti{tt}_{c}")

            def ev(e, g=g, tt=tt, pb=pb):
                dst = bass.AP(ctx.xT, (g * 4) * T + tt * 128,
                              [[8 * T, 128], [T, 4], [1, 128]])
                return e.tensor_copy(dst, pb[:])
            p.emit("dve", ev, r=[pr],
                   w=[ctx.xT_r[g * 4 + j] for j in range(4)], tag=f"ei{tt}_{g}")


def emit_transpose_out_store(ctx, out_ap):
    p = ctx.p
    ctx.af.reset()
    o0, r0 = ctx.af.take(1024, "otok0")
    o1, r1 = ctx.af.take(1024, "otok1")
    offs, rs = [o0, o1], [r0, r1]
    last = None
    for tt in range(16):
        bi = tt % 2
        sl = ctx.af.sl(offs[bi], 1024)
        for g in range(2):
            pb, pr = ctx.pb[g], ctx.pb_r[g]
            for j in range(4):
                c = g * 4 + j
                p.emit("pe", lambda e, c=c, j=j, tt=tt, pb=pb: e.transpose(
                    pb[:, j * 128:(j + 1) * 128],
                    ctx.xT[:, c * T + tt * 128: c * T + (tt + 1) * 128],
                    ctx.id_f[:]),
                    r=[ctx.xT_r[c], ctx.id_f_r], w=[pr], tag=f"to{tt}_{c}")
            p.emit("dve", lambda e, g=g, sl=sl, pb=pb: e.tensor_copy(
                sl[:, g * 512:(g + 1) * 512], pb[:]),
                r=[pr], w=[rs[bi]], tag=f"eo{tt}_{g}")
        last = p.emit("sp", lambda e, sl=sl, tt=tt: e.dma_start(
            out=out_ap[tt * 128:(tt + 1) * 128, :], in_=sl),
            r=[rs[bi]], dma=True, tag=f"stx{tt}")
    p.barrier("sp", deps=[last], tag="final_wait")


def emit_ln(ctx, tag):
    p = ctx.p
    ctx.af.reset()
    ctx.ab.reset()
    o_sq, r_sq = ctx.ab.take(T, "sq")
    o_m, r_m = ctx.ab.take(T, "mrep")
    o_r, r_r = ctx.ab.take(T, "rrep")
    o_v, r_v = ctx.af.take(T, "var")
    o_rw, r_rows = ctx.af.take(1024, "lnrows")
    rows = ctx.af.sb[0:1, o_rw:o_rw + 1024]
    sq = ctx.ab.sl(o_sq, T)
    mrep = ctx.ab.sl(o_m, T)
    rrep = ctx.ab.sl(o_r, T)
    var = ctx.af.sl(o_v, T)
    for tb in range(4):
        pbx, pbq = ctx.pb[0], ctx.pb[1]
        for c in range(8):
            p.emit("act", lambda e, c=c, tb=tb: e.activation(
                sq[:, tb * 512:(tb + 1) * 512],
                ctx.xT[:, c * T + tb * 512: c * T + (tb + 1) * 512], AF.Square),
                r=[ctx.xT_r[c]], w=[r_sq], tag=f"{tag}sq{tb}_{c}")
            p.emit("pe", lambda e, c=c, tb=tb, pbx=pbx: e.matmul(
                pbx[0:1, :], ctx.ones_col_f[:],
                ctx.xT[:, c * T + tb * 512: c * T + (tb + 1) * 512],
                start=(c == 0), stop=(c == 7)),
                r=[ctx.xT_r[c], ctx.ones_col_f_r], w=[ctx.pb_r[0]],
                tag=f"{tag}sx{tb}_{c}")
            p.emit("pe", lambda e, c=c, tb=tb, pbq=pbq: e.matmul(
                pbq[0:1, :], ctx.ones_col_b[:],
                sq[:, tb * 512:(tb + 1) * 512],
                start=(c == 0), stop=(c == 7)),
                r=[r_sq, ctx.ones_col_b_r], w=[ctx.pb_r[1]],
                tag=f"{tag}sq2{tb}_{c}")
        p.emit("dve", lambda e, pbx=pbx: e.tensor_copy(
            rows[0:1, 0:512], pbx[0:1, :]),
            r=[ctx.pb_r[0]], w=[r_rows], tag=f"{tag}es{tb}")
        p.emit("dve", lambda e, pbq=pbq: e.tensor_copy(
            rows[0:1, 512:1024], pbq[0:1, :]),
            r=[ctx.pb_r[1]], w=[r_rows], tag=f"{tag}eq{tb}")
        pm, pr2 = ctx.pb[2], ctx.pb[3]
        p.emit("pe", lambda e, pm=pm: e.matmul(
            pm[:], ctx.ones_row_f[:], rows[0:1, 0:512],
            start=True, stop=True),
            r=[r_rows, ctx.ones_row_f_r], w=[ctx.pb_r[2]], tag=f"{tag}rm{tb}")
        p.emit("pe", lambda e, pr2=pr2: e.matmul(
            pr2[:], ctx.ones_row_f[:], rows[0:1, 512:1024],
            start=True, stop=True),
            r=[r_rows, ctx.ones_row_f_r], w=[ctx.pb_r[3]], tag=f"{tag}rr{tb}")
        p.emit("dve", lambda e, tb=tb, pm=pm: e.tensor_scalar_mul(
            mrep[:, tb * 512:(tb + 1) * 512], pm[:], 1.0 / D),
            r=[ctx.pb_r[2]], w=[r_m], tag=f"{tag}mn{tb}")
        p.emit("dve", lambda e, tb=tb: e.tensor_mul(
            var[:, tb * 512:(tb + 1) * 512],
            mrep[:, tb * 512:(tb + 1) * 512],
            mrep[:, tb * 512:(tb + 1) * 512]),
            r=[r_m], w=[r_v], tag=f"{tag}m2{tb}")
        p.emit("dve", lambda e, tb=tb, pr2=pr2: e.scalar_tensor_tensor(
            var[:, tb * 512:(tb + 1) * 512], pr2[:], 1.0 / D,
            var[:, tb * 512:(tb + 1) * 512], OP.mult, OP.subtract),
            r=[ctx.pb_r[3], r_v], w=[r_v], tag=f"{tag}vr{tb}")
        p.emit("dve", lambda e, tb=tb: e.tensor_scalar_add(
            var[:, tb * 512:(tb + 1) * 512],
            var[:, tb * 512:(tb + 1) * 512], 1e-5),
            r=[r_v], w=[r_v], tag=f"{tag}ve{tb}")
        p.emit("act", lambda e, tb=tb: e.activation(
            var[:, tb * 512:(tb + 1) * 512],
            var[:, tb * 512:(tb + 1) * 512], AF.Ln),
            r=[r_v], w=[r_v], tag=f"{tag}ln{tb}")
        p.emit("act", lambda e, tb=tb: e.activation(
            rrep[:, tb * 512:(tb + 1) * 512],
            var[:, tb * 512:(tb + 1) * 512], AF.Exp, scale=-0.5),
            r=[r_v], w=[r_r], tag=f"{tag}ex{tb}")
    for c in range(8):
        p.emit("dve", lambda e, c=c: e.tensor_sub(
            sq[:], ctx.xT[:, c * T:(c + 1) * T], mrep[:]),
            r=[ctx.xT_r[c], r_m], w=[r_sq], tag=f"{tag}xm{c}")
        p.emit("dve", lambda e, c=c: e.tensor_mul(
            ctx.xn[:, c * T:(c + 1) * T], sq[:], rrep[:]),
            r=[r_sq, r_r], w=[ctx.xn_r[c]], tag=f"{tag}xr{c}")


def load_w_dc_slice(ctx, tgt_off, tgt_r, wt, col0, ncols, arena, tag):
    p = ctx.p
    if isinstance(wt, tuple):
        wt, base = wt
        col0 = col0 + base
    nin = int(wt.shape[0])
    wcols = int(wt.shape[1])
    ndc = nin // 128

    def fn(e):
        dst = bass.AP(arena.sb, tgt_off, [[arena.cols, 128], [ncols, ndc], [1, ncols]])
        src = bass.AP(wt, col0, [[wcols, 128], [wcols * 128, ndc],
                                 [1, ncols]])
        return e.dma_start(out=dst, in_=src)
    return p.emit("pool", fn, w=[tgt_r], dma=True, tag=tag)


def rope_tables():
    inv = 1.0 / (10000.0 ** (np.arange(0, DH, 2) / DH))
    pos = np.arange(T)
    ang = pos[None, :] * inv[:, None]
    cos64 = np.concatenate([np.cos(ang), np.cos(ang)], 0)
    sinS = np.concatenate([-np.sin(ang), np.sin(ang)], 0)
    return np.concatenate([cos64, cos64], 0), np.concatenate([sinS, sinS], 0)


def rot_perm_cols(w_t):
    idx = np.arange(w_t.shape[1])
    src = (idx // DH) * DH + ((idx % DH) + 32) % DH
    return np.ascontiguousarray(w_t[:, src])


def attn_kt_plan(kind, g):
    out = []
    if kind == "full":
        for kt in range(16):
            out.append((kt, 0, 512, []))
    elif kind == "causal":
        for kt in range(4 * g + 4):
            c0 = max(0, (kt - 4 * g) * 128)
            masks = [("blta", (kt - 4 * g) * 128)] if kt >= 4 * g else []
            out.append((kt, c0, 512, masks))
    else:
        for kt in range(max(0, 4 * g - 2), min(16, 4 * g + 6)):
            qlo = max(4 * g, kt - 2)
            qhi = min(4 * g + 3, kt + 2)
            c0 = (qlo - 4 * g) * 128
            c1 = (qhi - 4 * g + 1) * 128
            masks = []
            if 4 * g <= kt - 2 <= 4 * g + 3:
                masks.append(("blta", (kt - 2 - 4 * g) * 128))
            if 4 * g <= kt + 2 <= 4 * g + 3:
                masks.append(("bgta", (kt + 2 - 4 * g) * 128))
            out.append((kt, c0, c1, masks))
    return out


def emit_attn_common(ctx, tag, cfg):
    p = ctx.p
    s = cfg["scale"]
    SH = 30.0
    ctx.ab.reset()
    ctx.af.reset()
    o_vt, _ = ctx.ab.take(16 * 8 * 65, "v_tok")
    v_tok_r = [[Res(f"{tag}vt{half}_{kt}") for kt in range(16)] for half in range(2)]
    for half in range(2):
        for kt in range(16):
            v_tok_r[half][kt].readers = list(ctx.ab.carry)
            ctx.ab.live.append(v_tok_r[half][kt])
    aug = []
    for nm in ("k65a", "k65b", "q65a", "q65b"):
        aug.append(ctx.ab.take(T, nm))
    (o_ka, r_ka), (o_kb, r_kb), (o_qa, r_qa), (o_qb, r_qb) = aug
    o_e, _ = ctx.ab.take(1024, "expb")
    expb_r = [Res(f"{tag}ex{i}") for i in range(2)]
    for r_ in expb_r:
        r_.readers = list(ctx.ab.carry)
        ctx.ab.live.append(r_)
    o_w, r_w = ctx.ab.take(2048, "wslice")
    o_wv, r_wv = ctx.ab.take(2048, "wv")
    o_sqs, r_sqs = ctx.ab.take(512, "sqs")
    o_rec, r_rec = ctx.af.take(512, "recb")
    o_rt, r_rt = ctx.af.take(512, "ropet")
    o_rr, r_rr = ctx.af.take(640, "rrow")
    o_nqa, r_nqa = ctx.af.take(T, "nqrowA")
    o_nqb, r_nqb = ctx.af.take(T, "nqrowB")
    recb = ctx.af.sl(o_rec, 512)
    ropet = ctx.af.sl(o_rt, 512)
    rrow = ctx.af.sl(o_rr, 640)
    if cfg["rope"]:
        cos128, sin128 = rope_tables()
        cosb, cosb_r = ctx.reg.const_sb("rope_cos", cos128, BF16)
        sinb, sinb_r = ctx.reg.const_sb("rope_sin", sin128, BF16)

    VT = 8 * 65

    for o_k, r_k in ((o_ka, r_ka), (o_kb, r_kb)):
        p.emit("dve", lambda e, o_k=o_k: e.memset(
            ctx.ab.sb[64:65, o_k:o_k + T], 1.0),
            w=[r_k], tag=f"{tag}kones")

    def emit_v_quarter(quarter):
        load_w_dc_slice(ctx, o_wv, r_wv, cfg["wv_t"], quarter * 256, 256,
                        ctx.ab, f"{tag}ldv{quarter}")
        half = quarter // 2
        for kt in range(16):
            pb, pbr = ctx.pb[kt % 2], ctx.pb_r[kt % 2]
            for dc in range(8):
                p.emit("pe", lambda e, kt=kt, dc=dc, pb=pb: e.matmul(
                    pb[:, 0:256],
                    ctx.xn[:, dc * T + kt * 128: dc * T + (kt + 1) * 128],
                    ctx.ab.sl(o_wv + dc * 256, 256),
                    start=(dc == 0), stop=(dc == 7 and cfg.get("bv") is None)),
                    r=[ctx.xn_r[dc], r_wv], w=[pbr], tag=f"{tag}v{quarter}_{kt}_{dc}")
            if cfg.get("bv") is not None:
                brow, brr, boff = cfg["bv"]
                p.emit("pe", lambda e, pb=pb, quarter=quarter, brow=brow,
                       boff=boff: e.matmul(
                    pb[:, 0:256], ctx.ones_row_b[:],
                    brow[0:1, boff + quarter * 256: boff + (quarter + 1) * 256],
                    start=False, stop=True),
                    r=[brr, ctx.ones_row_b_r], w=[pbr], tag=f"{tag}vb{quarter}_{kt}")

            def evv(e, kt=kt, quarter=quarter, pb=pb):
                dst = bass.AP(ctx.ab.sb, o_vt + kt * VT + (quarter % 2) * 4 * 65,
                              [[ctx.ab.cols, 128], [65, 4], [1, 64]])
                return e.tensor_copy(dst, pb[:, 0:256])
            p.emit("dve", evv, r=[pbr], w=[v_tok_r[half][kt]],
                   tag=f"{tag}ve{quarter}_{kt}")
        if quarter % 2 == 1:
            def setones(e, half=half):
                dst = bass.AP(ctx.ab.sb, o_vt + 64,
                              [[ctx.ab.cols, 128], [VT, 16], [65, 8]])
                return e.memset(dst, 1.0)
            p.emit("dve", setones, r=[], w=v_tok_r[half], tag=f"{tag}vo{half}")

    def proj(c, what, dsts, bias):
        wi_main, wi_rot = 0, 1
        names = [(cfg["w" + what + "_t"], wi_main)] + (
            [(cfg["w" + what + "_rot_t"], wi_rot)] if cfg["rope"] else [])
        for wt, wi in names:
            load_w_dc_slice(ctx, o_w + wi * 1024, r_w, wt, c * 128, 128,
                            ctx.ab, f"{tag}lw{what}{c}_{wi}")
        (o_da, r_da), (o_db, r_db) = dsts
        for tb in range(4):
            pbq, pbrot = ctx.pb[4], ctx.pb[5]
            for dc in range(8):
                p.emit("pe", lambda e, tb=tb, dc=dc, pbq=pbq: e.matmul(
                    pbq[:],
                    ctx.ab.sl(o_w + 0 * 1024 + dc * 128, 128),
                    ctx.xn[:, dc * T + tb * 512: dc * T + (tb + 1) * 512],
                    start=(dc == 0), stop=(dc == 7 and bias is None)),
                    r=[r_w, ctx.xn_r[dc]], w=[ctx.pb_r[4]],
                    tag=f"{tag}{what}mm{c}_{tb}_{dc}")
            if bias is not None:
                brow, brr, boff = bias
                p.emit("pe", lambda e, pbq=pbq, brow=brow, boff=boff, c=c: e.matmul(
                    pbq[:], brow[0:1, boff + c * 128: boff + (c + 1) * 128],
                    ctx.ones_512_b[:], start=False, stop=True),
                    r=[brr, ctx.ones_512_b_r], w=[ctx.pb_r[4]],
                    tag=f"{tag}{what}b{c}_{tb}")
            p.emit("act", lambda e, pbq=pbq: e.activation(
                ctx.ab.sl(o_sqs, 512), pbq[:], AF.Square),
                r=[ctx.pb_r[4]], w=[r_sqs], tag=f"{tag}{what}sq{c}_{tb}")
            for hh in range(2):
                pbn = ctx.pb[6 + hh]
                p.emit("pe", lambda e, hh=hh, pbn=pbn: e.matmul(
                    pbn[0:1, :], ctx.ones_col_b[hh * 64:(hh + 1) * 64, :],
                    ctx.ab.sb[hh * 64:(hh + 1) * 64, o_sqs:o_sqs + 512],
                    start=True, stop=True),
                    r=[r_sqs, ctx.ones_col_b_r], w=[ctx.pb_r[6 + hh]],
                    tag=f"{tag}{what}nm{c}_{tb}_{hh}")
                if what == "q":
                    o_nq = (o_nqa, o_nqb)[hh]
                    r_nq = (r_nqa, r_nqb)[hh]
                    p.emit("dve", lambda e, tb=tb, hh=hh, pbn=pbn, o_nq=o_nq:
                           e.tensor_copy(
                        ctx.af.sb[0:1, o_nq + tb * 512: o_nq + (tb + 1) * 512],
                        pbn[0:1, :]),
                        r=[ctx.pb_r[6 + hh]], w=[r_nq],
                        tag=f"{tag}nq{c}_{tb}_{hh}")
                else:
                    p.emit("dve", lambda e, tb=tb, hh=hh, pbn=pbn: e.reduce_max(
                        rrow[0:1, 32 + hh * 4 + tb: 33 + hh * 4 + tb],
                        pbn[0:1, :], AX.X),
                        r=[ctx.pb_r[6 + hh]], w=[r_rr],
                        tag=f"{tag}nk{c}_{tb}_{hh}")
            if cfg["rope"]:
                for dc in range(8):
                    p.emit("pe", lambda e, tb=tb, dc=dc, pbrot=pbrot: e.matmul(
                        pbrot[:],
                        ctx.ab.sl(o_w + 1 * 1024 + dc * 128, 128),
                        ctx.xn[:, dc * T + tb * 512: dc * T + (tb + 1) * 512],
                        start=(dc == 0), stop=(dc == 7)),
                        r=[r_w, ctx.xn_r[dc]], w=[ctx.pb_r[5]],
                        tag=f"{tag}{what}rm{c}_{tb}_{dc}")
                p.emit("dve", lambda e, tb=tb, pbq=pbq: e.tensor_mul(
                    ropet[:], pbq[:], cosb[:, tb * 512:(tb + 1) * 512]),
                    r=[ctx.pb_r[4], cosb_r], w=[r_rt], tag=f"{tag}{what}rc{c}_{tb}")
                p.emit("dve", lambda e, tb=tb, pbrot=pbrot: e.tensor_mul(
                    recb[:], pbrot[:], sinb[:, tb * 512:(tb + 1) * 512]),
                    r=[ctx.pb_r[5], sinb_r], w=[r_rec], tag=f"{tag}{what}rs{c}_{tb}")
                for hh, (o_d, r_d) in enumerate(((o_da, r_da), (o_db, r_db))):
                    p.emit("dve", lambda e, tb=tb, hh=hh, o_d=o_d: e.tensor_add(
                        ctx.ab.sb[0:64, o_d + tb * 512: o_d + (tb + 1) * 512],
                        ropet[hh * 64:(hh + 1) * 64, :],
                        recb[hh * 64:(hh + 1) * 64, :]),
                        r=[r_rt, r_rec], w=[r_d], tag=f"{tag}{what}ra{c}_{tb}_{hh}")
            else:
                for hh, (o_d, r_d) in enumerate(((o_da, r_da), (o_db, r_db))):
                    p.emit("dve", lambda e, tb=tb, hh=hh, o_d=o_d, pbq=pbq:
                           e.tensor_copy(
                        ctx.ab.sb[0:64, o_d + tb * 512: o_d + (tb + 1) * 512],
                        pbq[hh * 64:(hh + 1) * 64, :]),
                        r=[ctx.pb_r[4]], w=[r_d], tag=f"{tag}{what}ev{c}_{tb}_{hh}")

    for half in range(2):
        emit_v_quarter(2 * half)
        emit_v_quarter(2 * half + 1)
        for c in range(4 * half, 4 * half + 4):
            proj(c, "k", ((o_ka, r_ka), (o_kb, r_kb)), cfg.get("bk"))
            proj(c, "q", ((o_qa, r_qa), (o_qb, r_qb)), cfg.get("bq"))
            for hh in range(2):
                o_q65 = (o_qa, o_qb)[hh]
                r_q65 = (r_qa, r_qb)[hh]
                o_nq = (o_nqa, o_nqb)[hh]
                r_nq = (r_nqa, r_nqb)[hh]
                p.emit("dve", lambda e, hh=hh: e.reduce_max(
                    rrow[0:1, 40 + hh:41 + hh],
                    rrow[0:1, 32 + hh * 4:36 + hh * 4], AX.X),
                    r=[r_rr], w=[r_rr], tag=f"{tag}mk{c}_{hh}")
                p.emit("dve", lambda e, hh=hh, o_nq=o_nq: e.tensor_scalar_mul(
                    ctx.af.sb[0:1, o_nq:o_nq + T],
                    ctx.af.sb[0:1, o_nq:o_nq + T],
                    rrow[0:1, 40 + hh:41 + hh]),
                    r=[r_nq, r_rr], w=[r_nq], tag=f"{tag}bm{c}_{hh}",
                    selfsync=True)
                p.emit("act", lambda e, o_nq=o_nq: e.activation(
                    ctx.af.sb[0:1, o_nq:o_nq + T],
                    ctx.af.sb[0:1, o_nq:o_nq + T], AF.Sqrt),
                    r=[r_nq], w=[r_nq], tag=f"{tag}bq{c}_{hh}")
                p.emit("dve", lambda e, o_nq=o_nq, o_q65=o_q65: e.tensor_scalar(
                    ctx.ab.sb[64:65, o_q65:o_q65 + T],
                    ctx.af.sb[0:1, o_nq:o_nq + T], -1.0, SH / s,
                    OP.mult, OP.add),
                    r=[r_nq], w=[r_q65], tag=f"{tag}bw{c}_{hh}")

            if ATTN_DEBUG is not None and c == 0:
                ATTN_DEBUG(ctx, tag + "p0", {
                    "k65a": (ctx.ab.sb[0:128, o_ka:o_ka + T], [r_ka]),
                    "q65a": (ctx.ab.sb[0:128, o_qa:o_qa + T], [r_qa]),
                    "nqa": (ctx.af.sb[0:1, o_nqa:o_nqa + T], [r_nqa]),
                    "rrow": (rrow[:], [r_rr])})
            for hh in range(2):
                h = 2 * c + hh
                o_k65 = (o_ka, o_kb)[hh]
                r_k65 = (r_ka, r_kb)[hh]
                o_q65 = (o_qa, o_qb)[hh]
                r_q65 = (r_qa, r_qb)[hh]
                hloc = 2 * (c - 4 * half) + hh
                for g in range(4):
                    kts = attn_kt_plan(cfg["kind"], g)
                    av_bank, av_r = ctx.pb[6], ctx.pb_r[6]
                    n_kts = len(kts)

                    def emit_av(i, kt, c0, c1, hloc=hloc, h=h, g=g,
                                n_kts=n_kts, av_bank=av_bank, av_r=av_r,
                                half=half):
                        p.emit("pe", lambda e, i=i, kt=kt, c0=c0, c1=c1: e.matmul(
                            av_bank[0:65, c0:c1],
                            bass.AP(ctx.ab.sb, o_vt + kt * VT + hloc * 65,
                                    [[ctx.ab.cols, 128], [1, 65]]),
                            ctx.ab.sb[:, o_e + (i % 2) * 512 + c0:
                                      o_e + (i % 2) * 512 + c1],
                            start=(i == 0), stop=(i == n_kts - 1)),
                            r=[v_tok_r[half][kt], expb_r[i % 2]], w=[av_r],
                            tag=f"{tag}av{h}_{g}_{kt}")

                    prev = None
                    for i, (kt, c0, c1, masks) in enumerate(kts):
                        sb, sbr = ctx.pb[i % 2], ctx.pb_r[i % 2]
                        p.emit("pe", lambda e, kt=kt, c0=c0, c1=c1, sb=sb, g=g,
                               o_k65=o_k65, o_q65=o_q65,
                               nm=len(masks): e.matmul(
                            sb[:, c0:c1],
                            ctx.ab.sb[0:65, o_k65 + kt * 128: o_k65 + (kt + 1) * 128],
                            ctx.ab.sb[0:65, o_q65 + g * 512 + c0: o_q65 + g * 512 + c1],
                            start=True, stop=(nm == 0)),
                            r=[r_k65, r_q65], w=[sbr], tag=f"{tag}sc{h}_{g}_{kt}")
                        for mi, (mt, mc0) in enumerate(masks):
                            mco = ctx.mk_blta if mt == "blta" else ctx.mk_bgta
                            mcr = ctx.mk_blta_r if mt == "blta" else ctx.mk_bgta_r
                            p.emit("pe", lambda e, sb=sb, mc0=mc0, mco=mco: e.matmul(
                                sb[:, mc0:mc0 + 128], mco[:], ctx.id_b[:],
                                start=False, stop=(mi == len(masks) - 1)),
                                r=[mcr, ctx.id_b_r], w=[sbr],
                                tag=f"{tag}m{h}_{g}_{kt}_{mi}")
                        p.emit("act", lambda e, i=i, c0=c0, c1=c1, sb=sb: e.activation(
                            ctx.ab.sb[:, o_e + (i % 2) * 512 + c0:
                                      o_e + (i % 2) * 512 + c1],
                            sb[:, c0:c1], AF.Exp, scale=s),
                            r=[sbr], w=[expb_r[i % 2]],
                            tag=f"{tag}e{h}_{g}_{kt}")
                        if prev is not None:
                            emit_av(*prev)
                        prev = (i, kt, c0, c1)
                    emit_av(*prev)
                    p.emit("dve", lambda e: e.tensor_copy(
                        rrow[0:1, 128:640], av_bank[64:65, :]),
                        r=[av_r], w=[r_rr], tag=f"{tag}s{h}_{g}")
                    if ATTN_DEBUG is not None and h == 0 and g == 0:
                        ATTN_DEBUG(ctx, tag + "se", {
                            "sumexp": (rrow[0:1, 128:640], [r_rr])})
                    p.emit("act", lambda e: e.activation(
                        rrow[0:1, 128:640], rrow[0:1, 128:640], AF.Ln),
                        r=[r_rr], w=[r_rr], tag=f"{tag}sl{h}_{g}")
                    p.emit("act", lambda e: e.activation(
                        rrow[0:1, 128:640], rrow[0:1, 128:640], AF.Exp,
                        scale=-1.0),
                        r=[r_rr], w=[r_rr], tag=f"{tag}sx{h}_{g}")
                    if ATTN_DEBUG is not None and h == 0 and g == 0:
                        ATTN_DEBUG(ctx, tag + "sx", {
                            "recrow": (rrow[0:1, 128:640], [r_rr])})
                    pb7b = ctx.pb[7]
                    p.emit("pe", lambda e, pb7b=pb7b: e.matmul(
                        pb7b[:, :], ctx.ones_row_f[:],
                        rrow[0:1, 128:640], start=True, stop=True),
                        r=[r_rr, ctx.ones_row_f_r], w=[ctx.pb_r[7]],
                        tag=f"{tag}sr{h}_{g}")
                    p.emit("dve", lambda e, pb7b=pb7b: e.tensor_copy(
                        recb[0:64, :], pb7b[0:64, :]),
                        r=[ctx.pb_r[7]], w=[r_rec], tag=f"{tag}se{h}_{g}")

                    def hev(e, hb=hh * 64, g=g, c=c):
                        return e.tensor_mul(
                            ctx.heads[hb:hb + 64,
                                      c * T + g * 512: c * T + (g + 1) * 512],
                            av_bank[0:64, :], recb[0:64, :])
                    p.emit("dve", hev, r=[av_r, r_rec], w=[ctx.heads_r[c]],
                           tag=f"{tag}h{h}_{g}")
                    if ATTN_DEBUG is not None and h == 0 and g == 0:
                        ATTN_DEBUG(ctx, tag + "rc", {
                            "recb": (recb[0:64, :], [r_rec])})

    ctx.ab.reset()
    o_wo, r_wo = ctx.ab.take(8192, "wob")
    for ohalf in range(2):
        load_w_dc_slice(ctx, o_wo, r_wo, cfg["wo_t"], ohalf * 512, 512,
                        ctx.ab, f"{tag}lwo{ohalf}")
        for oc in range(4):
            occ = ohalf * 4 + oc
            for tb in range(4):
                pb, pbr = ctx.pb[2 + (tb % 2)], ctx.pb_r[2 + (tb % 2)]
                for dc in range(8):
                    p.emit("pe", lambda e, oc=oc, tb=tb, dc=dc, pb=pb: e.matmul(
                        pb[:],
                        ctx.ab.sl(o_wo + dc * 512 + oc * 128, 128),
                        ctx.heads[:, dc * T + tb * 512: dc * T + (tb + 1) * 512],
                        start=(dc == 0),
                        stop=(dc == 7 and cfg.get("bo") is None)),
                        r=[r_wo, ctx.heads_r[dc]], w=[pbr],
                        tag=f"{tag}o{occ}_{tb}_{dc}")
                if cfg.get("bo") is not None:
                    brow, brr, boff = cfg["bo"]
                    p.emit("pe", lambda e, occ=occ, pb=pb, brow=brow, boff=boff: e.matmul(
                        pb[:], brow[0:1, boff + occ * 128: boff + (occ + 1) * 128],
                        ctx.ones_512_b[:], start=False, stop=True),
                        r=[brr, ctx.ones_512_b_r], w=[pbr],
                        tag=f"{tag}ob{occ}_{tb}")
                p.emit("dve", lambda e, occ=occ, tb=tb, pb=pb: e.tensor_add(
                    ctx.xT[:, occ * T + tb * 512: occ * T + (tb + 1) * 512],
                    ctx.xT[:, occ * T + tb * 512: occ * T + (tb + 1) * 512],
                    pb[:]),
                    r=[pbr, ctx.xT_r[occ]], w=[ctx.xT_r[occ]],
                    tag=f"{tag}oe{occ}_{tb}")



def emit_l3(ctx, reg):
    p = ctx.p
    emit_ln(ctx, "l3n_")
    tr = lambda k: (lambda inp, k=k: np.ascontiguousarray(inp[k].T))
    cfg = dict(
        wq_t=reg.dram_in("L3wq_t", [D, D], tr("L3_attn_Wq")),
        wk_t=reg.dram_in("L3wk_t", [D, D], tr("L3_attn_Wk")),
        wv_t=reg.dram_in("L3wv_t", [D, D], tr("L3_attn_Wv")),
        wo_t=reg.dram_in("L3wo_t", [D, D], tr("L3_attn_Wo")),
        wq_rot_t=reg.dram_in("L3wqr_t", [D, D],
                             lambda inp: rot_perm_cols(np.ascontiguousarray(inp["L3_attn_Wq"].T))),
        wk_rot_t=reg.dram_in("L3wkr_t", [D, D],
                             lambda inp: rot_perm_cols(np.ascontiguousarray(inp["L3_attn_Wk"].T))),
        rope=True, kind="causal", scale=1.0 / 8.0)
    emit_attn_common(ctx, "l3a_", cfg)

    w_t = reg.dram_in("L3w_t", [D, D], tr("L3_ffn_weights"))
    sc = reg.dram_in("L3sc", [128, 8],
                     lambda inp: np.ascontiguousarray(
                         inp["L3_ffn_scale"].reshape(8, 128).T))
    bc = reg.dram_in("L3bc", [128, 8],
                     lambda inp: np.ascontiguousarray(
                         inp["L3_ffn_bias"].reshape(8, 128).T))
    ctx.ab.reset()
    ctx.af.reset()
    o_w, r_w = ctx.ab.take(8192, "l3fw")
    o_t, r_t = ctx.af.take(512, "l3ft")
    o_sc, r_sc = ctx.af.take(16, "l3fsc")
    tmp = ctx.af.sl(o_t, 512)
    scb = ctx.af.sl(o_sc, 16)
    p.emit("sp", lambda e: e.dma_start(out=scb[:, 0:8], in_=sc.ap()),
           w=[r_sc], dma=True, tag="l3f_lsc")
    p.emit("sp", lambda e: e.dma_start(out=scb[:, 8:16], in_=bc.ap()),
           w=[r_sc], dma=True, tag="l3f_lbc")
    for ohalf in range(2):
        load_w_dc_slice(ctx, o_w, r_w, w_t, ohalf * 512, 512, ctx.ab,
                        f"l3f_lw{ohalf}")
        p.emit("dve", lambda e: e.tensor_scalar(
            ctx.ab.sl(o_w, 8192), ctx.ab.sl(o_w, 8192), 0.0, 2.0,
            OP.is_ge, OP.mult),
            r=[r_w], w=[r_w], tag=f"l3f_s1{ohalf}")
        p.emit("dve", lambda e: e.tensor_scalar_add(
            ctx.ab.sl(o_w, 8192), ctx.ab.sl(o_w, 8192), -1.0),
            r=[r_w], w=[r_w], tag=f"l3f_s2{ohalf}")
        for oc in range(4):
            occ = ohalf * 4 + oc
            for tb in range(4):
                pb, pbr = ctx.pb[4 + (tb % 2)], ctx.pb_r[4 + (tb % 2)]
                for dc in range(8):
                    p.emit("pe", lambda e, oc=oc, tb=tb, dc=dc, pb=pb: e.matmul(
                        pb[:],
                        ctx.ab.sl(o_w + dc * 512 + oc * 128, 128),
                        ctx.xn[:, dc * T + tb * 512: dc * T + (tb + 1) * 512],
                        start=(dc == 0), stop=(dc == 7)),
                        r=[r_w, ctx.xn_r[dc]], w=[pbr],
                        tag=f"l3f_m{occ}_{tb}_{dc}")
                p.emit("dve", lambda e, occ=occ, pb=pb: e.tensor_scalar(
                    tmp[:], pb[:], scb[:, occ:occ + 1], scb[:, 8 + occ:9 + occ],
                    OP.mult, OP.add),
                    r=[pbr, r_sc], w=[r_t], tag=f"l3f_sb{occ}_{tb}")
                p.emit("dve", lambda e: e.tensor_scalar(
                    tmp[:], tmp[:], 100.0, -100.0, OP.min, OP.max),
                    r=[r_t], w=[r_t], tag=f"l3f_cl{occ}_{tb}")
                p.emit("dve", lambda e, occ=occ, tb=tb: e.tensor_add(
                    ctx.xT[:, occ * T + tb * 512: occ * T + (tb + 1) * 512],
                    ctx.xT[:, occ * T + tb * 512: occ * T + (tb + 1) * 512],
                    tmp[:]),
                    r=[r_t, ctx.xT_r[occ]], w=[ctx.xT_r[occ]],
                    tag=f"l3f_x{occ}_{tb}")


def bias_row(ctx, reg, name, key, n=D):
    t = reg.dram_in(name, [1, n],
                    lambda inp, key=key: np.asarray(inp[key]).reshape(1, -1))
    sb = ctx.p.sbuf("sb_" + name, [1, n], BF16)
    res = Res(name)
    ctx.p.emit("pool", lambda e: e.dma_start(out=sb[:], in_=t.ap()),
               w=[res], dma=True, tag="ld_" + name)
    return (sb, res, 0)


def emit_router(ctx, reg, tag, w_t, nE, brow, renorm):
    p = ctx.p
    o_rw, r_rw = ctx.ab.take(8 * nE, "rw")
    o_gt, r_gt = ctx.af.take(16 * nE, "gates_tok")
    o_lg, r_lg = ctx.af.take(T, "logits")
    o_sc, r_sc = ctx.af.take(4 * nE + 8, "rscratch")
    load_w_dc_slice(ctx, o_rw, r_rw, w_t, 0, nE, ctx.ab, f"{tag}ldw")
    lg = ctx.af.sl(o_lg, T)
    for tb in range(4):
        pb, pbr = ctx.pb[6], ctx.pb_r[6]
        for dc in range(8):
            p.emit("pe", lambda e, dc=dc, tb=tb, pb=pb: e.matmul(
                pb[0:nE, 0:512], ctx.ab.sl(o_rw + dc * nE, nE),
                ctx.xn[:, dc * T + tb * 512: dc * T + (tb + 1) * 512],
                start=(dc == 0), stop=(dc == 7 and brow is None)),
                r=[r_rw, ctx.xn_r[dc]], w=[pbr], tag=f"{tag}mm{tb}_{dc}")
        if brow is not None:
            bsb, bres, boff = brow
            p.emit("pe", lambda e, pb=pb, bsb=bsb, boff=boff: e.matmul(
                pb[0:nE, 0:512], bsb[0:1, boff:boff + nE],
                ctx.ones_512_b[:], start=False, stop=True),
                r=[bres, ctx.ones_512_b_r], w=[pbr], tag=f"{tag}b{tb}")
        p.emit("dve", lambda e, tb=tb, pb=pb: e.tensor_copy(
            lg[0:nE, tb * 512:(tb + 1) * 512], pb[0:nE, 0:512]),
            r=[pbr], w=[r_lg], tag=f"{tag}ev{tb}")
    sc = ctx.af.sl(o_sc, 4 * nE + 8)
    for tt in range(16):
        pb, pbr = ctx.pb[7], ctx.pb_r[7]
        p.emit("pe", lambda e, tt=tt, pb=pb: e.transpose(
            pb[:, 0:nE], lg[0:nE, tt * 128:(tt + 1) * 128],
            ctx.id_f[0:nE, 0:nE]),
            r=[r_lg, ctx.id_f_r], w=[pbr], tag=f"{tag}tr{tt}")
        C = 3 * nE
        p.emit("dve", lambda e, pb=pb: e.reduce_max(
            sc[:, C:C + 1], pb[:, 0:nE], AX.X, None, None, True),
            r=[pbr], w=[r_sc], tag=f"{tag}rm{tt}")
        p.emit("act", lambda e, pb=pb: e.activation(
            sc[:, 0:nE], pb[:, 0:nE], AF.Exp, bias=sc[:, C:C + 1],
            accum_out=sc[:, C + 1:C + 2]),
            r=[pbr, r_sc], w=[r_sc], tag=f"{tag}ex{tt}")
        p.emit("dve", lambda e: e.reciprocal(
            sc[:, C + 2:C + 3], sc[:, C + 1:C + 2]),
            r=[r_sc], w=[r_sc], tag=f"{tag}rc{tt}")
        p.emit("dve", lambda e: e.tensor_scalar_mul(
            sc[:, 0:nE], sc[:, 0:nE], sc[:, C + 2:C + 3]),
            r=[r_sc], w=[r_sc], tag=f"{tag}pr{tt}", selfsync=True)
        p.emit("dve", lambda e: e.reduce_max(
            sc[:, C + 3:C + 4], sc[:, 0:nE], AX.X),
            r=[r_sc], w=[r_sc], tag=f"{tag}m1{tt}")
        p.emit("dve", lambda e: e.tensor_scalar(
            sc[:, nE:2 * nE], sc[:, 0:nE], sc[:, C + 3:C + 4], NEG,
            OP.is_equal, OP.mult),
            r=[r_sc], w=[r_sc], tag=f"{tag}mk{tt}", selfsync=True)
        p.emit("dve", lambda e: e.tensor_add(
            sc[:, nE:2 * nE], sc[:, nE:2 * nE], sc[:, 0:nE]),
            r=[r_sc], w=[r_sc], tag=f"{tag}ms{tt}")
        p.emit("dve", lambda e: e.reduce_max(
            sc[:, C + 4:C + 5], sc[:, nE:2 * nE], AX.X),
            r=[r_sc], w=[r_sc], tag=f"{tag}m2{tt}")
        p.emit("dve", lambda e: e.tensor_scalar(
            sc[:, 2 * nE:3 * nE], sc[:, 0:nE], sc[:, C + 4:C + 5], None,
            OP.is_ge),
            r=[r_sc], w=[r_sc], tag=f"{tag}ge{tt}", selfsync=True)
        if renorm:
            p.emit("dve", lambda e: e.tensor_add(
                sc[:, C + 5:C + 6], sc[:, C + 3:C + 4], sc[:, C + 4:C + 5]),
                r=[r_sc], w=[r_sc], tag=f"{tag}s12{tt}")
            p.emit("dve", lambda e: e.reciprocal(
                sc[:, C + 6:C + 7], sc[:, C + 5:C + 6]),
                r=[r_sc], w=[r_sc], tag=f"{tag}rs{tt}")
            p.emit("dve", lambda e: e.tensor_scalar_mul(
                sc[:, 0:nE], sc[:, 0:nE], sc[:, C + 6:C + 7]),
                r=[r_sc], w=[r_sc], tag=f"{tag}rn{tt}", selfsync=True)
        p.emit("dve", lambda e, tt=tt: e.tensor_mul(
            ctx.af.sl(o_gt + tt * nE, nE), sc[:, 2 * nE:3 * nE], sc[:, 0:nE]),
            r=[r_sc], w=[r_gt], tag=f"{tag}g{tt}")
    if ATTN_DEBUG is not None:
        ATTN_DEBUG(ctx, tag + "gt", {"gates": (ctx.af.sl(o_gt, 16 * nE), [r_gt]),
                                     "sc": (sc[:], [r_sc]),
                                     "lg": (lg[:], [r_lg])})
    return o_gt, r_gt


def emit_g_block(ctx, tag, o_gt, r_gt, nE, tb, o_g, r_g, o_bc, r_bc):
    p = ctx.p
    for half in range(2):
        tt = tb * 2 + half
        pb, pbr = ctx.pb[6], ctx.pb_r[6]
        for e in range(nE):
            p.emit("dve", lambda e_, e=e, tt=tt: e_.tensor_scalar_mul(
                ctx.ab.sl(o_bc, 128), ctx.ones_tile_b[:],
                bass.AP(ctx.af.sb, o_gt + tt * nE + e,
                        [[ctx.af.cols, 128], [1, 1]])),
                r=[r_gt, ctx.ones_tile_b_r], w=[r_bc],
                tag=f"{tag}bc{tb}_{half}_{e}", selfsync=True)
            p.emit("pe", lambda e_, pb=pb: e_.matmul(
                pb[:, 0:128], ctx.ab.sl(o_bc, 128), ctx.id_b[:],
                start=True, stop=True),
                r=[r_bc, ctx.id_b_r], w=[pbr], tag=f"{tag}gr{tb}_{half}_{e}")
            p.emit("dve", lambda e_, e=e, half=half, pb=pb: e_.tensor_copy(
                ctx.ab.sl(o_g + e * 256 + half * 128, 128), pb[:, 0:128]),
                r=[pbr], w=[r_g], tag=f"{tag}ge{tb}_{half}_{e}")


def emit_l0_ffn(ctx, reg):
    p = ctx.p
    w1 = reg.dram_in("L0w1", [8 * D, 4096], lambda inp: np.ascontiguousarray(
        inp["L0_ffn_w1"].reshape(8 * D, 4096)))
    w2 = reg.dram_in("L0w2", [8 * 4096, D], lambda inp: np.ascontiguousarray(
        inp["L0_ffn_w2"].reshape(8 * 4096, D)))
    rw_t = reg.dram_in("L0rw", [D, 9], lambda inp: inp["L0_ffn_router_w"])
    rb = bias_row(ctx, reg, "L0rb", "L0_ffn_router_b", 9)
    ctx.ab.reset()
    ctx.af.reset()
    nE = 9
    o_gt, r_gt = emit_router(ctx, reg, "l0r_", rw_t, nE, rb, False)
    o_h, r_h = ctx.ab.take(32 * 256, "h")
    o_w1, r_w1 = ctx.ab.take(8 * 512, "w1blk")
    o_w2, r_w2 = ctx.ab.take(4 * 1024, "w2blk")
    o_g, r_g = ctx.ab.take(nE * 256, "G")
    o_bc, r_bc = ctx.ab.take(128, "bcast")
    o_t, r_t = ctx.af.take(256, "idg")
    for tb in range(8):
        emit_g_block(ctx, "l0g_", o_gt, r_gt, nE, tb, o_g, r_g, o_bc, r_bc)
        for e in range(8):
            for fb in range(8):
                def ldw1(eng, e=e, fb=fb):
                    dst = bass.AP(ctx.ab.sb, o_w1,
                                  [[ctx.ab.cols, 128], [512, 8], [1, 512]])
                    src = bass.AP(w1, e * D * 4096 + fb * 512,
                                  [[4096, 128], [4096 * 128, 8], [1, 512]])
                    return eng.dma_start(out=dst, in_=src)
                p.emit("pool", ldw1, w=[r_w1], dma=True, tag=f"l0f_lw1_{tb}_{e}_{fb}")
                for fc in range(4):
                    pb, pbr = ctx.pb[4 + fc % 2], ctx.pb_r[4 + fc % 2]
                    for dc in range(8):
                        p.emit("pe", lambda eng, dc=dc, fc=fc, tb=tb, pb=pb: eng.matmul(
                            pb[:, 0:256],
                            ctx.ab.sl(o_w1 + dc * 512 + fc * 128, 128),
                            ctx.xn[:, dc * T + tb * 256: dc * T + (tb + 1) * 256],
                            start=(dc == 0), stop=(dc == 7)),
                            r=[r_w1, ctx.xn_r[dc]], w=[pbr],
                            tag=f"l0f_h{tb}_{e}_{fb}_{fc}_{dc}")
                    fci = fb * 4 + fc
                    p.emit("act", lambda eng, fci=fci, pb=pb: eng.activation(
                        ctx.ab.sl(o_h + fci * 256, 256), pb[:, 0:256], AF.Gelu),
                        r=[pbr], w=[r_h], tag=f"l0f_g{tb}_{e}_{fci}")
                    p.emit("dve", lambda eng, fci=fci, e=e: eng.tensor_mul(
                        ctx.ab.sl(o_h + fci * 256, 256),
                        ctx.ab.sl(o_h + fci * 256, 256),
                        ctx.ab.sl(o_g + e * 256, 256)),
                        r=[r_h, r_g], w=[r_h], tag=f"l0f_gm{tb}_{e}_{fci}")
            for fb in range(8):
                def ldw2(eng, e=e, fb=fb):
                    dst = bass.AP(ctx.ab.sb, o_w2,
                                  [[ctx.ab.cols, 128], [1024, 4], [1, 1024]])
                    src = bass.AP(w2, (e * 4096 + fb * 512) * D,
                                  [[D, 128], [D * 128, 4], [1, 1024]])
                    return eng.dma_start(out=dst, in_=src)
                p.emit("pool", ldw2, w=[r_w2], dma=True, tag=f"l0f_lw2_{tb}_{e}_{fb}")
                for dc in range(8):
                    ybank, ybr = ctx.pb[dc // 2], ctx.pb_r[dc // 2]
                    ycol = (dc % 2) * 256
                    for fcl in range(4):
                        fci = fb * 4 + fcl
                        st = (e == 0 and fb == 0 and fcl == 0 and dc % 2 == 0)
                        last = (e == 7 and fb == 7 and fcl == 3 and dc % 2 == 1)
                        p.emit("pe", lambda eng, dc=dc, fcl=fcl, fci=fci,
                               ybank=ybank, ycol=ycol, st=st,
                               last=last: eng.matmul(
                            ybank[:, ycol:ycol + 256],
                            ctx.ab.sl(o_w2 + fcl * 1024 + dc * 128, 128),
                            ctx.ab.sl(o_h + fci * 256, 256),
                            start=st, stop=last),
                            r=[r_w2, r_h], w=[ybr],
                            tag=f"l0f_y{tb}_{e}_{fb}_{dc}_{fcl}")
        for dc in range(8):
            ybank, ybr = ctx.pb[dc // 2], ctx.pb_r[dc // 2]
            ycol = (dc % 2) * 256
            p.emit("dve", lambda eng, dc=dc, tb=tb, ybank=ybank, ycol=ycol: eng.tensor_add(
                ctx.xT[:, dc * T + tb * 256: dc * T + (tb + 1) * 256],
                ctx.xT[:, dc * T + tb * 256: dc * T + (tb + 1) * 256],
                ybank[:, ycol:ycol + 256]),
                r=[ybr, ctx.xT_r[dc]], w=[ctx.xT_r[dc]], tag=f"l0f_ye{tb}_{dc}")
        for dc in range(8):
            p.emit("dve", lambda eng, dc=dc, tb=tb: eng.tensor_mul(
                ctx.af.sl(o_t, 256),
                ctx.xn[:, dc * T + tb * 256: dc * T + (tb + 1) * 256],
                ctx.ab.sl(o_g + 8 * 256, 256)),
                r=[ctx.xn_r[dc], r_g], w=[r_t], tag=f"l0f_i1{tb}_{dc}")
            p.emit("dve", lambda eng, dc=dc, tb=tb: eng.tensor_add(
                ctx.xT[:, dc * T + tb * 256: dc * T + (tb + 1) * 256],
                ctx.xT[:, dc * T + tb * 256: dc * T + (tb + 1) * 256],
                ctx.af.sl(o_t, 256)),
                r=[r_t, ctx.xT_r[dc]], w=[ctx.xT_r[dc]], tag=f"l0f_i2{tb}_{dc}")


def emit_l0(ctx, reg):
    emit_ln(ctx, "l0n_")
    tr = lambda k: (lambda inp, k=k: np.ascontiguousarray(np.asarray(inp[k]).T))
    cfg = dict(
        wq_t=reg.dram_in("L0wq_t", [D, D], tr("L0_attn_wq")),
        wk_t=reg.dram_in("L0wk_t", [D, D], tr("L0_attn_wk")),
        wv_t=reg.dram_in("L0wv_t", [D, D], tr("L0_attn_wv")),
        wo_t=reg.dram_in("L0wo_t", [D, D], tr("L0_attn_wo")),
        bq=bias_row(ctx, reg, "L0bq", "L0_attn_bq"),
        bk=bias_row(ctx, reg, "L0bk", "L0_attn_bk"),
        bv=bias_row(ctx, reg, "L0bv", "L0_attn_bv"),
        bo=bias_row(ctx, reg, "L0bo", "L0_attn_bo"),
        rope=False, kind="band", scale=1.0 / 8.0)
    emit_attn_common(ctx, "l0a_", cfg)
    emit_l0_ffn(ctx, reg)


def emit_l2_ffn(ctx, reg):
    p = ctx.p
    w1 = reg.dram_in("L2w1", [8 * D, D], lambda inp: np.ascontiguousarray(
        inp["L2_ffn_w1"].reshape(8 * D, D)))
    w2 = reg.dram_in("L2w2", [8 * D, D], lambda inp: np.ascontiguousarray(
        inp["L2_ffn_w2"].reshape(8 * D, D)))
    rw_t = reg.dram_in("L2rw", [D, 8], lambda inp: inp["L2_ffn_router_w"])
    sgu = reg.dram_in("L2sgu", [D, 2048],
                      lambda inp: inp["L2_ffn_shared_gate_up"])
    sdn = reg.dram_in("L2sdn", [D, D], lambda inp: inp["L2_ffn_shared_down"])
    sgw = reg.dram_in("L2sgw", [128, 8], lambda inp: np.ascontiguousarray(
        np.asarray(inp["L2_ffn_shared_gate_w"]).reshape(8, 128).T))
    ctx.ab.reset()
    ctx.af.reset()
    nE = 8
    o_gt, r_gt = emit_router(ctx, reg, "l2r_", rw_t, nE, None, False)
    o_h, r_h = ctx.ab.take(8 * 256, "h")
    o_wb, r_wb = ctx.ab.take(4096, "wblk")
    o_g, r_g = ctx.ab.take(nE * 256, "G")
    o_bc, r_bc = ctx.ab.take(128, "bcast")
    o_gs, r_gs = ctx.ab.take(8 * 256, "gsh")
    o_us, r_us = ctx.ab.take(8 * 256, "ush")
    o_sa, r_sa = ctx.ab.take(8 * 256, "sact")
    o_sr, r_sr = ctx.ab.take(256, "sigrep")
    o_sw, r_sw = ctx.ab.take(8, "sgw")
    o_sd2, r_sd2 = ctx.ab.take(4096, "sdnblk")
    o_row, r_row = ctx.af.take(256, "sigrow")
    p.emit("pool", lambda e: e.dma_start(
        out=bass.AP(ctx.ab.sb, o_sw, [[ctx.ab.cols, 128], [1, 8]]),
        in_=sgw.ap()), w=[r_sw], dma=True, tag="l2f_lsgw")
    for tb in range(8):
        emit_g_block(ctx, "l2g_", o_gt, r_gt, nE, tb, o_g, r_g, o_bc, r_bc)
        for half in range(4):
            load_w_dc_slice(ctx, o_wb, r_wb, sgu, half * 512, 512, ctx.ab,
                            f"l2f_lgu{tb}_{half}")
            for j in range(4):
                oc = half * 4 + j
                pb, pbr = ctx.pb[4 + j % 2], ctx.pb_r[4 + j % 2]
                for dc in range(8):
                    p.emit("pe", lambda e, dc=dc, j=j, tb=tb, pb=pb: e.matmul(
                        pb[:, 0:256],
                        ctx.ab.sl(o_wb + dc * 512 + j * 128, 128),
                        ctx.xn[:, dc * T + tb * 256: dc * T + (tb + 1) * 256],
                        start=(dc == 0), stop=(dc == 7)),
                        r=[r_wb, ctx.xn_r[dc]], w=[pbr],
                        tag=f"l2f_gu{tb}_{oc}_{dc}")
                if oc < 8:
                    p.emit("act", lambda e, oc=oc, pb=pb: e.activation(
                        ctx.ab.sl(o_gs + oc * 256, 256), pb[:, 0:256], AF.Silu),
                        r=[pbr], w=[r_gs], tag=f"l2f_sl{tb}_{oc}")
                else:
                    p.emit("dve", lambda e, oc=oc, pb=pb: e.tensor_copy(
                        ctx.ab.sl(o_us + (oc - 8) * 256, 256), pb[:, 0:256]),
                        r=[pbr], w=[r_us], tag=f"l2f_up{tb}_{oc}")
        pb6, pbr6 = ctx.pb[6], ctx.pb_r[6]
        for dc in range(8):
            p.emit("pe", lambda e, dc=dc, tb=tb, pb6=pb6: e.matmul(
                pb6[0:1, 0:256], ctx.ab.sl(o_sw + dc, 1),
                ctx.xn[:, dc * T + tb * 256: dc * T + (tb + 1) * 256],
                start=(dc == 0), stop=(dc == 7)),
                r=[r_sw, ctx.xn_r[dc]], w=[pbr6], tag=f"l2f_sg{tb}_{dc}")
        p.emit("act", lambda e, pb6=pb6: e.activation(
            ctx.af.sl(o_row, 256)[0:1, :], pb6[0:1, 0:256], AF.Sigmoid),
            r=[pbr6], w=[r_row], tag=f"l2f_si{tb}")
        p.emit("pe", lambda e, pb6=pb6: e.matmul(
            pb6[:, 0:256], ctx.ones_row_f[:],
            ctx.af.sl(o_row, 256)[0:1, :], start=True, stop=True),
            r=[r_row, ctx.ones_row_f_r], w=[pbr6], tag=f"l2f_sr{tb}")
        p.emit("dve", lambda e, pb6=pb6: e.tensor_copy(
            ctx.ab.sl(o_sr, 256), pb6[:, 0:256]),
            r=[pbr6], w=[r_sr], tag=f"l2f_se{tb}")
        for oc in range(8):
            p.emit("dve", lambda e, oc=oc: e.tensor_mul(
                ctx.ab.sl(o_sa + oc * 256, 256),
                ctx.ab.sl(o_gs + oc * 256, 256),
                ctx.ab.sl(o_us + oc * 256, 256)),
                r=[r_gs, r_us], w=[r_sa], tag=f"l2f_m1{tb}_{oc}")
            p.emit("dve", lambda e, oc=oc: e.tensor_mul(
                ctx.ab.sl(o_sa + oc * 256, 256),
                ctx.ab.sl(o_sa + oc * 256, 256),
                ctx.ab.sl(o_sr, 256)),
                r=[r_sa, r_sr], w=[r_sa], tag=f"l2f_m2{tb}_{oc}")
        for half in range(2) if "shared" not in L2_SKIP else []:
            load_w_dc_slice(ctx, o_sd2, r_sd2, sdn, half * 512, 512, ctx.ab,
                            f"l2f_lsd{tb}_{half}")
            for dc in range(4):
                dcc = half * 4 + dc
                ybank, ybr = ctx.pb[dcc // 2], ctx.pb_r[dcc // 2]
                ycol = (dcc % 2) * 256
                for fc in range(8):
                    p.emit("pe", lambda e, fc=fc, dc=dc, ybank=ybank, ycol=ycol,
                           st=(fc == 0 and dcc % 2 == 0),
                           lst=(fc == 7 and dcc % 2 == 1 and "experts" in L2_SKIP): e.matmul(
                        ybank[:, ycol:ycol + 256],
                        ctx.ab.sl(o_sd2 + fc * 512 + dc * 128, 128),
                        ctx.ab.sl(o_sa + fc * 256, 256),
                        start=st, stop=lst),
                        r=[r_sd2, r_sa], w=[ybr], tag=f"l2f_sd{tb}_{dcc}_{fc}")
        for e_ in range(8) if "experts" not in L2_SKIP else []:
            for half in range(2):
                def ldw1(eng, e_=e_, half=half):
                    dst = bass.AP(ctx.ab.sb, o_wb,
                                  [[ctx.ab.cols, 128], [512, 8], [1, 512]])
                    src = bass.AP(w1, e_ * D * D + half * 512,
                                  [[D, 128], [D * 128, 8], [1, 512]])
                    return eng.dma_start(out=dst, in_=src)
                p.emit("pool", ldw1, w=[r_wb], dma=True,
                       tag=f"l2f_lw1_{tb}_{e_}_{half}")
                for fc in range(4):
                    fci = half * 4 + fc
                    pb, pbr = ctx.pb[4 + fc % 2], ctx.pb_r[4 + fc % 2]
                    for dc in range(8):
                        p.emit("pe", lambda eng, dc=dc, fc=fc, tb=tb, pb=pb: eng.matmul(
                            pb[:, 0:256],
                            ctx.ab.sl(o_wb + dc * 512 + fc * 128, 128),
                            ctx.xn[:, dc * T + tb * 256: dc * T + (tb + 1) * 256],
                            start=(dc == 0), stop=(dc == 7)),
                            r=[r_wb, ctx.xn_r[dc]], w=[pbr],
                            tag=f"l2f_h{tb}_{e_}_{fci}_{dc}")
                    p.emit("act", lambda eng, fci=fci, pb=pb: eng.activation(
                        ctx.ab.sl(o_h + fci * 256, 256), pb[:, 0:256], AF.Silu),
                        r=[pbr], w=[r_h], tag=f"l2f_sh{tb}_{e_}_{fci}")
                    p.emit("dve", lambda eng, fci=fci, e_=e_: eng.tensor_mul(
                        ctx.ab.sl(o_h + fci * 256, 256),
                        ctx.ab.sl(o_h + fci * 256, 256),
                        ctx.ab.sl(o_g + e_ * 256, 256)),
                        r=[r_h, r_g], w=[r_h], tag=f"l2f_gm{tb}_{e_}_{fci}")
            for half in range(2):
                def ldw2(eng, e_=e_, half=half):
                    dst = bass.AP(ctx.ab.sb, o_wb,
                                  [[ctx.ab.cols, 128], [1024, 4], [1, 1024]])
                    src = bass.AP(w2, (e_ * D + half * 512) * D,
                                  [[D, 128], [D * 128, 4], [1, 1024]])
                    return eng.dma_start(out=dst, in_=src)
                p.emit("pool", ldw2, w=[r_wb], dma=True,
                       tag=f"l2f_lw2_{tb}_{e_}_{half}")
                for dc in range(8):
                    ybank, ybr = ctx.pb[dc // 2], ctx.pb_r[dc // 2]
                    ycol = (dc % 2) * 256
                    for fcl in range(4):
                        fci = half * 4 + fcl
                        last = (e_ == 7 and half == 1 and fcl == 3 and dc % 2 == 1)
                        st = (e_ == 0 and half == 0 and fcl == 0 and dc % 2 == 0
                              and "shared" in L2_SKIP)
                        p.emit("pe", lambda eng, dc=dc, fcl=fcl, fci=fci,
                               ybank=ybank, ycol=ycol, last=last, st=st: eng.matmul(
                            ybank[:, ycol:ycol + 256],
                            ctx.ab.sl(o_wb + fcl * 1024 + dc * 128, 128),
                            ctx.ab.sl(o_h + fci * 256, 256),
                            start=st, stop=last),
                            r=[r_wb, r_h], w=[ybr],
                            tag=f"l2f_y{tb}_{e_}_{half}_{dc}_{fcl}")
        for dc in range(8):
            ybank, ybr = ctx.pb[dc // 2], ctx.pb_r[dc // 2]
            ycol = (dc % 2) * 256
            p.emit("dve", lambda eng, dc=dc, tb=tb, ybank=ybank, ycol=ycol: eng.tensor_add(
                ctx.xT[:, dc * T + tb * 256: dc * T + (tb + 1) * 256],
                ctx.xT[:, dc * T + tb * 256: dc * T + (tb + 1) * 256],
                ybank[:, ycol:ycol + 256]),
                r=[ybr, ctx.xT_r[dc]], w=[ctx.xT_r[dc]], tag=f"l2f_ye{tb}_{dc}")


def emit_l2(ctx, reg):
    emit_ln(ctx, "l2n_")
    qkv_t = reg.dram_in("L2qkv_t", [D, 3 * D],
                        lambda inp: np.ascontiguousarray(inp["L2_attn_w_qkv"].T))
    cfg = dict(
        wq_t=(qkv_t, 0), wk_t=(qkv_t, D), wv_t=(qkv_t, 2 * D),
        wo_t=reg.dram_in("L2wp_t", [D, D],
                         lambda inp: np.ascontiguousarray(inp["L2_attn_w_proj"].T)),
        bo=bias_row(ctx, reg, "L2bp", "L2_attn_b_proj"),
        rope=False, kind="full", scale=1.0 / 8.0)
    emit_attn_common(ctx, "l2a_", cfg)
    emit_l2_ffn(ctx, reg)


def l1_ranges(u):
    s = (64 + 512 * u) % T
    if s + 512 <= T:
        return [(s, 512)]
    return [(s, T - s), (0, 512 - (T - s))]


def emit_l1_attn(ctx, reg):
    p = ctx.p

    def perm_cols():
        o = np.arange(3 * D)
        j, rest = o // D, o % D
        h, dk = rest // DH, rest % DH
        return h * 192 + j * DH + dk

    qkv = reg.dram_in("L1qkv", [D, 3 * D],
                      lambda inp: np.ascontiguousarray(
                          np.asarray(inp["L1_attn_w_qkv"])[:, perm_cols()]))
    bias = reg.dram_in("L1bqkv", [1, 3 * D],
                       lambda inp: np.ascontiguousarray(
                           np.asarray(inp["L1_attn_b_qkv"])[perm_cols()]).reshape(1, -1))
    wo = reg.dram_in("L1wo", [D, D], lambda inp: inp["L1_attn_w_o"])
    bsb = p.sbuf("l1_bqkv", [1, 3 * D], BF16)
    bres = Res("l1bq")
    p.emit("pool", lambda e: e.dma_start(out=bsb[:], in_=bias.ap()),
           w=[bres], dma=True, tag="l1a_ldb")
    bo = bias_row(ctx, reg, "L1bo", "L1_attn_b_o")
    s = 1.0 / 8.0
    SH = 30.0
    ctx.ab.reset()
    ctx.af.reset()
    aug = [ctx.ab.take(512, nm) for nm in ("k65a", "k65b", "q65a", "q65b")]
    (o_ka, r_ka), (o_kb, r_kb), (o_qa, r_qa), (o_qb, r_qb) = aug
    o_vt, r_vt = ctx.ab.take(4 * 16 * 65, "v_tok")
    o_w, r_w = ctx.ab.take(4096, "wslice")
    o_wo, r_wo = ctx.ab.take(8192, "wo")
    o_e, r_e = ctx.ab.take(512, "expb")
    o_sqs, r_sqs = ctx.ab.take(512, "sqs")
    o_rec, r_rec = ctx.af.take(512, "recb")
    o_rr, r_rr = ctx.af.take(640, "rrow")
    o_nqa, r_nqa = ctx.af.take(512, "nqa")
    o_nqb, r_nqb = ctx.af.take(512, "nqb")
    rrow = ctx.af.sl(o_rr, 640)
    recb = ctx.af.sl(o_rec, 512)
    VT = 16 * 65
    load_w_dc_slice(ctx, o_wo, r_wo, wo, 0, 1024, ctx.ab, "l1a_lwo")
    for o_k, r_k in ((o_ka, r_ka), (o_kb, r_kb)):
        p.emit("dve", lambda e, o_k=o_k: e.memset(
            ctx.ab.sb[64:65, o_k:o_k + 512], 1.0), w=[r_k], tag="l1a_kones")

    for u in range(4):
        rngs = l1_ranges(u)
        for half in range(2) if 'v' in L1A_PARTS else []:
            load_w_dc_slice(ctx, o_w, r_w, qkv, 2 * D + half * 512, 512,
                            ctx.ab, f"l1a_lv{u}_{half}")
            for wi in range(4):
                pb, pbr = ctx.pb[2 + wi % 2], ctx.pb_r[2 + wi % 2]
                mb = 0
                for (rs, rl) in rngs:
                    t0 = wi * 128
                    a = max(t0, mb)
                    b = min(t0 + 128, mb + rl)
                    if a < b:
                        for dc in range(8):
                            p.emit("pe", lambda e, dc=dc, pb=pb, a=a, b=b,
                                   rs=rs, mb=mb, t0=t0: e.matmul(
                                pb[a - t0:b - t0, 0:512],
                                ctx.xn[:, dc * T + rs + (a - mb):
                                       dc * T + rs + (b - mb)],
                                ctx.ab.sl(o_w + dc * 512, 512),
                                start=(dc == 0), stop=False),
                                r=[ctx.xn_r[dc], r_w], w=[pbr],
                                tag=f"l1a_v{u}_{half}_{wi}_{a}_{dc}")
                        p.emit("pe", lambda e, pb=pb, a=a, b=b, t0=t0,
                               half=half: e.matmul(
                            pb[a - t0:b - t0, 0:512],
                            ctx.ones_row_b[0:1, 0:b - a],
                            bsb[0:1, 2 * D + half * 512: 2 * D + (half + 1) * 512],
                            start=False, stop=True),
                            r=[bres, ctx.ones_row_b_r], w=[pbr],
                            tag=f"l1a_vb{u}_{half}_{wi}_{a}")
                    mb += rl

                def evv(e, wi=wi, half=half, pb=pb):
                    dst = bass.AP(ctx.ab.sb, o_vt + wi * VT + half * 8 * 65,
                                  [[ctx.ab.cols, 128], [65, 8], [1, 64]])
                    return e.tensor_copy(dst, pb[:, 0:512])
                p.emit("dve", evv, r=[pbr], w=[r_vt], tag=f"l1a_ve{u}_{half}_{wi}")

        def setvones(e):
            dst = bass.AP(ctx.ab.sb, o_vt + 64,
                          [[ctx.ab.cols, 128], [VT, 4], [65, 16]])
            return e.memset(dst, 1.0)
        p.emit("dve", setvones, r=[], w=[r_vt], tag=f"l1a_vo{u}")

        for c in range(8) if 'proj' in L1A_PARTS else []:
            def proj(what, base, dsts, nq_mode, c=c, u=u, rngs=rngs):
                load_w_dc_slice(ctx, o_w, r_w, qkv, base + c * 128, 128,
                                ctx.ab, f"l1a_lw{what}{u}_{c}")
                pbq = ctx.pb[4]
                mb = 0
                for (rs, rl) in rngs:
                    for dc in range(8):
                        p.emit("pe", lambda e, dc=dc, rs=rs, rl=rl, mb=mb,
                               pbq=pbq: e.matmul(
                            pbq[:, mb:mb + rl],
                            ctx.ab.sl(o_w + dc * 128, 128),
                            ctx.xn[:, dc * T + rs: dc * T + rs + rl],
                            start=(dc == 0), stop=False),
                            r=[r_w, ctx.xn_r[dc]], w=[ctx.pb_r[4]],
                            tag=f"l1a_{what}mm{u}_{c}_{mb}_{dc}")
                    p.emit("pe", lambda e, rl=rl, mb=mb, pbq=pbq, base=base: e.matmul(
                        pbq[:, mb:mb + rl],
                        bsb[0:1, base + c * 128: base + (c + 1) * 128],
                        ctx.ones_512_b[0:1, 0:rl], start=False, stop=True),
                        r=[bres, ctx.ones_512_b_r], w=[ctx.pb_r[4]],
                        tag=f"l1a_{what}b{u}_{c}_{mb}")
                    mb += rl
                p.emit("act", lambda e, pbq=pbq: e.activation(
                    ctx.ab.sl(o_sqs, 512), pbq[:], AF.Square),
                    r=[ctx.pb_r[4]], w=[r_sqs], tag=f"l1a_{what}sq{u}_{c}")
                for hh in range(2):
                    pbn = ctx.pb[6 + hh]
                    p.emit("pe", lambda e, hh=hh, pbn=pbn: e.matmul(
                        pbn[0:1, :], ctx.ones_col_b[hh * 64:(hh + 1) * 64, :],
                        ctx.ab.sb[hh * 64:(hh + 1) * 64, o_sqs:o_sqs + 512],
                        start=True, stop=True),
                        r=[r_sqs, ctx.ones_col_b_r], w=[ctx.pb_r[6 + hh]],
                        tag=f"l1a_{what}nm{u}_{c}_{hh}")
                    if nq_mode:
                        o_nq = (o_nqa, o_nqb)[hh]
                        r_nq = (r_nqa, r_nqb)[hh]
                        p.emit("dve", lambda e, pbn=pbn, o_nq=o_nq: e.tensor_copy(
                            ctx.af.sb[0:1, o_nq:o_nq + 512], pbn[0:1, :]),
                            r=[ctx.pb_r[6 + hh]], w=[r_nq],
                            tag=f"l1a_nq{u}_{c}_{hh}")
                    else:
                        p.emit("dve", lambda e, hh=hh, pbn=pbn: e.reduce_max(
                            rrow[0:1, 40 + hh:41 + hh], pbn[0:1, :], AX.X),
                            r=[ctx.pb_r[6 + hh]], w=[r_rr],
                            tag=f"l1a_nk{u}_{c}_{hh}")
                for hh, (o_d, r_d) in enumerate(dsts):
                    p.emit("dve", lambda e, hh=hh, o_d=o_d, pbq=pbq: e.tensor_copy(
                        ctx.ab.sb[0:64, o_d:o_d + 512],
                        pbq[hh * 64:(hh + 1) * 64, :]),
                        r=[ctx.pb_r[4]], w=[r_d], tag=f"l1a_{what}ev{u}_{c}_{hh}")

            proj("k", D, ((o_ka, r_ka), (o_kb, r_kb)), False)
            proj("q", 0, ((o_qa, r_qa), (o_qb, r_qb)), True)
            for hh in range(2):
                o_q65 = (o_qa, o_qb)[hh]
                r_q65 = (r_qa, r_qb)[hh]
                o_nq = (o_nqa, o_nqb)[hh]
                r_nq = (r_nqa, r_nqb)[hh]
                p.emit("dve", lambda e, hh=hh, o_nq=o_nq: e.tensor_scalar_mul(
                    ctx.af.sb[0:1, o_nq:o_nq + 512],
                    ctx.af.sb[0:1, o_nq:o_nq + 512],
                    rrow[0:1, 40 + hh:41 + hh]),
                    r=[r_nq, r_rr], w=[r_nq], tag=f"l1a_bm{u}_{c}_{hh}",
                    selfsync=True)
                p.emit("act", lambda e, o_nq=o_nq: e.activation(
                    ctx.af.sb[0:1, o_nq:o_nq + 512],
                    ctx.af.sb[0:1, o_nq:o_nq + 512], AF.Sqrt),
                    r=[r_nq], w=[r_nq], tag=f"l1a_bq{u}_{c}_{hh}")
                p.emit("dve", lambda e, o_nq=o_nq, o_q65=o_q65: e.tensor_scalar(
                    ctx.ab.sb[64:65, o_q65:o_q65 + 512],
                    ctx.af.sb[0:1, o_nq:o_nq + 512], -1.0, SH / s,
                    OP.mult, OP.add),
                    r=[r_nq], w=[r_q65], tag=f"l1a_bw{u}_{c}_{hh}")

            for hh in range(2) if 'scores' in L1A_PARTS else []:
                h = 2 * c + hh
                o_k65 = (o_ka, o_kb)[hh]
                r_k65 = (r_ka, r_kb)[hh]
                o_q65 = (o_qa, o_qb)[hh]
                r_q65 = (r_qa, r_qb)[hh]
                av_bank, av_r = ctx.pb[5], ctx.pb_r[5]
                for wi in range(4):
                    sb, sbr = ctx.pb[wi % 2], ctx.pb_r[wi % 2]
                    p.emit("pe", lambda e, wi=wi, sb=sb, o_k65=o_k65,
                           o_q65=o_q65: e.matmul(
                        sb[:, 0:128],
                        ctx.ab.sb[0:65, o_k65 + wi * 128: o_k65 + (wi + 1) * 128],
                        ctx.ab.sb[0:65, o_q65 + wi * 128: o_q65 + (wi + 1) * 128],
                        start=True, stop=False),
                        r=[r_k65, r_q65], w=[sbr], tag=f"l1a_sc{u}_{h}_{wi}")
                    p.emit("pe", lambda e, sb=sb: e.matmul(
                        sb[:, 0:128], ctx.mk_l1[:], ctx.id_b[:],
                        start=False, stop=True),
                        r=[ctx.mk_l1_r, ctx.id_b_r], w=[sbr],
                        tag=f"l1a_mk{u}_{h}_{wi}")
                    p.emit("act", lambda e, sb=sb: e.activation(
                        ctx.ab.sb[:, o_e:o_e + 128], sb[:, 0:128],
                        AF.Exp, scale=s),
                        r=[sbr], w=[r_e], tag=f"l1a_e{u}_{h}_{wi}")
                    if 'av' in L1A_PARTS:
                        p.emit("pe", lambda e, wi=wi, av_bank=av_bank, h=h: e.matmul(
                            av_bank[0:65, wi * 128: wi * 128 + 128],
                            ctx.ab.sb[0:128,
                                      o_vt + wi * VT + h * 65:
                                      o_vt + wi * VT + h * 65 + 65],
                            ctx.ab.sb[:, o_e:o_e + 128],
                            start=(wi == 0), stop=(wi == 3)),
                            r=[r_vt, r_e], w=[av_r],
                            tag=f"l1a_av{u}_{h}_{wi}")
                if 'recip' not in L1A_PARTS:
                    continue
                p.emit("dve", lambda e, av_bank=av_bank: e.tensor_copy(
                    rrow[0:1, 128:640], av_bank[64:65, :]),
                    r=[av_r], w=[r_rr], tag=f"l1a_s{u}_{h}")
                p.emit("act", lambda e: e.activation(
                    rrow[0:1, 128:640], rrow[0:1, 128:640], AF.Ln),
                    r=[r_rr], w=[r_rr], tag=f"l1a_sl{u}_{h}")
                p.emit("act", lambda e: e.activation(
                    rrow[0:1, 128:640], rrow[0:1, 128:640], AF.Exp, scale=-1.0),
                    r=[r_rr], w=[r_rr], tag=f"l1a_sx{u}_{h}")
                pb7b = ctx.pb[7]
                p.emit("pe", lambda e, pb7b=pb7b: e.matmul(
                    pb7b[:, :], ctx.ones_row_f[:], rrow[0:1, 128:640],
                    start=True, stop=True),
                    r=[r_rr, ctx.ones_row_f_r], w=[ctx.pb_r[7]],
                    tag=f"l1a_sr{u}_{h}")
                p.emit("dve", lambda e, pb7b=pb7b: e.tensor_copy(
                    recb[0:64, :], pb7b[0:64, :]),
                    r=[ctx.pb_r[7]], w=[r_rec], tag=f"l1a_se{u}_{h}")

                def hev(e, hb=hh * 64, u=u, c=c, av_bank=av_bank):
                    return e.tensor_mul(
                        ctx.heads[hb:hb + 64,
                                  c * T + u * 512: c * T + (u + 1) * 512],
                        av_bank[0:64, :], recb[0:64, :])
                p.emit("dve", hev, r=[av_r, r_rec], w=[ctx.heads_r[c]],
                       tag=f"l1a_h{u}_{h}")

        for oc in range(8) if 'oproj' in L1A_PARTS else []:
            pb, pbr = ctx.pb[2 + oc % 2], ctx.pb_r[2 + oc % 2]
            for dc in range(8):
                p.emit("pe", lambda e, oc=oc, dc=dc, u=u, pb=pb: e.matmul(
                    pb[:],
                    ctx.ab.sl(o_wo + dc * 1024 + oc * 128, 128),
                    ctx.heads[:, dc * T + u * 512: dc * T + (u + 1) * 512],
                    start=(dc == 0), stop=False),
                    r=[r_wo, ctx.heads_r[dc]], w=[pbr],
                    tag=f"l1a_o{u}_{oc}_{dc}")
            brow, brr, boff = bo
            p.emit("pe", lambda e, oc=oc, pb=pb, brow=brow, boff=boff: e.matmul(
                pb[:], brow[0:1, boff + oc * 128: boff + (oc + 1) * 128],
                ctx.ones_512_b[:], start=False, stop=True),
                r=[brr, ctx.ones_512_b_r], w=[pbr], tag=f"l1a_ob{u}_{oc}")
            mb = 0
            for (rs, rl) in rngs:
                p.emit("dve", lambda e, oc=oc, rs=rs, rl=rl, mb=mb, pb=pb: e.tensor_add(
                    ctx.xT[:, oc * T + rs: oc * T + rs + rl],
                    ctx.xT[:, oc * T + rs: oc * T + rs + rl],
                    pb[:, mb:mb + rl]),
                    r=[pbr, ctx.xT_r[oc]], w=[ctx.xT_r[oc]],
                    tag=f"l1a_oe{u}_{oc}_{mb}")
                mb += rl


def emit_l1_ffn(ctx, reg):
    p = ctx.p
    gw_t = reg.dram_in("L1gw", [D, 8],
                       lambda inp: np.ascontiguousarray(np.asarray(inp["L1_ffn_gate_w"]).T))
    wg_t = reg.dram_in("L1wg", [D, 4096],
                       lambda inp: np.ascontiguousarray(np.asarray(inp["L1_ffn_w_gate"]).T))
    wu_t = reg.dram_in("L1wu", [D, 4096],
                       lambda inp: np.ascontiguousarray(np.asarray(inp["L1_ffn_w_up"]).T))
    wd_t = reg.dram_in("L1wd", [4096, D],
                       lambda inp: np.ascontiguousarray(np.asarray(inp["L1_ffn_w_down"]).T))
    a1c = reg.dram_in("L1a1", [D, 128],
                      lambda inp: np.ascontiguousarray(
                          np.asarray(inp["L1_ffn_A1"]).transpose(2, 0, 1).reshape(D, 128)))
    a3c = reg.dram_in("L1a3", [D, 128],
                      lambda inp: np.ascontiguousarray(
                          np.asarray(inp["L1_ffn_A3"]).transpose(2, 0, 1).reshape(D, 128)))
    b1c = reg.dram_in("L1b1", [16, 8 * 4096],
                      lambda inp: np.ascontiguousarray(
                          np.asarray(inp["L1_ffn_B1"]).transpose(2, 0, 1).reshape(16, 8 * 4096)))
    b3c = reg.dram_in("L1b3", [16, 8 * 4096],
                      lambda inp: np.ascontiguousarray(
                          np.asarray(inp["L1_ffn_B3"]).transpose(2, 0, 1).reshape(16, 8 * 4096)))
    a2c = reg.dram_in("L1a2", [4096, 128],
                      lambda inp: np.ascontiguousarray(
                          np.asarray(inp["L1_ffn_A2"]).transpose(2, 0, 1).reshape(4096, 128)))
    b2c = reg.dram_in("L1b2", [16, 8 * D],
                      lambda inp: np.ascontiguousarray(
                          np.asarray(inp["L1_ffn_B2"]).transpose(2, 0, 1).reshape(16, 8 * D)))
    ctx.ab.reset()
    ctx.af.reset()
    nE = 8
    o_gt, r_gt = emit_router(ctx, reg, "l1r_", gw_t, nE, None, True)
    o_wgu, r_wgu = ctx.ab.take(8192, "wgu_b2")
    o_wd, r_wd = ctx.ab.take(2048, "wdblk")
    o_b13, r_b13 = ctx.ab.take(1024, "b13blk")
    o_a2, r_a2 = ctx.ab.take(512, "a2blk")
    o_g, r_g = ctx.ab.take(nE * 256, "G")
    o_bc, r_bc = ctx.ab.take(128, "bcast")
    o_t13, r_t13 = ctx.ab.take(4096, "t13_zg")
    o_c1, r_c1 = ctx.ab.take(1024, "c1")
    o_c3, r_c3 = ctx.ab.take(1024, "c3")
    o_us, r_us = ctx.ab.take(512, "us")
    o_ag, r_ag = ctx.ab.take(512, "ag")
    o_mx, r_mx = ctx.ab.take(1024, "mix")
    o_z, r_z = ctx.af.take(2048, "z")

    def t_slot(si, e_):
        return o_t13 + (si * 8 + e_) * 256

    for tb in range(8):
        emit_g_block(ctx, "l1g_", o_gt, r_gt, nE, tb, o_g, r_g, o_bc, r_bc)
        load_w_dc_slice(ctx, o_wgu, r_wgu, a1c, 0, 128, ctx.ab, f"l1f_la1{tb}")
        load_w_dc_slice(ctx, o_wgu + 1024, r_wgu, a3c, 0, 128, ctx.ab,
                        f"l1f_la3{tb}")
        for si in range(2):
            for e_ in range(8):
                pb, pbr = ctx.pb[6], ctx.pb_r[6]
                for dc in range(8):
                    p.emit("pe", lambda e, dc=dc, tb=tb, si=si, e_=e_, pb=pb: e.matmul(
                        pb[0:16, 0:256],
                        bass.AP(ctx.ab.sb, o_wgu + si * 1024 + dc * 128 + e_ * 16,
                                [[ctx.ab.cols, 128], [1, 16]]),
                        ctx.xn[:, dc * T + tb * 256: dc * T + (tb + 1) * 256],
                        start=(dc == 0), stop=(dc == 7)),
                        r=[r_wgu, ctx.xn_r[dc]], w=[pbr],
                        tag=f"l1f_t{tb}_{si}_{e_}_{dc}")
                p.emit("dve", lambda e, si=si, e_=e_, pb=pb: e.tensor_scalar_mul(
                    ctx.ab.sb[0:16, t_slot(si, e_): t_slot(si, e_) + 256],
                    pb[0:16, 0:256], 2.0),
                    r=[pbr], w=[r_t13], tag=f"l1f_te{tb}_{si}_{e_}")
        p.emit("dve", lambda e: e.memset(
            ctx.af.sb[0:16, o_z:o_z + 2048], 0.0), w=[r_z], tag=f"l1f_z0{tb}")
        for fb in range(8):
            load_w_dc_slice(ctx, o_wgu, r_wgu, wg_t, fb * 512, 512, ctx.ab,
                            f"l1f_lg{tb}_{fb}")
            load_w_dc_slice(ctx, o_wgu + 4096, r_wgu, wu_t, fb * 512, 512,
                            ctx.ab, f"l1f_lu{tb}_{fb}")

            def lda2(e, fb=fb):
                dst = bass.AP(ctx.ab.sb, o_a2,
                              [[ctx.ab.cols, 128], [128, 4], [1, 128]])
                src = bass.AP(a2c, fb * 512 * 128,
                              [[128, 128], [128 * 128, 4], [1, 128]])
                return e.dma_start(out=dst, in_=src)
            p.emit("pool", lda2, w=[r_a2], dma=True, tag=f"l1f_la2{tb}_{fb}")
            for si, (oc_, rc_) in enumerate(((o_c1, r_c1), (o_c3, r_c3))):
                for fc in range(4):
                    pb, pbr = ctx.pb[4 + si], ctx.pb_r[4 + si]
                    for dc in range(8):
                        p.emit("pe", lambda e, dc=dc, fc=fc, si=si, tb=tb,
                               pb=pb: e.matmul(
                            pb[:, (fc % 2) * 256:(fc % 2) * 256 + 256],
                            ctx.ab.sl(o_wgu + si * 4096 + dc * 512 + fc * 128, 128),
                            ctx.xn[:, dc * T + tb * 256: dc * T + (tb + 1) * 256],
                            start=(dc == 0 and fc % 2 == 0),
                            stop=(dc == 7 and fc % 2 == 1)),
                            r=[r_wgu, ctx.xn_r[dc]], w=[pbr],
                            tag=f"l1f_c{tb}_{fb}_{si}_{fc}_{dc}")
                    if fc % 2 == 1:
                        p.emit("dve", lambda e, fc=fc, oc_=oc_, rc_=rc_, pb=pb:
                               e.tensor_copy(
                            ctx.ab.sl(oc_ + (fc - 1) * 256, 512), pb[:, 0:512]),
                            r=[pbr], w=[rc_], tag=f"l1f_ce{tb}_{fb}_{si}_{fc}")
            for e_ in range(8):
                def ldb1(eng, e_=e_, fb=fb):
                    dst = bass.AP(ctx.ab.sb, o_b13, [[ctx.ab.cols, 16], [1, 512]])
                    src = bass.AP(b1c, e_ * 4096 + fb * 512,
                                  [[8 * 4096, 16], [1, 512]])
                    return eng.dma_start(out=dst, in_=src)
                p.emit("pool", ldb1, w=[r_b13], dma=True,
                       tag=f"l1f_lb1{tb}_{fb}_{e_}")

                def ldb3(eng, e_=e_, fb=fb):
                    dst = bass.AP(ctx.ab.sb, o_b13 + 512,
                                  [[ctx.ab.cols, 16], [1, 512]])
                    src = bass.AP(b3c, e_ * 4096 + fb * 512,
                                  [[8 * 4096, 16], [1, 512]])
                    return eng.dma_start(out=dst, in_=src)
                p.emit("pool", ldb3, w=[r_b13], dma=True,
                       tag=f"l1f_lb3{tb}_{fb}_{e_}")
                pz, pzr = ctx.pb[6], ctx.pb_r[6]
                for fp in range(2):
                    pd1, pd1r = ctx.pb[4], ctx.pb_r[4]
                    pd3, pd3r = ctx.pb[5], ctx.pb_r[5]
                    for fc in (2 * fp, 2 * fp + 1):
                        p.emit("pe", lambda e, fc=fc, e_=e_, pd1=pd1: e.matmul(
                            pd1[:, (fc % 2) * 256:(fc % 2) * 256 + 256],
                            ctx.ab.sb[0:16, o_b13 + fc * 128: o_b13 + (fc + 1) * 128],
                            ctx.ab.sb[0:16, t_slot(0, e_): t_slot(0, e_) + 256],
                            start=(fc % 2 == 0), stop=(fc % 2 == 1)),
                            r=[r_b13, r_t13], w=[pd1r],
                            tag=f"l1f_d1_{tb}_{fb}_{e_}_{fc}")
                        p.emit("pe", lambda e, fc=fc, e_=e_, pd3=pd3: e.matmul(
                            pd3[:, (fc % 2) * 256:(fc % 2) * 256 + 256],
                            ctx.ab.sb[0:16, o_b13 + 512 + fc * 128:
                                      o_b13 + 512 + (fc + 1) * 128],
                            ctx.ab.sb[0:16, t_slot(1, e_): t_slot(1, e_) + 256],
                            start=(fc % 2 == 0), stop=(fc % 2 == 1)),
                            r=[r_b13, r_t13], w=[pd3r],
                            tag=f"l1f_d3_{tb}_{fb}_{e_}_{fc}")
                    p.emit("dve", lambda e, fp=fp, pd1=pd1: e.scalar_tensor_tensor(
                        ctx.ab.sl(o_us, 512), pd1[:, 0:512], 1.0,
                        ctx.ab.sl(o_c1 + fp * 512, 512), OP.mult, OP.add),
                        r=[pd1r, r_c1], w=[r_us], tag=f"l1f_u{tb}_{fb}_{e_}_{fp}")
                    p.emit("act", lambda e: e.activation(
                        ctx.ab.sl(o_us, 512), ctx.ab.sl(o_us, 512), AF.Silu),
                        r=[r_us], w=[r_us], tag=f"l1f_s{tb}_{fb}_{e_}_{fp}")
                    p.emit("dve", lambda e, fp=fp, pd3=pd3: e.scalar_tensor_tensor(
                        ctx.ab.sl(o_ag, 512), pd3[:, 0:512], 1.0,
                        ctx.ab.sl(o_c3 + fp * 512, 512), OP.mult, OP.add),
                        r=[pd3r, r_c3], w=[r_ag], tag=f"l1f_v{tb}_{fb}_{e_}_{fp}")
                    for fci in range(2):
                        p.emit("dve", lambda e, fci=fci, e_=e_: e.tensor_mul(
                            ctx.ab.sl(o_ag + fci * 256, 256),
                            ctx.ab.sl(o_ag + fci * 256, 256),
                            ctx.ab.sl(o_g + e_ * 256, 256)),
                            r=[r_ag, r_g], w=[r_ag],
                            tag=f"l1f_vg{tb}_{fb}_{e_}_{fp}_{fci}")
                    p.emit("dve", lambda e: e.tensor_mul(
                        ctx.ab.sl(o_ag, 512), ctx.ab.sl(o_ag, 512),
                        ctx.ab.sl(o_us, 512)),
                        r=[r_ag, r_us], w=[r_ag], tag=f"l1f_ag{tb}_{fb}_{e_}_{fp}")
                    if e_ == 0:
                        p.emit("dve", lambda e, fp=fp: e.tensor_copy(
                            ctx.ab.sl(o_mx + fp * 512, 512), ctx.ab.sl(o_ag, 512)),
                            r=[r_ag], w=[r_mx], tag=f"l1f_mx{tb}_{fb}_{e_}_{fp}")
                    else:
                        p.emit("dve", lambda e, fp=fp: e.tensor_add(
                            ctx.ab.sl(o_mx + fp * 512, 512),
                            ctx.ab.sl(o_mx + fp * 512, 512), ctx.ab.sl(o_ag, 512)),
                            r=[r_ag, r_mx], w=[r_mx], tag=f"l1f_mx{tb}_{fb}_{e_}_{fp}")
                    for fc in (2 * fp, 2 * fp + 1):
                        p.emit("pe", lambda e, fc=fc, e_=e_, fp=fp, pz=pz: e.matmul(
                            pz[0:16, 0:256],
                            bass.AP(ctx.ab.sb, o_a2 + fc * 128 + e_ * 16,
                                    [[ctx.ab.cols, 128], [1, 16]]),
                            ctx.ab.sl(o_ag + (fc % 2) * 256, 256),
                            start=(fc == 0), stop=(fc == 3)),
                            r=[r_a2, r_ag], w=[pzr],
                            tag=f"l1f_zm{tb}_{fb}_{e_}_{fc}")
                p.emit("dve", lambda e, e_=e_, pz=pz: e.tensor_add(
                    ctx.af.sb[0:16, o_z + e_ * 256: o_z + (e_ + 1) * 256],
                    ctx.af.sb[0:16, o_z + e_ * 256: o_z + (e_ + 1) * 256],
                    pz[0:16, 0:256]),
                    r=[pzr, r_z], w=[r_z], tag=f"l1f_ze{tb}_{fb}_{e_}")
            for dhalf in range(2):
                def ldwd(e, fb=fb, dhalf=dhalf):
                    dst = bass.AP(ctx.ab.sb, o_wd,
                                  [[ctx.ab.cols, 128], [512, 4], [1, 512]])
                    src = bass.AP(wd_t, fb * 512 * D + dhalf * 512,
                                  [[D, 128], [D * 128, 4], [1, 512]])
                    return e.dma_start(out=dst, in_=src)
                p.emit("pool", ldwd, w=[r_wd], dma=True,
                       tag=f"l1f_lwd{tb}_{fb}_{dhalf}")
                for dc in range(4):
                    dcc = dhalf * 4 + dc
                    ybank, ybr = ctx.pb[dcc // 2], ctx.pb_r[dcc // 2]
                    ycol = (dcc % 2) * 256
                    for fc in range(4):
                        p.emit("pe", lambda e, dc=dc, fc=fc, ybank=ybank,
                               ycol=ycol,
                               st=(fb == 0 and fc == 0 and dcc % 2 == 0): e.matmul(
                            ybank[:, ycol:ycol + 256],
                            ctx.ab.sl(o_wd + fc * 512 + dc * 128, 128),
                            ctx.ab.sl(o_mx + fc * 256, 256),
                            start=st, stop=False),
                            r=[r_wd, r_mx], w=[ybr],
                            tag=f"l1f_y{tb}_{fb}_{dcc}_{fc}")
        def ldb2(e):
            dst = bass.AP(ctx.ab.sb, o_wgu, [[ctx.ab.cols, 16], [1, 8192]])
            src = bass.AP(b2c, 0, [[8 * D, 16], [1, 8192]])
            return e.dma_start(out=dst, in_=src)
        p.emit("pool", ldb2, w=[r_wgu], dma=True, tag=f"l1f_lb2{tb}")
        for e_ in range(8):
            p.emit("dve", lambda e, e_=e_: e.tensor_scalar_mul(
                ctx.ab.sb[0:16, o_t13 + e_ * 256: o_t13 + (e_ + 1) * 256],
                ctx.af.sb[0:16, o_z + e_ * 256: o_z + (e_ + 1) * 256], 2.0),
                r=[r_z], w=[r_t13], tag=f"l1f_zg{tb}_{e_}")
        for dc in range(8):
            ybank, ybr = ctx.pb[dc // 2], ctx.pb_r[dc // 2]
            ycol = (dc % 2) * 256
            for e_ in range(8):
                p.emit("pe", lambda e, dc=dc, e_=e_, ybank=ybank, ycol=ycol,
                       lst=(e_ == 7 and dc % 2 == 1): e.matmul(
                    ybank[:, ycol:ycol + 256],
                    bass.AP(ctx.ab.sb, o_wgu + e_ * D + dc * 128,
                            [[ctx.ab.cols, 16], [1, 128]]),
                    ctx.ab.sb[0:16, o_t13 + e_ * 256: o_t13 + (e_ + 1) * 256],
                    start=False, stop=lst),
                    r=[r_wgu, r_t13], w=[ybr], tag=f"l1f_d2_{tb}_{dc}_{e_}")
        for dc in range(8):
            ybank, ybr = ctx.pb[dc // 2], ctx.pb_r[dc // 2]
            ycol = (dc % 2) * 256
            p.emit("dve", lambda e, dc=dc, tb=tb, ybank=ybank, ycol=ycol: e.tensor_add(
                ctx.xT[:, dc * T + tb * 256: dc * T + (tb + 1) * 256],
                ctx.xT[:, dc * T + tb * 256: dc * T + (tb + 1) * 256],
                ybank[:, ycol:ycol + 256]),
                r=[ybr, ctx.xT_r[dc]], w=[ctx.xT_r[dc]], tag=f"l1f_ye{tb}_{dc}")


def emit_l1(ctx, reg):
    emit_ln(ctx, "l1n_")
    emit_l1_attn(ctx, reg)
    emit_l1_ffn(ctx, reg)


LAYER_EMITTERS = {0: emit_l0, 1: emit_l1, 2: emit_l2, 3: emit_l3}


def build_program(layers, hook=None):
    nc = bass.Bass("TRN2", target_bir_lowering=False)
    p = P(nc)
    reg = Registry(p)
    x_ap = nc.dram_tensor("x", [T, D], F32, kind="ExternalInput").ap()
    out_ap = nc.dram_tensor("out", [T, D], F32, kind="ExternalOutput").ap()
    ctx = Ctx(p, reg)
    emit_load_transpose_in(ctx, x_ap)
    if hook:
        hook(ctx, reg, "after_load")
    for li in layers:
        LAYER_EMITTERS[li](ctx, reg)
        if hook:
            hook(ctx, reg, f"after_l{li}")
    emit_transpose_out_store(ctx, out_ap)
    if hook:
        hook(ctx, reg, "end")
    p.finalize_and_emit()
    return nc, reg, ctx


def kernel(**inputs):
    layers_groups = [[0], [1], [2], [3]]
    x = np.ascontiguousarray(inputs["L0_attn_x"], dtype=np.float32)
    cur = [np.ascontiguousarray(x[c]) for c in range(NCORE)]
    for group in layers_groups:
        nc, reg, _ = build_program(group)
        host = {}
        for name, fn in reg.host.items():
            arr = fn(inputs) if callable(fn) else fn
            host[name] = np.ascontiguousarray(arr, dtype=np.float32)
        in_maps = [dict(host, x=cur[c]) for c in range(NCORE)]
        res = run_bass_kernel_spmd(nc, in_maps, core_ids=list(range(NCORE)))
        cur = [np.asarray(res.results[c]["out"]) for c in range(NCORE)]
    return np.stack(cur, 0)


# version 02
# speedup vs baseline: 1.1477x; 1.1477x over previous
import contextlib
import numpy as np
import concourse.bass as bass
import concourse.mybir as mybir
from concourse.bass_utils import run_bass_kernel_spmd

F32 = mybir.dt.float32
BF16 = mybir.dt.bfloat16
AF = mybir.ActivationFunctionType
OP = mybir.AluOpType
AX = mybir.AxisListType

T = 2048
D = 1024
NH = 16
DH = 64
NCORE = 8
NEG = -1.0e30
ATTN_DEBUG = None
L2_SKIP = set()
L1A_PARTS = {'v', 'proj', 'scores', 'av', 'recip', 'oproj'}
RAW_WIN = 16


ENG_LIST = ("sp", "pool", "pe", "dve", "act")


class Res:
    __slots__ = ("name", "last_writer", "readers")

    def __init__(self, name=""):
        self.name = name
        self.last_writer = None
        self.readers = []


class Op:
    __slots__ = ("eng", "fn", "deps", "dma", "need_inc", "inc_val",
                 "dma_idx", "waits", "tag", "selfsync", "raw_same", "eseq")

    def __init__(self, eng, fn, dma, tag):
        self.selfsync = False
        self.raw_same = None
        self.eseq = 0
        self.eng = eng
        self.fn = fn
        self.dma = dma
        self.deps = set()
        self.need_inc = False
        self.inc_val = 0
        self.dma_idx = 0
        self.waits = []
        self.tag = tag


class P:
    def __init__(self, nc):
        self.nc = nc
        self.ops = []
        self.csem = {}
        self.dsem = {}
        self._es = contextlib.ExitStack()

    def sbuf(self, name, shape, dtype):
        return self.nc.alloc_sbuf_tensor(name, shape, dtype)

    def psum(self, name, shape, dtype=F32):
        return self.nc.alloc_psum_tensor(name, shape, dtype)

    def emit(self, eng, fn, r=(), w=(), deps=(), dma=False, tag="",
             selfsync=False):
        op = Op(eng, fn, dma, tag)
        op.selfsync = selfsync
        dd = op.deps
        for res in r:
            lw = res.last_writer
            if lw is not None:
                dd.add(lw)
                if lw.eng == eng and not lw.dma and eng in ("dve", "act", "pool"):
                    if op.raw_same is None:
                        op.raw_same = set()
                    op.raw_same.add(lw)
        for res in w:
            if res.last_writer is not None:
                dd.add(res.last_writer)
            dd.update(res.readers)
        for d in deps:
            if d is not None:
                dd.add(d)
        dd.discard(op)
        for res in r:
            res.readers.append(op)
        for res in w:
            res.last_writer = op
            res.readers = []
        self.ops.append(op)
        return op

    def barrier(self, eng, deps, tag="barrier"):
        return self.emit(eng, None, deps=deps, tag=tag)

    def finalize_and_emit(self):
        nc = self.nc
        for e in ENG_LIST:
            self.csem[e] = self._es.enter_context(nc.semaphore(f"cs_{e}"))
            self.dsem[e] = self._es.enter_context(nc.semaphore(f"ds_{e}"))
        eseq = {e: 0 for e in ENG_LIST}
        for op in self.ops:
            eseq[op.eng] += 1
            op.eseq = eseq[op.eng]
        global RAW_WIN
        for op in self.ops:
            for d in op.deps:
                near_raw = (op.raw_same is not None and d in op.raw_same
                            and op.eseq - d.eseq <= RAW_WIN)
                if not d.dma and (d.eng != op.eng or op.selfsync or near_raw):
                    if d.fn is None:
                        raise RuntimeError(f"cross-engine dep on barrier {d.tag}")
                    d.need_inc = True
        cnt = {e: 0 for e in ENG_LIST}
        dcnt = {e: 0 for e in ENG_LIST}
        for op in self.ops:
            if op.dma:
                dcnt[op.eng] += 1
                op.dma_idx = dcnt[op.eng]
            elif op.need_inc:
                cnt[op.eng] += 1
            op.inc_val = cnt[op.eng]
        wm = {e: {} for e in ENG_LIST}
        for op in self.ops:
            mywm = wm[op.eng]
            wl = {}
            for d in op.deps:
                if d.dma:
                    sem, v = self.dsem[d.eng], 16 * d.dma_idx
                elif d.eng == op.eng:
                    near_raw = (op.raw_same is not None and d in op.raw_same
                                and op.eseq - d.eseq <= RAW_WIN)
                    if not (op.selfsync or near_raw):
                        continue
                    sem, v = self.csem[d.eng], d.inc_val
                else:
                    sem, v = self.csem[d.eng], d.inc_val
                key = id(sem)
                if mywm.get(key, 0) >= v:
                    continue
                if key not in wl or wl[key][1] < v:
                    wl[key] = (sem, v)
            op.waits = list(wl.values())
            for key, (sem, v) in wl.items():
                mywm[key] = v
        per = {e: [] for e in ENG_LIST}
        for op in self.ops:
            per[op.eng].append(op)
        blk = self._es.enter_context(nc.Block())
        regmap = {"sp": blk.sync, "pool": blk.gpsimd, "pe": blk.tensor,
                  "dve": blk.vector, "act": blk.scalar}

        def mk(e):
            ops = per[e]

            def runner(engine):
                for op in ops:
                    for sem, v in op.waits:
                        engine.wait_ge(sem, v)
                    if op.fn is None:
                        continue
                    ins = op.fn(engine)
                    if op.dma:
                        ins.then_inc(self.dsem[op.eng], 16)
                    elif op.need_inc:
                        ins.then_inc(self.csem[op.eng], 1)
            return runner
        for e in ENG_LIST:
            regmap[e](mk(e))
        self._es.close()


class Arena:

    def __init__(self, p, name, cols, dtype):
        self.sb = p.sbuf(name, [128, cols], dtype)
        self.cols = cols
        self.name = name
        self.cursor = 0
        self.carry = []
        self.live = []

    def reset(self):
        carry = []
        for r in self.live:
            if r.last_writer is not None:
                carry.append(r.last_writer)
            carry.extend(r.readers)
        self.carry = carry
        self.live = []
        self.cursor = 0

    def take(self, ncols, name):
        off = self.cursor
        self.cursor += ncols
        if self.cursor > self.cols:
            raise RuntimeError(f"arena {self.name} overflow at {name}: "
                               f"{self.cursor} > {self.cols}")
        r = Res(name)
        r.readers = list(self.carry)
        self.live.append(r)
        return off, r

    def sl(self, off, ncols):
        return self.sb[:, off:off + ncols]

    def ap(self, off, dims):
        return bass.AP(self.sb, off, [[self.cols, 128]] + dims)



class Registry:
    def __init__(self, p):
        self.p = p
        self.host = {}
        self.handles = {}
        self._const_cache = {}

    def dram_in(self, name, shape, fn):
        if name in self.handles:
            return self.handles[name]
        t = self.p.nc.dram_tensor(name, list(shape), F32, kind="ExternalInput")
        self.host[name] = fn
        self.handles[name] = t
        return t

    def const_sb(self, name, arr, dtype):
        key = (name, dtype)
        if key in self._const_cache:
            return self._const_cache[key]
        arr = np.asarray(arr, dtype=np.float32)
        if arr.ndim == 1:
            arr = arr[None, :]
        t = self.dram_in("c_" + name, arr.shape, arr)
        sfx = "_bf" if dtype == BF16 else "_f"
        sb = self.p.sbuf("sbc_" + name + sfx, list(arr.shape), dtype)
        res = Res("c_" + name)
        self.p.emit("pool",
                    lambda e, sb=sb, t=t: e.dma_start(out=sb[:], in_=t.ap()),
                    w=[res], dma=True, tag="const_" + name)
        out = (sb, res)
        self._const_cache[key] = out
        return out


class Ctx:
    def __init__(self, p, reg):
        self.p = p
        self.reg = reg
        self.xT = p.sbuf("xT", [128, 8 * T], F32)
        self.xT_r = [Res(f"xT{c}") for c in range(8)]
        self.xn = p.sbuf("xn", [128, 8 * T], BF16)
        self.xn_r = [Res(f"xn{c}") for c in range(8)]
        self.heads = p.sbuf("heads", [128, 8 * T], BF16)
        self.heads_r = [Res(f"hd{c}") for c in range(8)]
        self.pb = [p.psum(f"pb{i}", [128, 512], F32) for i in range(8)]
        self.pb_r = [Res(f"pb{i}") for i in range(8)]
        self.af = Arena(p, "arenaF", 5824, F32)
        self.am = Arena(p, "arenaM", 4608, BF16)
        self.ab = Arena(p, "arenaB", 22224, BF16)
        r = reg.const_sb
        self.id_f, self.id_f_r = r("ident", np.eye(128), F32)
        self.id_b, self.id_b_r = r("ident", np.eye(128), BF16)
        self.ones_col_f, self.ones_col_f_r = r("ones_col", np.ones((128, 1)), F32)
        self.ones_col_b, self.ones_col_b_r = r("ones_col", np.ones((128, 1)), BF16)
        self.ones_row_f, self.ones_row_f_r = r("ones_row", np.ones((1, 128)), F32)
        self.ones_row_b, self.ones_row_b_r = r("ones_row", np.ones((1, 128)), BF16)
        self.ones_512_b, self.ones_512_b_r = r("ones_512", np.ones((1, 512)), BF16)
        self.ones_tile_b, self.ones_tile_b_r = r("ones_tile", np.ones((128, 128)), BF16)
        a = np.arange(128)
        m_blta = np.where(a[None, :] < a[:, None], NEG, 0.0)
        self.mk_blta, self.mk_blta_r = r("mask_blta_T", m_blta.T, BF16)
        self.mk_bgta, self.mk_bgta_r = r("mask_bgta_T", m_blta, BF16)
        blk = (a[:, None] // 64) != (a[None, :] // 64)
        self.mk_l1, self.mk_l1_r = r("mask_l1_T", np.where(blk, NEG, 0.0), BF16)



def emit_load_transpose_in(ctx, x_ap):
    p = ctx.p
    o_tok, r_tok0 = ctx.af.take(1024, "xtok0")
    o_tok1, r_tok1 = ctx.af.take(1024, "xtok1")
    offs = [o_tok, o_tok1]
    rs = [r_tok0, r_tok1]
    for tt in range(16):
        bi = tt % 2
        sl = ctx.af.sl(offs[bi], 1024)
        p.emit("sp", lambda e, sl=sl, tt=tt: e.dma_start(
            out=sl, in_=x_ap[tt * 128:(tt + 1) * 128, :]),
            w=[rs[bi]], dma=True, tag=f"ldx{tt}")
        for g in range(2):
            pb, pr = ctx.pb[g], ctx.pb_r[g]
            for j in range(4):
                c = g * 4 + j
                p.emit("pe", lambda e, sl=sl, c=c, j=j, pb=pb: e.transpose(
                    pb[:, j * 128:(j + 1) * 128],
                    sl[:, c * 128:(c + 1) * 128], ctx.id_f[:]),
                    r=[rs[bi], ctx.id_f_r], w=[pr], tag=f"ti{tt}_{c}")

            def ev(e, g=g, tt=tt, pb=pb):
                dst = bass.AP(ctx.xT, (g * 4) * T + tt * 128,
                              [[8 * T, 128], [T, 4], [1, 128]])
                return e.tensor_copy(dst, pb[:])
            p.emit("dve", ev, r=[pr],
                   w=[ctx.xT_r[g * 4 + j] for j in range(4)], tag=f"ei{tt}_{g}")


def emit_transpose_out_store(ctx, out_ap):
    p = ctx.p
    ctx.af.reset()
    o0, r0 = ctx.af.take(1024, "otok0")
    o1, r1 = ctx.af.take(1024, "otok1")
    offs, rs = [o0, o1], [r0, r1]
    last = None
    for tt in range(16):
        bi = tt % 2
        sl = ctx.af.sl(offs[bi], 1024)
        for g in range(2):
            pb, pr = ctx.pb[g], ctx.pb_r[g]
            for j in range(4):
                c = g * 4 + j
                p.emit("pe", lambda e, c=c, j=j, tt=tt, pb=pb: e.transpose(
                    pb[:, j * 128:(j + 1) * 128],
                    ctx.xT[:, c * T + tt * 128: c * T + (tt + 1) * 128],
                    ctx.id_f[:]),
                    r=[ctx.xT_r[c], ctx.id_f_r], w=[pr], tag=f"to{tt}_{c}")
            p.emit("dve", lambda e, g=g, sl=sl, pb=pb: e.tensor_copy(
                sl[:, g * 512:(g + 1) * 512], pb[:]),
                r=[pr], w=[rs[bi]], tag=f"eo{tt}_{g}")
        last = p.emit("sp", lambda e, sl=sl, tt=tt: e.dma_start(
            out=out_ap[tt * 128:(tt + 1) * 128, :], in_=sl),
            r=[rs[bi]], dma=True, tag=f"stx{tt}")
    p.barrier("sp", deps=[last], tag="final_wait")


def emit_ln(ctx, tag):
    p = ctx.p
    ctx.af.reset()
    ctx.ab.reset()
    ctx.am.reset()
    o_sq, r_sq = ctx.ab.take(T, "sq")
    o_m, r_m = ctx.ab.take(T, "mrep")
    o_r, r_r = ctx.ab.take(T, "rrep")
    o_v, r_v = ctx.af.take(T, "var")
    o_rw, r_rows = ctx.af.take(1024, "lnrows")
    rows = ctx.af.sb[0:1, o_rw:o_rw + 1024]
    sq = ctx.ab.sl(o_sq, T)
    mrep = ctx.ab.sl(o_m, T)
    rrep = ctx.ab.sl(o_r, T)
    var = ctx.af.sl(o_v, T)
    for tb in range(4):
        pbx, pbq = ctx.pb[0], ctx.pb[1]
        for c in range(8):
            p.emit("act", lambda e, c=c, tb=tb: e.activation(
                sq[:, tb * 512:(tb + 1) * 512],
                ctx.xT[:, c * T + tb * 512: c * T + (tb + 1) * 512], AF.Square),
                r=[ctx.xT_r[c]], w=[r_sq], tag=f"{tag}sq{tb}_{c}")
            p.emit("pe", lambda e, c=c, tb=tb, pbx=pbx: e.matmul(
                pbx[0:1, :], ctx.ones_col_f[:],
                ctx.xT[:, c * T + tb * 512: c * T + (tb + 1) * 512],
                start=(c == 0), stop=(c == 7)),
                r=[ctx.xT_r[c], ctx.ones_col_f_r], w=[ctx.pb_r[0]],
                tag=f"{tag}sx{tb}_{c}")
            p.emit("pe", lambda e, c=c, tb=tb, pbq=pbq: e.matmul(
                pbq[0:1, :], ctx.ones_col_b[:],
                sq[:, tb * 512:(tb + 1) * 512],
                start=(c == 0), stop=(c == 7)),
                r=[r_sq, ctx.ones_col_b_r], w=[ctx.pb_r[1]],
                tag=f"{tag}sq2{tb}_{c}")
        p.emit("dve", lambda e, pbx=pbx: e.tensor_copy(
            rows[0:1, 0:512], pbx[0:1, :]),
            r=[ctx.pb_r[0]], w=[r_rows], tag=f"{tag}es{tb}")
        p.emit("dve", lambda e, pbq=pbq: e.tensor_copy(
            rows[0:1, 512:1024], pbq[0:1, :]),
            r=[ctx.pb_r[1]], w=[r_rows], tag=f"{tag}eq{tb}")
        pm, pr2 = ctx.pb[2], ctx.pb[3]
        p.emit("pe", lambda e, pm=pm: e.matmul(
            pm[:], ctx.ones_row_f[:], rows[0:1, 0:512],
            start=True, stop=True),
            r=[r_rows, ctx.ones_row_f_r], w=[ctx.pb_r[2]], tag=f"{tag}rm{tb}")
        p.emit("pe", lambda e, pr2=pr2: e.matmul(
            pr2[:], ctx.ones_row_f[:], rows[0:1, 512:1024],
            start=True, stop=True),
            r=[r_rows, ctx.ones_row_f_r], w=[ctx.pb_r[3]], tag=f"{tag}rr{tb}")
        p.emit("dve", lambda e, tb=tb, pm=pm: e.tensor_scalar_mul(
            mrep[:, tb * 512:(tb + 1) * 512], pm[:], 1.0 / D),
            r=[ctx.pb_r[2]], w=[r_m], tag=f"{tag}mn{tb}")
        p.emit("dve", lambda e, tb=tb: e.tensor_mul(
            var[:, tb * 512:(tb + 1) * 512],
            mrep[:, tb * 512:(tb + 1) * 512],
            mrep[:, tb * 512:(tb + 1) * 512]),
            r=[r_m], w=[r_v], tag=f"{tag}m2{tb}")
        p.emit("dve", lambda e, tb=tb, pr2=pr2: e.scalar_tensor_tensor(
            var[:, tb * 512:(tb + 1) * 512], pr2[:], 1.0 / D,
            var[:, tb * 512:(tb + 1) * 512], OP.mult, OP.subtract),
            r=[ctx.pb_r[3], r_v], w=[r_v], tag=f"{tag}vr{tb}")
        p.emit("dve", lambda e, tb=tb: e.tensor_scalar_add(
            var[:, tb * 512:(tb + 1) * 512],
            var[:, tb * 512:(tb + 1) * 512], 1e-5),
            r=[r_v], w=[r_v], tag=f"{tag}ve{tb}")
        p.emit("act", lambda e, tb=tb: e.activation(
            var[:, tb * 512:(tb + 1) * 512],
            var[:, tb * 512:(tb + 1) * 512], AF.Ln),
            r=[r_v], w=[r_v], tag=f"{tag}ln{tb}")
        p.emit("act", lambda e, tb=tb: e.activation(
            rrep[:, tb * 512:(tb + 1) * 512],
            var[:, tb * 512:(tb + 1) * 512], AF.Exp, scale=-0.5),
            r=[r_v], w=[r_r], tag=f"{tag}ex{tb}")
    for c in range(8):
        p.emit("dve", lambda e, c=c: e.tensor_sub(
            sq[:], ctx.xT[:, c * T:(c + 1) * T], mrep[:]),
            r=[ctx.xT_r[c], r_m], w=[r_sq], tag=f"{tag}xm{c}")
        p.emit("dve", lambda e, c=c: e.tensor_mul(
            ctx.xn[:, c * T:(c + 1) * T], sq[:], rrep[:]),
            r=[r_sq, r_r], w=[ctx.xn_r[c]], tag=f"{tag}xr{c}")


def load_w_dc_slice(ctx, tgt_off, tgt_r, wt, col0, ncols, arena, tag):
    p = ctx.p
    if isinstance(wt, tuple):
        wt, base = wt
        col0 = col0 + base
    nin = int(wt.shape[0])
    wcols = int(wt.shape[1])
    ndc = nin // 128

    def fn(e):
        dst = bass.AP(arena.sb, tgt_off, [[arena.cols, 128], [ncols, ndc], [1, ncols]])
        src = bass.AP(wt, col0, [[wcols, 128], [wcols * 128, ndc],
                                 [1, ncols]])
        return e.dma_start(out=dst, in_=src)
    return p.emit("pool", fn, w=[tgt_r], dma=True, tag=tag)


def rope_tables():
    inv = 1.0 / (10000.0 ** (np.arange(0, DH, 2) / DH))
    pos = np.arange(T)
    ang = pos[None, :] * inv[:, None]
    cos64 = np.concatenate([np.cos(ang), np.cos(ang)], 0)
    sinS = np.concatenate([-np.sin(ang), np.sin(ang)], 0)
    return np.concatenate([cos64, cos64], 0), np.concatenate([sinS, sinS], 0)


def rot_perm_cols(w_t):
    idx = np.arange(w_t.shape[1])
    src = (idx // DH) * DH + ((idx % DH) + 32) % DH
    return np.ascontiguousarray(w_t[:, src])


def attn_kt_plan(kind, g):
    out = []
    if kind == "full":
        for kt in range(16):
            out.append((kt, 0, 512, []))
    elif kind == "causal":
        for kt in range(4 * g + 4):
            c0 = max(0, (kt - 4 * g) * 128)
            masks = [("blta", (kt - 4 * g) * 128)] if kt >= 4 * g else []
            out.append((kt, c0, 512, masks))
    else:
        for kt in range(max(0, 4 * g - 2), min(16, 4 * g + 6)):
            qlo = max(4 * g, kt - 2)
            qhi = min(4 * g + 3, kt + 2)
            c0 = (qlo - 4 * g) * 128
            c1 = (qhi - 4 * g + 1) * 128
            masks = []
            if 4 * g <= kt - 2 <= 4 * g + 3:
                masks.append(("blta", (kt - 2 - 4 * g) * 128))
            if 4 * g <= kt + 2 <= 4 * g + 3:
                masks.append(("bgta", (kt + 2 - 4 * g) * 128))
            out.append((kt, c0, c1, masks))
    return out


def emit_attn_common(ctx, tag, cfg):
    p = ctx.p
    s = cfg["scale"]
    SH = 30.0
    ctx.ab.reset()
    ctx.af.reset()
    o_vt, _ = ctx.ab.take(16 * 8 * 65, "v_tok")
    v_tok_r = [[Res(f"{tag}vt{half}_{kt}") for kt in range(16)] for half in range(2)]
    for half in range(2):
        for kt in range(16):
            v_tok_r[half][kt].readers = list(ctx.ab.carry)
            ctx.ab.live.append(v_tok_r[half][kt])
    aug = []
    for nm in ("k65a", "k65b", "q65a", "q65b"):
        aug.append(ctx.ab.take(T, nm))
    (o_ka, r_ka), (o_kb, r_kb), (o_qa, r_qa), (o_qb, r_qb) = aug
    o_e, _ = ctx.ab.take(1024, "expb")
    expb_r = [Res(f"{tag}ex{i}") for i in range(2)]
    for r_ in expb_r:
        r_.readers = list(ctx.ab.carry)
        ctx.ab.live.append(r_)
    o_w, r_w = ctx.ab.take(2048, "wslice")
    o_wv, r_wv = ctx.ab.take(2048, "wv")
    o_sqs, r_sqs = ctx.ab.take(512, "sqs")
    o_rec, r_rec = ctx.af.take(512, "recb")
    o_rt, r_rt = ctx.af.take(512, "ropet")
    o_rr, r_rr = ctx.af.take(640, "rrow")
    o_nqa, r_nqa = ctx.af.take(T, "nqrowA")
    o_nqb, r_nqb = ctx.af.take(T, "nqrowB")
    recb = ctx.af.sl(o_rec, 512)
    ropet = ctx.af.sl(o_rt, 512)
    rrow = ctx.af.sl(o_rr, 640)
    if cfg["rope"]:
        cos128, sin128 = rope_tables()
        tcos = ctx.reg.dram_in("c_rope_cos", [128, T], cos128)
        tsin = ctx.reg.dram_in("c_rope_sin", [128, T], sin128)
        o_cos, cosb_r = ctx.am.take(T, "rope_cos")
        o_sin, sinb_r = ctx.am.take(T, "rope_sin")
        cosb = ctx.am.sb[:, o_cos:o_cos + T]
        sinb = ctx.am.sb[:, o_sin:o_sin + T]
        p.emit("pool", lambda e: e.dma_start(out=cosb, in_=tcos.ap()),
               w=[cosb_r], dma=True, tag=f"{tag}ldcos")
        p.emit("pool", lambda e: e.dma_start(out=sinb, in_=tsin.ap()),
               w=[sinb_r], dma=True, tag=f"{tag}ldsin")

    VT = 8 * 65

    for o_k, r_k in ((o_ka, r_ka), (o_kb, r_kb)):
        p.emit("dve", lambda e, o_k=o_k: e.memset(
            ctx.ab.sb[64:65, o_k:o_k + T], 1.0),
            w=[r_k], tag=f"{tag}kones")

    def emit_v_quarter(quarter):
        load_w_dc_slice(ctx, o_wv, r_wv, cfg["wv_t"], quarter * 256, 256,
                        ctx.ab, f"{tag}ldv{quarter}")
        half = quarter // 2
        for kt in range(16):
            pb, pbr = ctx.pb[kt % 2], ctx.pb_r[kt % 2]
            for dc in range(8):
                p.emit("pe", lambda e, kt=kt, dc=dc, pb=pb: e.matmul(
                    pb[:, 0:256],
                    ctx.xn[:, dc * T + kt * 128: dc * T + (kt + 1) * 128],
                    ctx.ab.sl(o_wv + dc * 256, 256),
                    start=(dc == 0), stop=(dc == 7 and cfg.get("bv") is None)),
                    r=[ctx.xn_r[dc], r_wv], w=[pbr], tag=f"{tag}v{quarter}_{kt}_{dc}")
            if cfg.get("bv") is not None:
                brow, brr, boff = cfg["bv"]
                p.emit("pe", lambda e, pb=pb, quarter=quarter, brow=brow,
                       boff=boff: e.matmul(
                    pb[:, 0:256], ctx.ones_row_b[:],
                    brow[0:1, boff + quarter * 256: boff + (quarter + 1) * 256],
                    start=False, stop=True),
                    r=[brr, ctx.ones_row_b_r], w=[pbr], tag=f"{tag}vb{quarter}_{kt}")

            def evv(e, kt=kt, quarter=quarter, pb=pb):
                dst = bass.AP(ctx.ab.sb, o_vt + kt * VT + (quarter % 2) * 4 * 65,
                              [[ctx.ab.cols, 128], [65, 4], [1, 64]])
                return e.tensor_copy(dst, pb[:, 0:256])
            p.emit("dve", evv, r=[pbr], w=[v_tok_r[half][kt]],
                   tag=f"{tag}ve{quarter}_{kt}")
        if quarter % 2 == 1:
            def setones(e, half=half):
                dst = bass.AP(ctx.ab.sb, o_vt + 64,
                              [[ctx.ab.cols, 128], [VT, 16], [65, 8]])
                return e.memset(dst, 1.0)
            p.emit("dve", setones, r=[], w=v_tok_r[half], tag=f"{tag}vo{half}")

    def proj(c, what, dsts, bias):
        wi_main, wi_rot = 0, 1
        names = [(cfg["w" + what + "_t"], wi_main)] + (
            [(cfg["w" + what + "_rot_t"], wi_rot)] if cfg["rope"] else [])
        for wt, wi in names:
            load_w_dc_slice(ctx, o_w + wi * 1024, r_w, wt, c * 128, 128,
                            ctx.ab, f"{tag}lw{what}{c}_{wi}")
        (o_da, r_da), (o_db, r_db) = dsts
        for tb in range(4):
            pbq, pbrot = ctx.pb[4], ctx.pb[5]
            for dc in range(8):
                p.emit("pe", lambda e, tb=tb, dc=dc, pbq=pbq: e.matmul(
                    pbq[:],
                    ctx.ab.sl(o_w + 0 * 1024 + dc * 128, 128),
                    ctx.xn[:, dc * T + tb * 512: dc * T + (tb + 1) * 512],
                    start=(dc == 0), stop=(dc == 7 and bias is None)),
                    r=[r_w, ctx.xn_r[dc]], w=[ctx.pb_r[4]],
                    tag=f"{tag}{what}mm{c}_{tb}_{dc}")
            if bias is not None:
                brow, brr, boff = bias
                p.emit("pe", lambda e, pbq=pbq, brow=brow, boff=boff, c=c: e.matmul(
                    pbq[:], brow[0:1, boff + c * 128: boff + (c + 1) * 128],
                    ctx.ones_512_b[:], start=False, stop=True),
                    r=[brr, ctx.ones_512_b_r], w=[ctx.pb_r[4]],
                    tag=f"{tag}{what}b{c}_{tb}")
            p.emit("act", lambda e, pbq=pbq: e.activation(
                ctx.ab.sl(o_sqs, 512), pbq[:], AF.Square),
                r=[ctx.pb_r[4]], w=[r_sqs], tag=f"{tag}{what}sq{c}_{tb}")
            for hh in range(2):
                pbn = ctx.pb[6 + hh]
                p.emit("pe", lambda e, hh=hh, pbn=pbn: e.matmul(
                    pbn[0:1, :], ctx.ones_col_b[hh * 64:(hh + 1) * 64, :],
                    ctx.ab.sb[hh * 64:(hh + 1) * 64, o_sqs:o_sqs + 512],
                    start=True, stop=True),
                    r=[r_sqs, ctx.ones_col_b_r], w=[ctx.pb_r[6 + hh]],
                    tag=f"{tag}{what}nm{c}_{tb}_{hh}")
                if what == "q":
                    o_nq = (o_nqa, o_nqb)[hh]
                    r_nq = (r_nqa, r_nqb)[hh]
                    p.emit("dve", lambda e, tb=tb, hh=hh, pbn=pbn, o_nq=o_nq:
                           e.tensor_copy(
                        ctx.af.sb[0:1, o_nq + tb * 512: o_nq + (tb + 1) * 512],
                        pbn[0:1, :]),
                        r=[ctx.pb_r[6 + hh]], w=[r_nq],
                        tag=f"{tag}nq{c}_{tb}_{hh}")
                else:
                    p.emit("dve", lambda e, tb=tb, hh=hh, pbn=pbn: e.reduce_max(
                        rrow[0:1, 32 + hh * 4 + tb: 33 + hh * 4 + tb],
                        pbn[0:1, :], AX.X),
                        r=[ctx.pb_r[6 + hh]], w=[r_rr],
                        tag=f"{tag}nk{c}_{tb}_{hh}")
            if cfg["rope"]:
                for dc in range(8):
                    p.emit("pe", lambda e, tb=tb, dc=dc, pbrot=pbrot: e.matmul(
                        pbrot[:],
                        ctx.ab.sl(o_w + 1 * 1024 + dc * 128, 128),
                        ctx.xn[:, dc * T + tb * 512: dc * T + (tb + 1) * 512],
                        start=(dc == 0), stop=(dc == 7)),
                        r=[r_w, ctx.xn_r[dc]], w=[ctx.pb_r[5]],
                        tag=f"{tag}{what}rm{c}_{tb}_{dc}")
                p.emit("dve", lambda e, tb=tb, pbq=pbq: e.tensor_mul(
                    ropet[:], pbq[:], cosb[:, tb * 512:(tb + 1) * 512]),
                    r=[ctx.pb_r[4], cosb_r], w=[r_rt], tag=f"{tag}{what}rc{c}_{tb}")
                p.emit("dve", lambda e, tb=tb, pbrot=pbrot: e.tensor_mul(
                    recb[:], pbrot[:], sinb[:, tb * 512:(tb + 1) * 512]),
                    r=[ctx.pb_r[5], sinb_r], w=[r_rec], tag=f"{tag}{what}rs{c}_{tb}")
                for hh, (o_d, r_d) in enumerate(((o_da, r_da), (o_db, r_db))):
                    p.emit("dve", lambda e, tb=tb, hh=hh, o_d=o_d: e.tensor_add(
                        ctx.ab.sb[0:64, o_d + tb * 512: o_d + (tb + 1) * 512],
                        ropet[hh * 64:(hh + 1) * 64, :],
                        recb[hh * 64:(hh + 1) * 64, :]),
                        r=[r_rt, r_rec], w=[r_d], tag=f"{tag}{what}ra{c}_{tb}_{hh}")
            else:
                for hh, (o_d, r_d) in enumerate(((o_da, r_da), (o_db, r_db))):
                    p.emit("dve", lambda e, tb=tb, hh=hh, o_d=o_d, pbq=pbq:
                           e.tensor_copy(
                        ctx.ab.sb[0:64, o_d + tb * 512: o_d + (tb + 1) * 512],
                        pbq[hh * 64:(hh + 1) * 64, :]),
                        r=[ctx.pb_r[4]], w=[r_d], tag=f"{tag}{what}ev{c}_{tb}_{hh}")

    for half in range(2):
        emit_v_quarter(2 * half)
        emit_v_quarter(2 * half + 1)
        for c in range(4 * half, 4 * half + 4):
            proj(c, "k", ((o_ka, r_ka), (o_kb, r_kb)), cfg.get("bk"))
            proj(c, "q", ((o_qa, r_qa), (o_qb, r_qb)), cfg.get("bq"))
            for hh in range(2):
                o_q65 = (o_qa, o_qb)[hh]
                r_q65 = (r_qa, r_qb)[hh]
                o_nq = (o_nqa, o_nqb)[hh]
                r_nq = (r_nqa, r_nqb)[hh]
                p.emit("dve", lambda e, hh=hh: e.reduce_max(
                    rrow[0:1, 40 + hh:41 + hh],
                    rrow[0:1, 32 + hh * 4:36 + hh * 4], AX.X),
                    r=[r_rr], w=[r_rr], tag=f"{tag}mk{c}_{hh}")
                p.emit("dve", lambda e, hh=hh, o_nq=o_nq: e.tensor_scalar_mul(
                    ctx.af.sb[0:1, o_nq:o_nq + T],
                    ctx.af.sb[0:1, o_nq:o_nq + T],
                    rrow[0:1, 40 + hh:41 + hh]),
                    r=[r_nq, r_rr], w=[r_nq], tag=f"{tag}bm{c}_{hh}",
                    selfsync=True)
                p.emit("act", lambda e, o_nq=o_nq: e.activation(
                    ctx.af.sb[0:1, o_nq:o_nq + T],
                    ctx.af.sb[0:1, o_nq:o_nq + T], AF.Sqrt),
                    r=[r_nq], w=[r_nq], tag=f"{tag}bq{c}_{hh}")
                p.emit("dve", lambda e, o_nq=o_nq, o_q65=o_q65: e.tensor_scalar(
                    ctx.ab.sb[64:65, o_q65:o_q65 + T],
                    ctx.af.sb[0:1, o_nq:o_nq + T], -1.0, SH / s,
                    OP.mult, OP.add),
                    r=[r_nq], w=[r_q65], tag=f"{tag}bw{c}_{hh}")

            if ATTN_DEBUG is not None and c == 0:
                ATTN_DEBUG(ctx, tag + "p0", {
                    "k65a": (ctx.ab.sb[0:128, o_ka:o_ka + T], [r_ka]),
                    "q65a": (ctx.ab.sb[0:128, o_qa:o_qa + T], [r_qa]),
                    "nqa": (ctx.af.sb[0:1, o_nqa:o_nqa + T], [r_nqa]),
                    "rrow": (rrow[:], [r_rr])})
            for hh in range(2):
                h = 2 * c + hh
                o_k65 = (o_ka, o_kb)[hh]
                r_k65 = (r_ka, r_kb)[hh]
                o_q65 = (o_qa, o_qb)[hh]
                r_q65 = (r_qa, r_qb)[hh]
                hloc = 2 * (c - 4 * half) + hh
                for g in range(4):
                    kts = attn_kt_plan(cfg["kind"], g)
                    av_bank, av_r = ctx.pb[6], ctx.pb_r[6]
                    n_kts = len(kts)

                    def emit_av(i, kt, c0, c1, hloc=hloc, h=h, g=g,
                                n_kts=n_kts, av_bank=av_bank, av_r=av_r,
                                half=half):
                        p.emit("pe", lambda e, i=i, kt=kt, c0=c0, c1=c1: e.matmul(
                            av_bank[0:65, c0:c1],
                            bass.AP(ctx.ab.sb, o_vt + kt * VT + hloc * 65,
                                    [[ctx.ab.cols, 128], [1, 65]]),
                            ctx.ab.sb[:, o_e + (i % 2) * 512 + c0:
                                      o_e + (i % 2) * 512 + c1],
                            start=(i == 0), stop=(i == n_kts - 1)),
                            r=[v_tok_r[half][kt], expb_r[i % 2]], w=[av_r],
                            tag=f"{tag}av{h}_{g}_{kt}")

                    prev = None
                    for i, (kt, c0, c1, masks) in enumerate(kts):
                        sb, sbr = ctx.pb[i % 2], ctx.pb_r[i % 2]
                        p.emit("pe", lambda e, kt=kt, c0=c0, c1=c1, sb=sb, g=g,
                               o_k65=o_k65, o_q65=o_q65,
                               nm=len(masks): e.matmul(
                            sb[:, c0:c1],
                            ctx.ab.sb[0:65, o_k65 + kt * 128: o_k65 + (kt + 1) * 128],
                            ctx.ab.sb[0:65, o_q65 + g * 512 + c0: o_q65 + g * 512 + c1],
                            start=True, stop=(nm == 0)),
                            r=[r_k65, r_q65], w=[sbr], tag=f"{tag}sc{h}_{g}_{kt}")
                        for mi, (mt, mc0) in enumerate(masks):
                            mco = ctx.mk_blta if mt == "blta" else ctx.mk_bgta
                            mcr = ctx.mk_blta_r if mt == "blta" else ctx.mk_bgta_r
                            p.emit("pe", lambda e, sb=sb, mc0=mc0, mco=mco: e.matmul(
                                sb[:, mc0:mc0 + 128], mco[:], ctx.id_b[:],
                                start=False, stop=(mi == len(masks) - 1)),
                                r=[mcr, ctx.id_b_r], w=[sbr],
                                tag=f"{tag}m{h}_{g}_{kt}_{mi}")
                        p.emit("act", lambda e, i=i, c0=c0, c1=c1, sb=sb: e.activation(
                            ctx.ab.sb[:, o_e + (i % 2) * 512 + c0:
                                      o_e + (i % 2) * 512 + c1],
                            sb[:, c0:c1], AF.Exp, scale=s),
                            r=[sbr], w=[expb_r[i % 2]],
                            tag=f"{tag}e{h}_{g}_{kt}")
                        if prev is not None:
                            emit_av(*prev)
                        prev = (i, kt, c0, c1)
                    emit_av(*prev)
                    p.emit("dve", lambda e: e.tensor_copy(
                        rrow[0:1, 128:640], av_bank[64:65, :]),
                        r=[av_r], w=[r_rr], tag=f"{tag}s{h}_{g}")
                    if ATTN_DEBUG is not None and h == 0 and g == 0:
                        ATTN_DEBUG(ctx, tag + "se", {
                            "sumexp": (rrow[0:1, 128:640], [r_rr])})
                    p.emit("act", lambda e: e.activation(
                        rrow[0:1, 128:640], rrow[0:1, 128:640], AF.Ln),
                        r=[r_rr], w=[r_rr], tag=f"{tag}sl{h}_{g}")
                    p.emit("act", lambda e: e.activation(
                        rrow[0:1, 128:640], rrow[0:1, 128:640], AF.Exp,
                        scale=-1.0),
                        r=[r_rr], w=[r_rr], tag=f"{tag}sx{h}_{g}")
                    if ATTN_DEBUG is not None and h == 0 and g == 0:
                        ATTN_DEBUG(ctx, tag + "sx", {
                            "recrow": (rrow[0:1, 128:640], [r_rr])})
                    pb7b = ctx.pb[7]
                    p.emit("pe", lambda e, pb7b=pb7b: e.matmul(
                        pb7b[:, :], ctx.ones_row_f[:],
                        rrow[0:1, 128:640], start=True, stop=True),
                        r=[r_rr, ctx.ones_row_f_r], w=[ctx.pb_r[7]],
                        tag=f"{tag}sr{h}_{g}")
                    p.emit("dve", lambda e, pb7b=pb7b: e.tensor_copy(
                        recb[0:64, :], pb7b[0:64, :]),
                        r=[ctx.pb_r[7]], w=[r_rec], tag=f"{tag}se{h}_{g}")

                    def hev(e, hb=hh * 64, g=g, c=c):
                        return e.tensor_mul(
                            ctx.heads[hb:hb + 64,
                                      c * T + g * 512: c * T + (g + 1) * 512],
                            av_bank[0:64, :], recb[0:64, :])
                    p.emit("dve", hev, r=[av_r, r_rec], w=[ctx.heads_r[c]],
                           tag=f"{tag}h{h}_{g}")
                    if ATTN_DEBUG is not None and h == 0 and g == 0:
                        ATTN_DEBUG(ctx, tag + "rc", {
                            "recb": (recb[0:64, :], [r_rec])})

    ctx.ab.reset()
    o_wo, r_wo = ctx.ab.take(8192, "wob")
    for ohalf in range(2):
        load_w_dc_slice(ctx, o_wo, r_wo, cfg["wo_t"], ohalf * 512, 512,
                        ctx.ab, f"{tag}lwo{ohalf}")
        for oc in range(4):
            occ = ohalf * 4 + oc
            for tb in range(4):
                pb, pbr = ctx.pb[2 + (tb % 2)], ctx.pb_r[2 + (tb % 2)]
                for dc in range(8):
                    p.emit("pe", lambda e, oc=oc, tb=tb, dc=dc, pb=pb: e.matmul(
                        pb[:],
                        ctx.ab.sl(o_wo + dc * 512 + oc * 128, 128),
                        ctx.heads[:, dc * T + tb * 512: dc * T + (tb + 1) * 512],
                        start=(dc == 0),
                        stop=(dc == 7 and cfg.get("bo") is None)),
                        r=[r_wo, ctx.heads_r[dc]], w=[pbr],
                        tag=f"{tag}o{occ}_{tb}_{dc}")
                if cfg.get("bo") is not None:
                    brow, brr, boff = cfg["bo"]
                    p.emit("pe", lambda e, occ=occ, pb=pb, brow=brow, boff=boff: e.matmul(
                        pb[:], brow[0:1, boff + occ * 128: boff + (occ + 1) * 128],
                        ctx.ones_512_b[:], start=False, stop=True),
                        r=[brr, ctx.ones_512_b_r], w=[pbr],
                        tag=f"{tag}ob{occ}_{tb}")
                p.emit("dve", lambda e, occ=occ, tb=tb, pb=pb: e.tensor_add(
                    ctx.xT[:, occ * T + tb * 512: occ * T + (tb + 1) * 512],
                    ctx.xT[:, occ * T + tb * 512: occ * T + (tb + 1) * 512],
                    pb[:]),
                    r=[pbr, ctx.xT_r[occ]], w=[ctx.xT_r[occ]],
                    tag=f"{tag}oe{occ}_{tb}")



def emit_l3(ctx, reg):
    p = ctx.p
    emit_ln(ctx, "l3n_")
    tr = lambda k: (lambda inp, k=k: np.ascontiguousarray(inp[k].T))
    cfg = dict(
        wq_t=reg.dram_in("L3wq_t", [D, D], tr("L3_attn_Wq")),
        wk_t=reg.dram_in("L3wk_t", [D, D], tr("L3_attn_Wk")),
        wv_t=reg.dram_in("L3wv_t", [D, D], tr("L3_attn_Wv")),
        wo_t=reg.dram_in("L3wo_t", [D, D], tr("L3_attn_Wo")),
        wq_rot_t=reg.dram_in("L3wqr_t", [D, D],
                             lambda inp: rot_perm_cols(np.ascontiguousarray(inp["L3_attn_Wq"].T))),
        wk_rot_t=reg.dram_in("L3wkr_t", [D, D],
                             lambda inp: rot_perm_cols(np.ascontiguousarray(inp["L3_attn_Wk"].T))),
        rope=True, kind="causal", scale=1.0 / 8.0)
    emit_attn_common(ctx, "l3a_", cfg)

    w_t = reg.dram_in("L3w_t", [D, D], tr("L3_ffn_weights"))
    sc = reg.dram_in("L3sc", [128, 8],
                     lambda inp: np.ascontiguousarray(
                         inp["L3_ffn_scale"].reshape(8, 128).T))
    bc = reg.dram_in("L3bc", [128, 8],
                     lambda inp: np.ascontiguousarray(
                         inp["L3_ffn_bias"].reshape(8, 128).T))
    ctx.ab.reset()
    ctx.af.reset()
    o_w, r_w = ctx.ab.take(8192, "l3fw")
    o_t, r_t = ctx.af.take(512, "l3ft")
    o_sc, r_sc = ctx.af.take(16, "l3fsc")
    tmp = ctx.af.sl(o_t, 512)
    scb = ctx.af.sl(o_sc, 16)
    p.emit("sp", lambda e: e.dma_start(out=scb[:, 0:8], in_=sc.ap()),
           w=[r_sc], dma=True, tag="l3f_lsc")
    p.emit("sp", lambda e: e.dma_start(out=scb[:, 8:16], in_=bc.ap()),
           w=[r_sc], dma=True, tag="l3f_lbc")
    for ohalf in range(2):
        load_w_dc_slice(ctx, o_w, r_w, w_t, ohalf * 512, 512, ctx.ab,
                        f"l3f_lw{ohalf}")
        p.emit("dve", lambda e: e.tensor_scalar(
            ctx.ab.sl(o_w, 8192), ctx.ab.sl(o_w, 8192), 0.0, 2.0,
            OP.is_ge, OP.mult),
            r=[r_w], w=[r_w], tag=f"l3f_s1{ohalf}")
        p.emit("dve", lambda e: e.tensor_scalar_add(
            ctx.ab.sl(o_w, 8192), ctx.ab.sl(o_w, 8192), -1.0),
            r=[r_w], w=[r_w], tag=f"l3f_s2{ohalf}")
        for oc in range(4):
            occ = ohalf * 4 + oc
            for tb in range(4):
                pb, pbr = ctx.pb[4 + (tb % 2)], ctx.pb_r[4 + (tb % 2)]
                for dc in range(8):
                    p.emit("pe", lambda e, oc=oc, tb=tb, dc=dc, pb=pb: e.matmul(
                        pb[:],
                        ctx.ab.sl(o_w + dc * 512 + oc * 128, 128),
                        ctx.xn[:, dc * T + tb * 512: dc * T + (tb + 1) * 512],
                        start=(dc == 0), stop=(dc == 7)),
                        r=[r_w, ctx.xn_r[dc]], w=[pbr],
                        tag=f"l3f_m{occ}_{tb}_{dc}")
                p.emit("dve", lambda e, occ=occ, pb=pb: e.tensor_scalar(
                    tmp[:], pb[:], scb[:, occ:occ + 1], scb[:, 8 + occ:9 + occ],
                    OP.mult, OP.add),
                    r=[pbr, r_sc], w=[r_t], tag=f"l3f_sb{occ}_{tb}")
                p.emit("dve", lambda e: e.tensor_scalar(
                    tmp[:], tmp[:], 100.0, -100.0, OP.min, OP.max),
                    r=[r_t], w=[r_t], tag=f"l3f_cl{occ}_{tb}")
                p.emit("dve", lambda e, occ=occ, tb=tb: e.tensor_add(
                    ctx.xT[:, occ * T + tb * 512: occ * T + (tb + 1) * 512],
                    ctx.xT[:, occ * T + tb * 512: occ * T + (tb + 1) * 512],
                    tmp[:]),
                    r=[r_t, ctx.xT_r[occ]], w=[ctx.xT_r[occ]],
                    tag=f"l3f_x{occ}_{tb}")


def bias_row(ctx, reg, name, key, n=D):
    t = reg.dram_in(name, [1, n],
                    lambda inp, key=key: np.asarray(inp[key]).reshape(1, -1))
    off, res = ctx.am.take(n, name)
    sb = ctx.am.sb[0:1, off:off + n]
    ctx.p.emit("pool", lambda e: e.dma_start(out=sb, in_=t.ap()),
               w=[res], dma=True, tag="ld_" + name)
    return (sb, res, 0)


def emit_router(ctx, reg, tag, w_t, nE, brow, renorm):
    p = ctx.p
    o_rw, r_rw = ctx.ab.take(8 * nE, "rw")
    o_gt, r_gt = ctx.af.take(16 * nE, "gates_tok")
    o_lg, r_lg = ctx.af.take(T, "logits")
    o_sc, r_sc = ctx.af.take(4 * nE + 8, "rscratch")
    load_w_dc_slice(ctx, o_rw, r_rw, w_t, 0, nE, ctx.ab, f"{tag}ldw")
    lg = ctx.af.sl(o_lg, T)
    for tb in range(4):
        pb, pbr = ctx.pb[6], ctx.pb_r[6]
        for dc in range(8):
            p.emit("pe", lambda e, dc=dc, tb=tb, pb=pb: e.matmul(
                pb[0:nE, 0:512], ctx.ab.sl(o_rw + dc * nE, nE),
                ctx.xn[:, dc * T + tb * 512: dc * T + (tb + 1) * 512],
                start=(dc == 0), stop=(dc == 7 and brow is None)),
                r=[r_rw, ctx.xn_r[dc]], w=[pbr], tag=f"{tag}mm{tb}_{dc}")
        if brow is not None:
            bsb, bres, boff = brow
            p.emit("pe", lambda e, pb=pb, bsb=bsb, boff=boff: e.matmul(
                pb[0:nE, 0:512], bsb[0:1, boff:boff + nE],
                ctx.ones_512_b[:], start=False, stop=True),
                r=[bres, ctx.ones_512_b_r], w=[pbr], tag=f"{tag}b{tb}")
        p.emit("dve", lambda e, tb=tb, pb=pb: e.tensor_copy(
            lg[0:nE, tb * 512:(tb + 1) * 512], pb[0:nE, 0:512]),
            r=[pbr], w=[r_lg], tag=f"{tag}ev{tb}")
    sc = ctx.af.sl(o_sc, 4 * nE + 8)
    for tt in range(16):
        pb, pbr = ctx.pb[7], ctx.pb_r[7]
        p.emit("pe", lambda e, tt=tt, pb=pb: e.transpose(
            pb[:, 0:nE], lg[0:nE, tt * 128:(tt + 1) * 128],
            ctx.id_f[0:nE, 0:nE]),
            r=[r_lg, ctx.id_f_r], w=[pbr], tag=f"{tag}tr{tt}")
        C = 3 * nE
        p.emit("dve", lambda e, pb=pb: e.reduce_max(
            sc[:, C:C + 1], pb[:, 0:nE], AX.X, None, None, True),
            r=[pbr], w=[r_sc], tag=f"{tag}rm{tt}")
        p.emit("act", lambda e, pb=pb: e.activation(
            sc[:, 0:nE], pb[:, 0:nE], AF.Exp, bias=sc[:, C:C + 1],
            accum_out=sc[:, C + 1:C + 2]),
            r=[pbr, r_sc], w=[r_sc], tag=f"{tag}ex{tt}")
        p.emit("dve", lambda e: e.reciprocal(
            sc[:, C + 2:C + 3], sc[:, C + 1:C + 2]),
            r=[r_sc], w=[r_sc], tag=f"{tag}rc{tt}")
        p.emit("dve", lambda e: e.tensor_scalar_mul(
            sc[:, 0:nE], sc[:, 0:nE], sc[:, C + 2:C + 3]),
            r=[r_sc], w=[r_sc], tag=f"{tag}pr{tt}", selfsync=True)
        p.emit("dve", lambda e: e.reduce_max(
            sc[:, C + 3:C + 4], sc[:, 0:nE], AX.X),
            r=[r_sc], w=[r_sc], tag=f"{tag}m1{tt}")
        p.emit("dve", lambda e: e.tensor_scalar(
            sc[:, nE:2 * nE], sc[:, 0:nE], sc[:, C + 3:C + 4], NEG,
            OP.is_equal, OP.mult),
            r=[r_sc], w=[r_sc], tag=f"{tag}mk{tt}", selfsync=True)
        p.emit("dve", lambda e: e.tensor_add(
            sc[:, nE:2 * nE], sc[:, nE:2 * nE], sc[:, 0:nE]),
            r=[r_sc], w=[r_sc], tag=f"{tag}ms{tt}")
        p.emit("dve", lambda e: e.reduce_max(
            sc[:, C + 4:C + 5], sc[:, nE:2 * nE], AX.X),
            r=[r_sc], w=[r_sc], tag=f"{tag}m2{tt}")
        p.emit("dve", lambda e: e.tensor_scalar(
            sc[:, 2 * nE:3 * nE], sc[:, 0:nE], sc[:, C + 4:C + 5], None,
            OP.is_ge),
            r=[r_sc], w=[r_sc], tag=f"{tag}ge{tt}", selfsync=True)
        if renorm:
            p.emit("dve", lambda e: e.tensor_add(
                sc[:, C + 5:C + 6], sc[:, C + 3:C + 4], sc[:, C + 4:C + 5]),
                r=[r_sc], w=[r_sc], tag=f"{tag}s12{tt}")
            p.emit("dve", lambda e: e.reciprocal(
                sc[:, C + 6:C + 7], sc[:, C + 5:C + 6]),
                r=[r_sc], w=[r_sc], tag=f"{tag}rs{tt}")
            p.emit("dve", lambda e: e.tensor_scalar_mul(
                sc[:, 0:nE], sc[:, 0:nE], sc[:, C + 6:C + 7]),
                r=[r_sc], w=[r_sc], tag=f"{tag}rn{tt}", selfsync=True)
        p.emit("dve", lambda e, tt=tt: e.tensor_mul(
            ctx.af.sl(o_gt + tt * nE, nE), sc[:, 2 * nE:3 * nE], sc[:, 0:nE]),
            r=[r_sc], w=[r_gt], tag=f"{tag}g{tt}")
    if ATTN_DEBUG is not None:
        ATTN_DEBUG(ctx, tag + "gt", {"gates": (ctx.af.sl(o_gt, 16 * nE), [r_gt]),
                                     "sc": (sc[:], [r_sc]),
                                     "lg": (lg[:], [r_lg])})
    return o_gt, r_gt


def emit_g_block(ctx, tag, o_gt, r_gt, nE, tb, o_g, r_g, o_bc, r_bc):
    p = ctx.p
    for half in range(2):
        tt = tb * 2 + half
        pb, pbr = ctx.pb[6], ctx.pb_r[6]
        for e in range(nE):
            p.emit("dve", lambda e_, e=e, tt=tt: e_.tensor_scalar_mul(
                ctx.ab.sl(o_bc, 128), ctx.ones_tile_b[:],
                bass.AP(ctx.af.sb, o_gt + tt * nE + e,
                        [[ctx.af.cols, 128], [1, 1]])),
                r=[r_gt, ctx.ones_tile_b_r], w=[r_bc],
                tag=f"{tag}bc{tb}_{half}_{e}", selfsync=True)
            p.emit("pe", lambda e_, pb=pb: e_.matmul(
                pb[:, 0:128], ctx.ab.sl(o_bc, 128), ctx.id_b[:],
                start=True, stop=True),
                r=[r_bc, ctx.id_b_r], w=[pbr], tag=f"{tag}gr{tb}_{half}_{e}")
            p.emit("dve", lambda e_, e=e, half=half, pb=pb: e_.tensor_copy(
                ctx.ab.sl(o_g + e * 256 + half * 128, 128), pb[:, 0:128]),
                r=[pbr], w=[r_g], tag=f"{tag}ge{tb}_{half}_{e}")


def emit_l0_ffn(ctx, reg):
    p = ctx.p
    w1 = reg.dram_in("L0w1", [8 * D, 4096], lambda inp: np.ascontiguousarray(
        inp["L0_ffn_w1"].reshape(8 * D, 4096)))
    w2 = reg.dram_in("L0w2", [8 * 4096, D], lambda inp: np.ascontiguousarray(
        inp["L0_ffn_w2"].reshape(8 * 4096, D)))
    rw_t = reg.dram_in("L0rw", [D, 9], lambda inp: inp["L0_ffn_router_w"])
    rb = bias_row(ctx, reg, "L0rb", "L0_ffn_router_b", 9)
    ctx.ab.reset()
    ctx.af.reset()
    nE = 9
    o_gt, r_gt = emit_router(ctx, reg, "l0r_", rw_t, nE, rb, False)
    o_h, r_h = ctx.ab.take(32 * 256, "h")
    o_w1, r_w1 = ctx.ab.take(8 * 512, "w1blk")
    o_w2, r_w2 = ctx.ab.take(4 * 1024, "w2blk")
    o_g, r_g = ctx.ab.take(nE * 256, "G")
    o_bc, r_bc = ctx.ab.take(128, "bcast")
    o_t, r_t = ctx.af.take(256, "idg")
    for tb in range(8):
        emit_g_block(ctx, "l0g_", o_gt, r_gt, nE, tb, o_g, r_g, o_bc, r_bc)
        for e in range(8):
            for fb in range(8):
                def ldw1(eng, e=e, fb=fb):
                    dst = bass.AP(ctx.ab.sb, o_w1,
                                  [[ctx.ab.cols, 128], [512, 8], [1, 512]])
                    src = bass.AP(w1, e * D * 4096 + fb * 512,
                                  [[4096, 128], [4096 * 128, 8], [1, 512]])
                    return eng.dma_start(out=dst, in_=src)
                p.emit("pool", ldw1, w=[r_w1], dma=True, tag=f"l0f_lw1_{tb}_{e}_{fb}")
                for fc in range(4):
                    pb, pbr = ctx.pb[4 + fc % 2], ctx.pb_r[4 + fc % 2]
                    for dc in range(8):
                        p.emit("pe", lambda eng, dc=dc, fc=fc, tb=tb, pb=pb: eng.matmul(
                            pb[:, 0:256],
                            ctx.ab.sl(o_w1 + dc * 512 + fc * 128, 128),
                            ctx.xn[:, dc * T + tb * 256: dc * T + (tb + 1) * 256],
                            start=(dc == 0), stop=(dc == 7)),
                            r=[r_w1, ctx.xn_r[dc]], w=[pbr],
                            tag=f"l0f_h{tb}_{e}_{fb}_{fc}_{dc}")
                    fci = fb * 4 + fc
                    p.emit("act", lambda eng, fci=fci, pb=pb: eng.activation(
                        ctx.ab.sl(o_h + fci * 256, 256), pb[:, 0:256], AF.Gelu),
                        r=[pbr], w=[r_h], tag=f"l0f_g{tb}_{e}_{fci}")
                    p.emit("dve", lambda eng, fci=fci, e=e: eng.tensor_mul(
                        ctx.ab.sl(o_h + fci * 256, 256),
                        ctx.ab.sl(o_h + fci * 256, 256),
                        ctx.ab.sl(o_g + e * 256, 256)),
                        r=[r_h, r_g], w=[r_h], tag=f"l0f_gm{tb}_{e}_{fci}")
            for fb in range(8):
                def ldw2(eng, e=e, fb=fb):
                    dst = bass.AP(ctx.ab.sb, o_w2,
                                  [[ctx.ab.cols, 128], [1024, 4], [1, 1024]])
                    src = bass.AP(w2, (e * 4096 + fb * 512) * D,
                                  [[D, 128], [D * 128, 4], [1, 1024]])
                    return eng.dma_start(out=dst, in_=src)
                p.emit("pool", ldw2, w=[r_w2], dma=True, tag=f"l0f_lw2_{tb}_{e}_{fb}")
                for dc in range(8):
                    ybank, ybr = ctx.pb[dc // 2], ctx.pb_r[dc // 2]
                    ycol = (dc % 2) * 256
                    for fcl in range(4):
                        fci = fb * 4 + fcl
                        st = (e == 0 and fb == 0 and fcl == 0 and dc % 2 == 0)
                        last = (e == 7 and fb == 7 and fcl == 3 and dc % 2 == 1)
                        p.emit("pe", lambda eng, dc=dc, fcl=fcl, fci=fci,
                               ybank=ybank, ycol=ycol, st=st,
                               last=last: eng.matmul(
                            ybank[:, ycol:ycol + 256],
                            ctx.ab.sl(o_w2 + fcl * 1024 + dc * 128, 128),
                            ctx.ab.sl(o_h + fci * 256, 256),
                            start=st, stop=last),
                            r=[r_w2, r_h], w=[ybr],
                            tag=f"l0f_y{tb}_{e}_{fb}_{dc}_{fcl}")
        for dc in range(8):
            ybank, ybr = ctx.pb[dc // 2], ctx.pb_r[dc // 2]
            ycol = (dc % 2) * 256
            p.emit("dve", lambda eng, dc=dc, tb=tb, ybank=ybank, ycol=ycol: eng.tensor_add(
                ctx.xT[:, dc * T + tb * 256: dc * T + (tb + 1) * 256],
                ctx.xT[:, dc * T + tb * 256: dc * T + (tb + 1) * 256],
                ybank[:, ycol:ycol + 256]),
                r=[ybr, ctx.xT_r[dc]], w=[ctx.xT_r[dc]], tag=f"l0f_ye{tb}_{dc}")
        for dc in range(8):
            p.emit("dve", lambda eng, dc=dc, tb=tb: eng.tensor_mul(
                ctx.af.sl(o_t, 256),
                ctx.xn[:, dc * T + tb * 256: dc * T + (tb + 1) * 256],
                ctx.ab.sl(o_g + 8 * 256, 256)),
                r=[ctx.xn_r[dc], r_g], w=[r_t], tag=f"l0f_i1{tb}_{dc}")
            p.emit("dve", lambda eng, dc=dc, tb=tb: eng.tensor_add(
                ctx.xT[:, dc * T + tb * 256: dc * T + (tb + 1) * 256],
                ctx.xT[:, dc * T + tb * 256: dc * T + (tb + 1) * 256],
                ctx.af.sl(o_t, 256)),
                r=[r_t, ctx.xT_r[dc]], w=[ctx.xT_r[dc]], tag=f"l0f_i2{tb}_{dc}")


def emit_l0(ctx, reg):
    emit_ln(ctx, "l0n_")
    tr = lambda k: (lambda inp, k=k: np.ascontiguousarray(np.asarray(inp[k]).T))
    cfg = dict(
        wq_t=reg.dram_in("L0wq_t", [D, D], tr("L0_attn_wq")),
        wk_t=reg.dram_in("L0wk_t", [D, D], tr("L0_attn_wk")),
        wv_t=reg.dram_in("L0wv_t", [D, D], tr("L0_attn_wv")),
        wo_t=reg.dram_in("L0wo_t", [D, D], tr("L0_attn_wo")),
        bq=bias_row(ctx, reg, "L0bq", "L0_attn_bq"),
        bk=bias_row(ctx, reg, "L0bk", "L0_attn_bk"),
        bv=bias_row(ctx, reg, "L0bv", "L0_attn_bv"),
        bo=bias_row(ctx, reg, "L0bo", "L0_attn_bo"),
        rope=False, kind="band", scale=1.0 / 8.0)
    emit_attn_common(ctx, "l0a_", cfg)
    emit_l0_ffn(ctx, reg)


def emit_l2_ffn(ctx, reg):
    p = ctx.p
    w1 = reg.dram_in("L2w1", [8 * D, D], lambda inp: np.ascontiguousarray(
        inp["L2_ffn_w1"].reshape(8 * D, D)))
    w2 = reg.dram_in("L2w2", [8 * D, D], lambda inp: np.ascontiguousarray(
        inp["L2_ffn_w2"].reshape(8 * D, D)))
    rw_t = reg.dram_in("L2rw", [D, 8], lambda inp: inp["L2_ffn_router_w"])
    sgu = reg.dram_in("L2sgu", [D, 2048],
                      lambda inp: inp["L2_ffn_shared_gate_up"])
    sdn = reg.dram_in("L2sdn", [D, D], lambda inp: inp["L2_ffn_shared_down"])
    sgw = reg.dram_in("L2sgw", [128, 8], lambda inp: np.ascontiguousarray(
        np.asarray(inp["L2_ffn_shared_gate_w"]).reshape(8, 128).T))
    ctx.ab.reset()
    ctx.af.reset()
    nE = 8
    o_gt, r_gt = emit_router(ctx, reg, "l2r_", rw_t, nE, None, False)
    o_h, r_h = ctx.ab.take(8 * 256, "h")
    o_wb, r_wb = ctx.ab.take(4096, "wblk")
    o_g, r_g = ctx.ab.take(nE * 256, "G")
    o_bc, r_bc = ctx.ab.take(128, "bcast")
    o_gs, r_gs = ctx.ab.take(8 * 256, "gsh")
    o_us, r_us = ctx.ab.take(8 * 256, "ush")
    o_sa, r_sa = ctx.ab.take(8 * 256, "sact")
    o_sr, r_sr = ctx.ab.take(256, "sigrep")
    o_sw, r_sw = ctx.ab.take(8, "sgw")
    o_sd2, r_sd2 = ctx.ab.take(4096, "sdnblk")
    o_row, r_row = ctx.af.take(256, "sigrow")
    p.emit("pool", lambda e: e.dma_start(
        out=bass.AP(ctx.ab.sb, o_sw, [[ctx.ab.cols, 128], [1, 8]]),
        in_=sgw.ap()), w=[r_sw], dma=True, tag="l2f_lsgw")
    for tb in range(8):
        emit_g_block(ctx, "l2g_", o_gt, r_gt, nE, tb, o_g, r_g, o_bc, r_bc)
        for half in range(4):
            load_w_dc_slice(ctx, o_wb, r_wb, sgu, half * 512, 512, ctx.ab,
                            f"l2f_lgu{tb}_{half}")
            for j in range(4):
                oc = half * 4 + j
                pb, pbr = ctx.pb[4 + j % 2], ctx.pb_r[4 + j % 2]
                for dc in range(8):
                    p.emit("pe", lambda e, dc=dc, j=j, tb=tb, pb=pb: e.matmul(
                        pb[:, 0:256],
                        ctx.ab.sl(o_wb + dc * 512 + j * 128, 128),
                        ctx.xn[:, dc * T + tb * 256: dc * T + (tb + 1) * 256],
                        start=(dc == 0), stop=(dc == 7)),
                        r=[r_wb, ctx.xn_r[dc]], w=[pbr],
                        tag=f"l2f_gu{tb}_{oc}_{dc}")
                if oc < 8:
                    p.emit("act", lambda e, oc=oc, pb=pb: e.activation(
                        ctx.ab.sl(o_gs + oc * 256, 256), pb[:, 0:256], AF.Silu),
                        r=[pbr], w=[r_gs], tag=f"l2f_sl{tb}_{oc}")
                else:
                    p.emit("dve", lambda e, oc=oc, pb=pb: e.tensor_copy(
                        ctx.ab.sl(o_us + (oc - 8) * 256, 256), pb[:, 0:256]),
                        r=[pbr], w=[r_us], tag=f"l2f_up{tb}_{oc}")
        pb6, pbr6 = ctx.pb[6], ctx.pb_r[6]
        for dc in range(8):
            p.emit("pe", lambda e, dc=dc, tb=tb, pb6=pb6: e.matmul(
                pb6[0:1, 0:256], ctx.ab.sl(o_sw + dc, 1),
                ctx.xn[:, dc * T + tb * 256: dc * T + (tb + 1) * 256],
                start=(dc == 0), stop=(dc == 7)),
                r=[r_sw, ctx.xn_r[dc]], w=[pbr6], tag=f"l2f_sg{tb}_{dc}")
        p.emit("act", lambda e, pb6=pb6: e.activation(
            ctx.af.sl(o_row, 256)[0:1, :], pb6[0:1, 0:256], AF.Sigmoid),
            r=[pbr6], w=[r_row], tag=f"l2f_si{tb}")
        p.emit("pe", lambda e, pb6=pb6: e.matmul(
            pb6[:, 0:256], ctx.ones_row_f[:],
            ctx.af.sl(o_row, 256)[0:1, :], start=True, stop=True),
            r=[r_row, ctx.ones_row_f_r], w=[pbr6], tag=f"l2f_sr{tb}")
        p.emit("dve", lambda e, pb6=pb6: e.tensor_copy(
            ctx.ab.sl(o_sr, 256), pb6[:, 0:256]),
            r=[pbr6], w=[r_sr], tag=f"l2f_se{tb}")
        for oc in range(8):
            p.emit("dve", lambda e, oc=oc: e.tensor_mul(
                ctx.ab.sl(o_sa + oc * 256, 256),
                ctx.ab.sl(o_gs + oc * 256, 256),
                ctx.ab.sl(o_us + oc * 256, 256)),
                r=[r_gs, r_us], w=[r_sa], tag=f"l2f_m1{tb}_{oc}")
            p.emit("dve", lambda e, oc=oc: e.tensor_mul(
                ctx.ab.sl(o_sa + oc * 256, 256),
                ctx.ab.sl(o_sa + oc * 256, 256),
                ctx.ab.sl(o_sr, 256)),
                r=[r_sa, r_sr], w=[r_sa], tag=f"l2f_m2{tb}_{oc}")
        for half in range(2) if "shared" not in L2_SKIP else []:
            load_w_dc_slice(ctx, o_sd2, r_sd2, sdn, half * 512, 512, ctx.ab,
                            f"l2f_lsd{tb}_{half}")
            for dc in range(4):
                dcc = half * 4 + dc
                ybank, ybr = ctx.pb[dcc // 2], ctx.pb_r[dcc // 2]
                ycol = (dcc % 2) * 256
                for fc in range(8):
                    p.emit("pe", lambda e, fc=fc, dc=dc, ybank=ybank, ycol=ycol,
                           st=(fc == 0 and dcc % 2 == 0),
                           lst=(fc == 7 and dcc % 2 == 1 and "experts" in L2_SKIP): e.matmul(
                        ybank[:, ycol:ycol + 256],
                        ctx.ab.sl(o_sd2 + fc * 512 + dc * 128, 128),
                        ctx.ab.sl(o_sa + fc * 256, 256),
                        start=st, stop=lst),
                        r=[r_sd2, r_sa], w=[ybr], tag=f"l2f_sd{tb}_{dcc}_{fc}")
        for e_ in range(8) if "experts" not in L2_SKIP else []:
            for half in range(2):
                def ldw1(eng, e_=e_, half=half):
                    dst = bass.AP(ctx.ab.sb, o_wb,
                                  [[ctx.ab.cols, 128], [512, 8], [1, 512]])
                    src = bass.AP(w1, e_ * D * D + half * 512,
                                  [[D, 128], [D * 128, 8], [1, 512]])
                    return eng.dma_start(out=dst, in_=src)
                p.emit("pool", ldw1, w=[r_wb], dma=True,
                       tag=f"l2f_lw1_{tb}_{e_}_{half}")
                for fc in range(4):
                    fci = half * 4 + fc
                    pb, pbr = ctx.pb[4 + fc % 2], ctx.pb_r[4 + fc % 2]
                    for dc in range(8):
                        p.emit("pe", lambda eng, dc=dc, fc=fc, tb=tb, pb=pb: eng.matmul(
                            pb[:, 0:256],
                            ctx.ab.sl(o_wb + dc * 512 + fc * 128, 128),
                            ctx.xn[:, dc * T + tb * 256: dc * T + (tb + 1) * 256],
                            start=(dc == 0), stop=(dc == 7)),
                            r=[r_wb, ctx.xn_r[dc]], w=[pbr],
                            tag=f"l2f_h{tb}_{e_}_{fci}_{dc}")
                    p.emit("act", lambda eng, fci=fci, pb=pb: eng.activation(
                        ctx.ab.sl(o_h + fci * 256, 256), pb[:, 0:256], AF.Silu),
                        r=[pbr], w=[r_h], tag=f"l2f_sh{tb}_{e_}_{fci}")
                    p.emit("dve", lambda eng, fci=fci, e_=e_: eng.tensor_mul(
                        ctx.ab.sl(o_h + fci * 256, 256),
                        ctx.ab.sl(o_h + fci * 256, 256),
                        ctx.ab.sl(o_g + e_ * 256, 256)),
                        r=[r_h, r_g], w=[r_h], tag=f"l2f_gm{tb}_{e_}_{fci}")
            for half in range(2):
                def ldw2(eng, e_=e_, half=half):
                    dst = bass.AP(ctx.ab.sb, o_wb,
                                  [[ctx.ab.cols, 128], [1024, 4], [1, 1024]])
                    src = bass.AP(w2, (e_ * D + half * 512) * D,
                                  [[D, 128], [D * 128, 4], [1, 1024]])
                    return eng.dma_start(out=dst, in_=src)
                p.emit("pool", ldw2, w=[r_wb], dma=True,
                       tag=f"l2f_lw2_{tb}_{e_}_{half}")
                for dc in range(8):
                    ybank, ybr = ctx.pb[dc // 2], ctx.pb_r[dc // 2]
                    ycol = (dc % 2) * 256
                    for fcl in range(4):
                        fci = half * 4 + fcl
                        last = (e_ == 7 and half == 1 and fcl == 3 and dc % 2 == 1)
                        st = (e_ == 0 and half == 0 and fcl == 0 and dc % 2 == 0
                              and "shared" in L2_SKIP)
                        p.emit("pe", lambda eng, dc=dc, fcl=fcl, fci=fci,
                               ybank=ybank, ycol=ycol, last=last, st=st: eng.matmul(
                            ybank[:, ycol:ycol + 256],
                            ctx.ab.sl(o_wb + fcl * 1024 + dc * 128, 128),
                            ctx.ab.sl(o_h + fci * 256, 256),
                            start=st, stop=last),
                            r=[r_wb, r_h], w=[ybr],
                            tag=f"l2f_y{tb}_{e_}_{half}_{dc}_{fcl}")
        for dc in range(8):
            ybank, ybr = ctx.pb[dc // 2], ctx.pb_r[dc // 2]
            ycol = (dc % 2) * 256
            p.emit("dve", lambda eng, dc=dc, tb=tb, ybank=ybank, ycol=ycol: eng.tensor_add(
                ctx.xT[:, dc * T + tb * 256: dc * T + (tb + 1) * 256],
                ctx.xT[:, dc * T + tb * 256: dc * T + (tb + 1) * 256],
                ybank[:, ycol:ycol + 256]),
                r=[ybr, ctx.xT_r[dc]], w=[ctx.xT_r[dc]], tag=f"l2f_ye{tb}_{dc}")


def emit_l2(ctx, reg):
    emit_ln(ctx, "l2n_")
    qkv_t = reg.dram_in("L2qkv_t", [D, 3 * D],
                        lambda inp: np.ascontiguousarray(inp["L2_attn_w_qkv"].T))
    cfg = dict(
        wq_t=(qkv_t, 0), wk_t=(qkv_t, D), wv_t=(qkv_t, 2 * D),
        wo_t=reg.dram_in("L2wp_t", [D, D],
                         lambda inp: np.ascontiguousarray(inp["L2_attn_w_proj"].T)),
        bo=bias_row(ctx, reg, "L2bp", "L2_attn_b_proj"),
        rope=False, kind="full", scale=1.0 / 8.0)
    emit_attn_common(ctx, "l2a_", cfg)
    emit_l2_ffn(ctx, reg)


def l1_ranges(u):
    s = (64 + 512 * u) % T
    if s + 512 <= T:
        return [(s, 512)]
    return [(s, T - s), (0, 512 - (T - s))]


def emit_l1_attn(ctx, reg):
    p = ctx.p

    def perm_cols():
        o = np.arange(3 * D)
        j, rest = o // D, o % D
        h, dk = rest // DH, rest % DH
        return h * 192 + j * DH + dk

    qkv = reg.dram_in("L1qkv", [D, 3 * D],
                      lambda inp: np.ascontiguousarray(
                          np.asarray(inp["L1_attn_w_qkv"])[:, perm_cols()]))
    bias = reg.dram_in("L1bqkv", [1, 3 * D],
                       lambda inp: np.ascontiguousarray(
                           np.asarray(inp["L1_attn_b_qkv"])[perm_cols()]).reshape(1, -1))
    wo = reg.dram_in("L1wo", [D, D], lambda inp: inp["L1_attn_w_o"])
    o_bq, bres = ctx.am.take(3 * D, "l1_bqkv")
    bsb = ctx.am.sb[0:1, o_bq:o_bq + 3 * D]
    p.emit("pool", lambda e: e.dma_start(out=bsb, in_=bias.ap()),
           w=[bres], dma=True, tag="l1a_ldb")
    bo = bias_row(ctx, reg, "L1bo", "L1_attn_b_o")
    s = 1.0 / 8.0
    SH = 30.0
    ctx.ab.reset()
    ctx.af.reset()
    aug = [ctx.ab.take(512, nm) for nm in ("k65a", "k65b", "q65a", "q65b")]
    (o_ka, r_ka), (o_kb, r_kb), (o_qa, r_qa), (o_qb, r_qb) = aug
    o_vt, r_vt = ctx.ab.take(4 * 16 * 65, "v_tok")
    o_w, r_w = ctx.ab.take(4096, "wslice")
    o_wo, r_wo = ctx.ab.take(8192, "wo")
    o_e, r_e = ctx.ab.take(512, "expb")
    o_sqs, r_sqs = ctx.ab.take(512, "sqs")
    o_rec, r_rec = ctx.af.take(512, "recb")
    o_rr, r_rr = ctx.af.take(640, "rrow")
    o_nqa, r_nqa = ctx.af.take(512, "nqa")
    o_nqb, r_nqb = ctx.af.take(512, "nqb")
    rrow = ctx.af.sl(o_rr, 640)
    recb = ctx.af.sl(o_rec, 512)
    VT = 16 * 65
    load_w_dc_slice(ctx, o_wo, r_wo, wo, 0, 1024, ctx.ab, "l1a_lwo")
    for o_k, r_k in ((o_ka, r_ka), (o_kb, r_kb)):
        p.emit("dve", lambda e, o_k=o_k: e.memset(
            ctx.ab.sb[64:65, o_k:o_k + 512], 1.0), w=[r_k], tag="l1a_kones")

    for u in range(4):
        rngs = l1_ranges(u)
        for half in range(2) if 'v' in L1A_PARTS else []:
            load_w_dc_slice(ctx, o_w, r_w, qkv, 2 * D + half * 512, 512,
                            ctx.ab, f"l1a_lv{u}_{half}")
            for wi in range(4):
                pb, pbr = ctx.pb[2 + wi % 2], ctx.pb_r[2 + wi % 2]
                mb = 0
                for (rs, rl) in rngs:
                    t0 = wi * 128
                    a = max(t0, mb)
                    b = min(t0 + 128, mb + rl)
                    if a < b:
                        for dc in range(8):
                            p.emit("pe", lambda e, dc=dc, pb=pb, a=a, b=b,
                                   rs=rs, mb=mb, t0=t0: e.matmul(
                                pb[a - t0:b - t0, 0:512],
                                ctx.xn[:, dc * T + rs + (a - mb):
                                       dc * T + rs + (b - mb)],
                                ctx.ab.sl(o_w + dc * 512, 512),
                                start=(dc == 0), stop=False),
                                r=[ctx.xn_r[dc], r_w], w=[pbr],
                                tag=f"l1a_v{u}_{half}_{wi}_{a}_{dc}")
                        p.emit("pe", lambda e, pb=pb, a=a, b=b, t0=t0,
                               half=half: e.matmul(
                            pb[a - t0:b - t0, 0:512],
                            ctx.ones_row_b[0:1, 0:b - a],
                            bsb[0:1, 2 * D + half * 512: 2 * D + (half + 1) * 512],
                            start=False, stop=True),
                            r=[bres, ctx.ones_row_b_r], w=[pbr],
                            tag=f"l1a_vb{u}_{half}_{wi}_{a}")
                    mb += rl

                def evv(e, wi=wi, half=half, pb=pb):
                    dst = bass.AP(ctx.ab.sb, o_vt + wi * VT + half * 8 * 65,
                                  [[ctx.ab.cols, 128], [65, 8], [1, 64]])
                    return e.tensor_copy(dst, pb[:, 0:512])
                p.emit("dve", evv, r=[pbr], w=[r_vt], tag=f"l1a_ve{u}_{half}_{wi}")

        def setvones(e):
            dst = bass.AP(ctx.ab.sb, o_vt + 64,
                          [[ctx.ab.cols, 128], [VT, 4], [65, 16]])
            return e.memset(dst, 1.0)
        p.emit("dve", setvones, r=[], w=[r_vt], tag=f"l1a_vo{u}")

        for c in range(8) if 'proj' in L1A_PARTS else []:
            def proj(what, base, dsts, nq_mode, c=c, u=u, rngs=rngs):
                load_w_dc_slice(ctx, o_w, r_w, qkv, base + c * 128, 128,
                                ctx.ab, f"l1a_lw{what}{u}_{c}")
                pbq = ctx.pb[4]
                mb = 0
                for (rs, rl) in rngs:
                    for dc in range(8):
                        p.emit("pe", lambda e, dc=dc, rs=rs, rl=rl, mb=mb,
                               pbq=pbq: e.matmul(
                            pbq[:, mb:mb + rl],
                            ctx.ab.sl(o_w + dc * 128, 128),
                            ctx.xn[:, dc * T + rs: dc * T + rs + rl],
                            start=(dc == 0), stop=False),
                            r=[r_w, ctx.xn_r[dc]], w=[ctx.pb_r[4]],
                            tag=f"l1a_{what}mm{u}_{c}_{mb}_{dc}")
                    p.emit("pe", lambda e, rl=rl, mb=mb, pbq=pbq, base=base: e.matmul(
                        pbq[:, mb:mb + rl],
                        bsb[0:1, base + c * 128: base + (c + 1) * 128],
                        ctx.ones_512_b[0:1, 0:rl], start=False, stop=True),
                        r=[bres, ctx.ones_512_b_r], w=[ctx.pb_r[4]],
                        tag=f"l1a_{what}b{u}_{c}_{mb}")
                    mb += rl
                p.emit("act", lambda e, pbq=pbq: e.activation(
                    ctx.ab.sl(o_sqs, 512), pbq[:], AF.Square),
                    r=[ctx.pb_r[4]], w=[r_sqs], tag=f"l1a_{what}sq{u}_{c}")
                for hh in range(2):
                    pbn = ctx.pb[6 + hh]
                    p.emit("pe", lambda e, hh=hh, pbn=pbn: e.matmul(
                        pbn[0:1, :], ctx.ones_col_b[hh * 64:(hh + 1) * 64, :],
                        ctx.ab.sb[hh * 64:(hh + 1) * 64, o_sqs:o_sqs + 512],
                        start=True, stop=True),
                        r=[r_sqs, ctx.ones_col_b_r], w=[ctx.pb_r[6 + hh]],
                        tag=f"l1a_{what}nm{u}_{c}_{hh}")
                    if nq_mode:
                        o_nq = (o_nqa, o_nqb)[hh]
                        r_nq = (r_nqa, r_nqb)[hh]
                        p.emit("dve", lambda e, pbn=pbn, o_nq=o_nq: e.tensor_copy(
                            ctx.af.sb[0:1, o_nq:o_nq + 512], pbn[0:1, :]),
                            r=[ctx.pb_r[6 + hh]], w=[r_nq],
                            tag=f"l1a_nq{u}_{c}_{hh}")
                    else:
                        p.emit("dve", lambda e, hh=hh, pbn=pbn: e.reduce_max(
                            rrow[0:1, 40 + hh:41 + hh], pbn[0:1, :], AX.X),
                            r=[ctx.pb_r[6 + hh]], w=[r_rr],
                            tag=f"l1a_nk{u}_{c}_{hh}")
                for hh, (o_d, r_d) in enumerate(dsts):
                    p.emit("dve", lambda e, hh=hh, o_d=o_d, pbq=pbq: e.tensor_copy(
                        ctx.ab.sb[0:64, o_d:o_d + 512],
                        pbq[hh * 64:(hh + 1) * 64, :]),
                        r=[ctx.pb_r[4]], w=[r_d], tag=f"l1a_{what}ev{u}_{c}_{hh}")

            proj("k", D, ((o_ka, r_ka), (o_kb, r_kb)), False)
            proj("q", 0, ((o_qa, r_qa), (o_qb, r_qb)), True)
            for hh in range(2):
                o_q65 = (o_qa, o_qb)[hh]
                r_q65 = (r_qa, r_qb)[hh]
                o_nq = (o_nqa, o_nqb)[hh]
                r_nq = (r_nqa, r_nqb)[hh]
                p.emit("dve", lambda e, hh=hh, o_nq=o_nq: e.tensor_scalar_mul(
                    ctx.af.sb[0:1, o_nq:o_nq + 512],
                    ctx.af.sb[0:1, o_nq:o_nq + 512],
                    rrow[0:1, 40 + hh:41 + hh]),
                    r=[r_nq, r_rr], w=[r_nq], tag=f"l1a_bm{u}_{c}_{hh}",
                    selfsync=True)
                p.emit("act", lambda e, o_nq=o_nq: e.activation(
                    ctx.af.sb[0:1, o_nq:o_nq + 512],
                    ctx.af.sb[0:1, o_nq:o_nq + 512], AF.Sqrt),
                    r=[r_nq], w=[r_nq], tag=f"l1a_bq{u}_{c}_{hh}")
                p.emit("dve", lambda e, o_nq=o_nq, o_q65=o_q65: e.tensor_scalar(
                    ctx.ab.sb[64:65, o_q65:o_q65 + 512],
                    ctx.af.sb[0:1, o_nq:o_nq + 512], -1.0, SH / s,
                    OP.mult, OP.add),
                    r=[r_nq], w=[r_q65], tag=f"l1a_bw{u}_{c}_{hh}")

            for hh in range(2) if 'scores' in L1A_PARTS else []:
                h = 2 * c + hh
                o_k65 = (o_ka, o_kb)[hh]
                r_k65 = (r_ka, r_kb)[hh]
                o_q65 = (o_qa, o_qb)[hh]
                r_q65 = (r_qa, r_qb)[hh]
                av_bank, av_r = ctx.pb[5], ctx.pb_r[5]
                for wi in range(4):
                    sb, sbr = ctx.pb[wi % 2], ctx.pb_r[wi % 2]
                    p.emit("pe", lambda e, wi=wi, sb=sb, o_k65=o_k65,
                           o_q65=o_q65: e.matmul(
                        sb[:, 0:128],
                        ctx.ab.sb[0:65, o_k65 + wi * 128: o_k65 + (wi + 1) * 128],
                        ctx.ab.sb[0:65, o_q65 + wi * 128: o_q65 + (wi + 1) * 128],
                        start=True, stop=False),
                        r=[r_k65, r_q65], w=[sbr], tag=f"l1a_sc{u}_{h}_{wi}")
                    p.emit("pe", lambda e, sb=sb: e.matmul(
                        sb[:, 0:128], ctx.mk_l1[:], ctx.id_b[:],
                        start=False, stop=True),
                        r=[ctx.mk_l1_r, ctx.id_b_r], w=[sbr],
                        tag=f"l1a_mk{u}_{h}_{wi}")
                    p.emit("act", lambda e, sb=sb: e.activation(
                        ctx.ab.sb[:, o_e:o_e + 128], sb[:, 0:128],
                        AF.Exp, scale=s),
                        r=[sbr], w=[r_e], tag=f"l1a_e{u}_{h}_{wi}")
                    if 'av' in L1A_PARTS:
                        p.emit("pe", lambda e, wi=wi, av_bank=av_bank, h=h: e.matmul(
                            av_bank[0:65, wi * 128: wi * 128 + 128],
                            ctx.ab.sb[0:128,
                                      o_vt + wi * VT + h * 65:
                                      o_vt + wi * VT + h * 65 + 65],
                            ctx.ab.sb[:, o_e:o_e + 128],
                            start=(wi == 0), stop=(wi == 3)),
                            r=[r_vt, r_e], w=[av_r],
                            tag=f"l1a_av{u}_{h}_{wi}")
                if 'recip' not in L1A_PARTS:
                    continue
                p.emit("dve", lambda e, av_bank=av_bank: e.tensor_copy(
                    rrow[0:1, 128:640], av_bank[64:65, :]),
                    r=[av_r], w=[r_rr], tag=f"l1a_s{u}_{h}")
                p.emit("act", lambda e: e.activation(
                    rrow[0:1, 128:640], rrow[0:1, 128:640], AF.Ln),
                    r=[r_rr], w=[r_rr], tag=f"l1a_sl{u}_{h}")
                p.emit("act", lambda e: e.activation(
                    rrow[0:1, 128:640], rrow[0:1, 128:640], AF.Exp, scale=-1.0),
                    r=[r_rr], w=[r_rr], tag=f"l1a_sx{u}_{h}")
                pb7b = ctx.pb[7]
                p.emit("pe", lambda e, pb7b=pb7b: e.matmul(
                    pb7b[:, :], ctx.ones_row_f[:], rrow[0:1, 128:640],
                    start=True, stop=True),
                    r=[r_rr, ctx.ones_row_f_r], w=[ctx.pb_r[7]],
                    tag=f"l1a_sr{u}_{h}")
                p.emit("dve", lambda e, pb7b=pb7b: e.tensor_copy(
                    recb[0:64, :], pb7b[0:64, :]),
                    r=[ctx.pb_r[7]], w=[r_rec], tag=f"l1a_se{u}_{h}")

                def hev(e, hb=hh * 64, u=u, c=c, av_bank=av_bank):
                    return e.tensor_mul(
                        ctx.heads[hb:hb + 64,
                                  c * T + u * 512: c * T + (u + 1) * 512],
                        av_bank[0:64, :], recb[0:64, :])
                p.emit("dve", hev, r=[av_r, r_rec], w=[ctx.heads_r[c]],
                       tag=f"l1a_h{u}_{h}")

        for oc in range(8) if 'oproj' in L1A_PARTS else []:
            pb, pbr = ctx.pb[2 + oc % 2], ctx.pb_r[2 + oc % 2]
            for dc in range(8):
                p.emit("pe", lambda e, oc=oc, dc=dc, u=u, pb=pb: e.matmul(
                    pb[:],
                    ctx.ab.sl(o_wo + dc * 1024 + oc * 128, 128),
                    ctx.heads[:, dc * T + u * 512: dc * T + (u + 1) * 512],
                    start=(dc == 0), stop=False),
                    r=[r_wo, ctx.heads_r[dc]], w=[pbr],
                    tag=f"l1a_o{u}_{oc}_{dc}")
            brow, brr, boff = bo
            p.emit("pe", lambda e, oc=oc, pb=pb, brow=brow, boff=boff: e.matmul(
                pb[:], brow[0:1, boff + oc * 128: boff + (oc + 1) * 128],
                ctx.ones_512_b[:], start=False, stop=True),
                r=[brr, ctx.ones_512_b_r], w=[pbr], tag=f"l1a_ob{u}_{oc}")
            mb = 0
            for (rs, rl) in rngs:
                p.emit("dve", lambda e, oc=oc, rs=rs, rl=rl, mb=mb, pb=pb: e.tensor_add(
                    ctx.xT[:, oc * T + rs: oc * T + rs + rl],
                    ctx.xT[:, oc * T + rs: oc * T + rs + rl],
                    pb[:, mb:mb + rl]),
                    r=[pbr, ctx.xT_r[oc]], w=[ctx.xT_r[oc]],
                    tag=f"l1a_oe{u}_{oc}_{mb}")
                mb += rl


def emit_l1_ffn(ctx, reg):
    p = ctx.p
    gw_t = reg.dram_in("L1gw", [D, 8],
                       lambda inp: np.ascontiguousarray(np.asarray(inp["L1_ffn_gate_w"]).T))
    wg_t = reg.dram_in("L1wg", [D, 4096],
                       lambda inp: np.ascontiguousarray(np.asarray(inp["L1_ffn_w_gate"]).T))
    wu_t = reg.dram_in("L1wu", [D, 4096],
                       lambda inp: np.ascontiguousarray(np.asarray(inp["L1_ffn_w_up"]).T))
    wd_t = reg.dram_in("L1wd", [4096, D],
                       lambda inp: np.ascontiguousarray(np.asarray(inp["L1_ffn_w_down"]).T))
    a1c = reg.dram_in("L1a1", [D, 128],
                      lambda inp: np.ascontiguousarray(
                          np.asarray(inp["L1_ffn_A1"]).transpose(2, 0, 1).reshape(D, 128)))
    a3c = reg.dram_in("L1a3", [D, 128],
                      lambda inp: np.ascontiguousarray(
                          np.asarray(inp["L1_ffn_A3"]).transpose(2, 0, 1).reshape(D, 128)))
    b1c = reg.dram_in("L1b1", [16, 8 * 4096],
                      lambda inp: np.ascontiguousarray(
                          np.asarray(inp["L1_ffn_B1"]).transpose(2, 0, 1).reshape(16, 8 * 4096)))
    b3c = reg.dram_in("L1b3", [16, 8 * 4096],
                      lambda inp: np.ascontiguousarray(
                          np.asarray(inp["L1_ffn_B3"]).transpose(2, 0, 1).reshape(16, 8 * 4096)))
    a2c = reg.dram_in("L1a2", [4096, 128],
                      lambda inp: np.ascontiguousarray(
                          np.asarray(inp["L1_ffn_A2"]).transpose(2, 0, 1).reshape(4096, 128)))
    b2c = reg.dram_in("L1b2", [16, 8 * D],
                      lambda inp: np.ascontiguousarray(
                          np.asarray(inp["L1_ffn_B2"]).transpose(2, 0, 1).reshape(16, 8 * D)))
    ctx.ab.reset()
    ctx.af.reset()
    nE = 8
    o_gt, r_gt = emit_router(ctx, reg, "l1r_", gw_t, nE, None, True)
    o_wgu, r_wgu = ctx.ab.take(8192, "wgu_b2")
    o_wd, r_wd = ctx.ab.take(2048, "wdblk")
    o_b13, r_b13 = ctx.ab.take(1024, "b13blk")
    o_a2, r_a2 = ctx.ab.take(512, "a2blk")
    o_g, r_g = ctx.ab.take(nE * 256, "G")
    o_bc, r_bc = ctx.ab.take(128, "bcast")
    o_t13, r_t13 = ctx.ab.take(4096, "t13_zg")
    o_c1, r_c1 = ctx.ab.take(1024, "c1")
    o_c3, r_c3 = ctx.ab.take(1024, "c3")
    o_us, r_us = ctx.ab.take(512, "us")
    o_ag, r_ag = ctx.ab.take(512, "ag")
    o_mx, r_mx = ctx.ab.take(1024, "mix")
    o_z, r_z = ctx.af.take(2048, "z")

    def t_slot(si, e_):
        return o_t13 + (si * 8 + e_) * 256

    for tb in range(8):
        emit_g_block(ctx, "l1g_", o_gt, r_gt, nE, tb, o_g, r_g, o_bc, r_bc)
        load_w_dc_slice(ctx, o_wgu, r_wgu, a1c, 0, 128, ctx.ab, f"l1f_la1{tb}")
        load_w_dc_slice(ctx, o_wgu + 1024, r_wgu, a3c, 0, 128, ctx.ab,
                        f"l1f_la3{tb}")
        for si in range(2):
            for e_ in range(8):
                pb, pbr = ctx.pb[6], ctx.pb_r[6]
                for dc in range(8):
                    p.emit("pe", lambda e, dc=dc, tb=tb, si=si, e_=e_, pb=pb: e.matmul(
                        pb[0:16, 0:256],
                        bass.AP(ctx.ab.sb, o_wgu + si * 1024 + dc * 128 + e_ * 16,
                                [[ctx.ab.cols, 128], [1, 16]]),
                        ctx.xn[:, dc * T + tb * 256: dc * T + (tb + 1) * 256],
                        start=(dc == 0), stop=(dc == 7)),
                        r=[r_wgu, ctx.xn_r[dc]], w=[pbr],
                        tag=f"l1f_t{tb}_{si}_{e_}_{dc}")
                p.emit("dve", lambda e, si=si, e_=e_, pb=pb: e.tensor_scalar_mul(
                    ctx.ab.sb[0:16, t_slot(si, e_): t_slot(si, e_) + 256],
                    pb[0:16, 0:256], 2.0),
                    r=[pbr], w=[r_t13], tag=f"l1f_te{tb}_{si}_{e_}")
        p.emit("dve", lambda e: e.memset(
            ctx.af.sb[0:16, o_z:o_z + 2048], 0.0), w=[r_z], tag=f"l1f_z0{tb}")
        for fb in range(8):
            load_w_dc_slice(ctx, o_wgu, r_wgu, wg_t, fb * 512, 512, ctx.ab,
                            f"l1f_lg{tb}_{fb}")
            load_w_dc_slice(ctx, o_wgu + 4096, r_wgu, wu_t, fb * 512, 512,
                            ctx.ab, f"l1f_lu{tb}_{fb}")

            def lda2(e, fb=fb):
                dst = bass.AP(ctx.ab.sb, o_a2,
                              [[ctx.ab.cols, 128], [128, 4], [1, 128]])
                src = bass.AP(a2c, fb * 512 * 128,
                              [[128, 128], [128 * 128, 4], [1, 128]])
                return e.dma_start(out=dst, in_=src)
            p.emit("pool", lda2, w=[r_a2], dma=True, tag=f"l1f_la2{tb}_{fb}")
            for si, (oc_, rc_) in enumerate(((o_c1, r_c1), (o_c3, r_c3))):
                for fc in range(4):
                    pb, pbr = ctx.pb[4 + si], ctx.pb_r[4 + si]
                    for dc in range(8):
                        p.emit("pe", lambda e, dc=dc, fc=fc, si=si, tb=tb,
                               pb=pb: e.matmul(
                            pb[:, (fc % 2) * 256:(fc % 2) * 256 + 256],
                            ctx.ab.sl(o_wgu + si * 4096 + dc * 512 + fc * 128, 128),
                            ctx.xn[:, dc * T + tb * 256: dc * T + (tb + 1) * 256],
                            start=(dc == 0 and fc % 2 == 0),
                            stop=(dc == 7 and fc % 2 == 1)),
                            r=[r_wgu, ctx.xn_r[dc]], w=[pbr],
                            tag=f"l1f_c{tb}_{fb}_{si}_{fc}_{dc}")
                    if fc % 2 == 1:
                        p.emit("dve", lambda e, fc=fc, oc_=oc_, rc_=rc_, pb=pb:
                               e.tensor_copy(
                            ctx.ab.sl(oc_ + (fc - 1) * 256, 512), pb[:, 0:512]),
                            r=[pbr], w=[rc_], tag=f"l1f_ce{tb}_{fb}_{si}_{fc}")
            for e_ in range(8):
                def ldb1(eng, e_=e_, fb=fb):
                    dst = bass.AP(ctx.ab.sb, o_b13, [[ctx.ab.cols, 16], [1, 512]])
                    src = bass.AP(b1c, e_ * 4096 + fb * 512,
                                  [[8 * 4096, 16], [1, 512]])
                    return eng.dma_start(out=dst, in_=src)
                p.emit("pool", ldb1, w=[r_b13], dma=True,
                       tag=f"l1f_lb1{tb}_{fb}_{e_}")

                def ldb3(eng, e_=e_, fb=fb):
                    dst = bass.AP(ctx.ab.sb, o_b13 + 512,
                                  [[ctx.ab.cols, 16], [1, 512]])
                    src = bass.AP(b3c, e_ * 4096 + fb * 512,
                                  [[8 * 4096, 16], [1, 512]])
                    return eng.dma_start(out=dst, in_=src)
                p.emit("pool", ldb3, w=[r_b13], dma=True,
                       tag=f"l1f_lb3{tb}_{fb}_{e_}")
                pz, pzr = ctx.pb[6], ctx.pb_r[6]
                for fp in range(2):
                    pd1, pd1r = ctx.pb[4], ctx.pb_r[4]
                    pd3, pd3r = ctx.pb[5], ctx.pb_r[5]
                    for fc in (2 * fp, 2 * fp + 1):
                        p.emit("pe", lambda e, fc=fc, e_=e_, pd1=pd1: e.matmul(
                            pd1[:, (fc % 2) * 256:(fc % 2) * 256 + 256],
                            ctx.ab.sb[0:16, o_b13 + fc * 128: o_b13 + (fc + 1) * 128],
                            ctx.ab.sb[0:16, t_slot(0, e_): t_slot(0, e_) + 256],
                            start=(fc % 2 == 0), stop=(fc % 2 == 1)),
                            r=[r_b13, r_t13], w=[pd1r],
                            tag=f"l1f_d1_{tb}_{fb}_{e_}_{fc}")
                        p.emit("pe", lambda e, fc=fc, e_=e_, pd3=pd3: e.matmul(
                            pd3[:, (fc % 2) * 256:(fc % 2) * 256 + 256],
                            ctx.ab.sb[0:16, o_b13 + 512 + fc * 128:
                                      o_b13 + 512 + (fc + 1) * 128],
                            ctx.ab.sb[0:16, t_slot(1, e_): t_slot(1, e_) + 256],
                            start=(fc % 2 == 0), stop=(fc % 2 == 1)),
                            r=[r_b13, r_t13], w=[pd3r],
                            tag=f"l1f_d3_{tb}_{fb}_{e_}_{fc}")
                    p.emit("dve", lambda e, fp=fp, pd1=pd1: e.scalar_tensor_tensor(
                        ctx.ab.sl(o_us, 512), pd1[:, 0:512], 1.0,
                        ctx.ab.sl(o_c1 + fp * 512, 512), OP.mult, OP.add),
                        r=[pd1r, r_c1], w=[r_us], tag=f"l1f_u{tb}_{fb}_{e_}_{fp}")
                    p.emit("act", lambda e: e.activation(
                        ctx.ab.sl(o_us, 512), ctx.ab.sl(o_us, 512), AF.Silu),
                        r=[r_us], w=[r_us], tag=f"l1f_s{tb}_{fb}_{e_}_{fp}")
                    p.emit("dve", lambda e, fp=fp, pd3=pd3: e.scalar_tensor_tensor(
                        ctx.ab.sl(o_ag, 512), pd3[:, 0:512], 1.0,
                        ctx.ab.sl(o_c3 + fp * 512, 512), OP.mult, OP.add),
                        r=[pd3r, r_c3], w=[r_ag], tag=f"l1f_v{tb}_{fb}_{e_}_{fp}")
                    for fci in range(2):
                        p.emit("dve", lambda e, fci=fci, e_=e_: e.tensor_mul(
                            ctx.ab.sl(o_ag + fci * 256, 256),
                            ctx.ab.sl(o_ag + fci * 256, 256),
                            ctx.ab.sl(o_g + e_ * 256, 256)),
                            r=[r_ag, r_g], w=[r_ag],
                            tag=f"l1f_vg{tb}_{fb}_{e_}_{fp}_{fci}")
                    p.emit("dve", lambda e: e.tensor_mul(
                        ctx.ab.sl(o_ag, 512), ctx.ab.sl(o_ag, 512),
                        ctx.ab.sl(o_us, 512)),
                        r=[r_ag, r_us], w=[r_ag], tag=f"l1f_ag{tb}_{fb}_{e_}_{fp}")
                    if e_ == 0:
                        p.emit("dve", lambda e, fp=fp: e.tensor_copy(
                            ctx.ab.sl(o_mx + fp * 512, 512), ctx.ab.sl(o_ag, 512)),
                            r=[r_ag], w=[r_mx], tag=f"l1f_mx{tb}_{fb}_{e_}_{fp}")
                    else:
                        p.emit("dve", lambda e, fp=fp: e.tensor_add(
                            ctx.ab.sl(o_mx + fp * 512, 512),
                            ctx.ab.sl(o_mx + fp * 512, 512), ctx.ab.sl(o_ag, 512)),
                            r=[r_ag, r_mx], w=[r_mx], tag=f"l1f_mx{tb}_{fb}_{e_}_{fp}")
                    for fc in (2 * fp, 2 * fp + 1):
                        p.emit("pe", lambda e, fc=fc, e_=e_, fp=fp, pz=pz: e.matmul(
                            pz[0:16, 0:256],
                            bass.AP(ctx.ab.sb, o_a2 + fc * 128 + e_ * 16,
                                    [[ctx.ab.cols, 128], [1, 16]]),
                            ctx.ab.sl(o_ag + (fc % 2) * 256, 256),
                            start=(fc == 0), stop=(fc == 3)),
                            r=[r_a2, r_ag], w=[pzr],
                            tag=f"l1f_zm{tb}_{fb}_{e_}_{fc}")
                p.emit("dve", lambda e, e_=e_, pz=pz: e.tensor_add(
                    ctx.af.sb[0:16, o_z + e_ * 256: o_z + (e_ + 1) * 256],
                    ctx.af.sb[0:16, o_z + e_ * 256: o_z + (e_ + 1) * 256],
                    pz[0:16, 0:256]),
                    r=[pzr, r_z], w=[r_z], tag=f"l1f_ze{tb}_{fb}_{e_}")
            for dhalf in range(2):
                def ldwd(e, fb=fb, dhalf=dhalf):
                    dst = bass.AP(ctx.ab.sb, o_wd,
                                  [[ctx.ab.cols, 128], [512, 4], [1, 512]])
                    src = bass.AP(wd_t, fb * 512 * D + dhalf * 512,
                                  [[D, 128], [D * 128, 4], [1, 512]])
                    return e.dma_start(out=dst, in_=src)
                p.emit("pool", ldwd, w=[r_wd], dma=True,
                       tag=f"l1f_lwd{tb}_{fb}_{dhalf}")
                for dc in range(4):
                    dcc = dhalf * 4 + dc
                    ybank, ybr = ctx.pb[dcc // 2], ctx.pb_r[dcc // 2]
                    ycol = (dcc % 2) * 256
                    for fc in range(4):
                        p.emit("pe", lambda e, dc=dc, fc=fc, ybank=ybank,
                               ycol=ycol,
                               st=(fb == 0 and fc == 0 and dcc % 2 == 0): e.matmul(
                            ybank[:, ycol:ycol + 256],
                            ctx.ab.sl(o_wd + fc * 512 + dc * 128, 128),
                            ctx.ab.sl(o_mx + fc * 256, 256),
                            start=st, stop=False),
                            r=[r_wd, r_mx], w=[ybr],
                            tag=f"l1f_y{tb}_{fb}_{dcc}_{fc}")
        def ldb2(e):
            dst = bass.AP(ctx.ab.sb, o_wgu, [[ctx.ab.cols, 16], [1, 8192]])
            src = bass.AP(b2c, 0, [[8 * D, 16], [1, 8192]])
            return e.dma_start(out=dst, in_=src)
        p.emit("pool", ldb2, w=[r_wgu], dma=True, tag=f"l1f_lb2{tb}")
        for e_ in range(8):
            p.emit("dve", lambda e, e_=e_: e.tensor_scalar_mul(
                ctx.ab.sb[0:16, o_t13 + e_ * 256: o_t13 + (e_ + 1) * 256],
                ctx.af.sb[0:16, o_z + e_ * 256: o_z + (e_ + 1) * 256], 2.0),
                r=[r_z], w=[r_t13], tag=f"l1f_zg{tb}_{e_}")
        for dc in range(8):
            ybank, ybr = ctx.pb[dc // 2], ctx.pb_r[dc // 2]
            ycol = (dc % 2) * 256
            for e_ in range(8):
                p.emit("pe", lambda e, dc=dc, e_=e_, ybank=ybank, ycol=ycol,
                       lst=(e_ == 7 and dc % 2 == 1): e.matmul(
                    ybank[:, ycol:ycol + 256],
                    bass.AP(ctx.ab.sb, o_wgu + e_ * D + dc * 128,
                            [[ctx.ab.cols, 16], [1, 128]]),
                    ctx.ab.sb[0:16, o_t13 + e_ * 256: o_t13 + (e_ + 1) * 256],
                    start=False, stop=lst),
                    r=[r_wgu, r_t13], w=[ybr], tag=f"l1f_d2_{tb}_{dc}_{e_}")
        for dc in range(8):
            ybank, ybr = ctx.pb[dc // 2], ctx.pb_r[dc // 2]
            ycol = (dc % 2) * 256
            p.emit("dve", lambda e, dc=dc, tb=tb, ybank=ybank, ycol=ycol: e.tensor_add(
                ctx.xT[:, dc * T + tb * 256: dc * T + (tb + 1) * 256],
                ctx.xT[:, dc * T + tb * 256: dc * T + (tb + 1) * 256],
                ybank[:, ycol:ycol + 256]),
                r=[ybr, ctx.xT_r[dc]], w=[ctx.xT_r[dc]], tag=f"l1f_ye{tb}_{dc}")


def emit_l1(ctx, reg):
    emit_ln(ctx, "l1n_")
    emit_l1_attn(ctx, reg)
    emit_l1_ffn(ctx, reg)


LAYER_EMITTERS = {0: emit_l0, 1: emit_l1, 2: emit_l2, 3: emit_l3}


def build_program(layers, hook=None):
    nc = bass.Bass("TRN2", target_bir_lowering=False)
    p = P(nc)
    reg = Registry(p)
    x_ap = nc.dram_tensor("x", [T, D], F32, kind="ExternalInput").ap()
    out_ap = nc.dram_tensor("out", [T, D], F32, kind="ExternalOutput").ap()
    ctx = Ctx(p, reg)
    emit_load_transpose_in(ctx, x_ap)
    if hook:
        hook(ctx, reg, "after_load")
    for li in layers:
        LAYER_EMITTERS[li](ctx, reg)
        if hook:
            hook(ctx, reg, f"after_l{li}")
    emit_transpose_out_store(ctx, out_ap)
    if hook:
        hook(ctx, reg, "end")
    p.finalize_and_emit()
    return nc, reg, ctx


def kernel(**inputs):
    layers_groups = [[0, 1, 2, 3]]
    x = np.ascontiguousarray(inputs["L0_attn_x"], dtype=np.float32)
    cur = [np.ascontiguousarray(x[c]) for c in range(NCORE)]
    for group in layers_groups:
        nc, reg, _ = build_program(group)
        host = {}
        for name, fn in reg.host.items():
            arr = fn(inputs) if callable(fn) else fn
            host[name] = np.ascontiguousarray(arr, dtype=np.float32)
        in_maps = [dict(host, x=cur[c]) for c in range(NCORE)]
        res = run_bass_kernel_spmd(nc, in_maps, core_ids=list(range(NCORE)))
        cur = [np.asarray(res.results[c]["out"]) for c in range(NCORE)]
    return np.stack(cur, 0)


# version 04 (final)
# speedup vs baseline: 1.7026x; 1.4835x over previous
import contextlib
import numpy as np
import concourse.bass as bass
import concourse.mybir as mybir
from concourse.bass_utils import run_bass_kernel_spmd

F32 = mybir.dt.float32
BF16 = mybir.dt.bfloat16
AF = mybir.ActivationFunctionType
OP = mybir.AluOpType
AX = mybir.AxisListType

T = 2048
D = 1024
NH = 16
DH = 64
NCORE = 8
NEG = -1.0e30
ATTN_DEBUG = None
L2_SKIP = set()
L1A_PARTS = {'v', 'proj', 'scores', 'av', 'recip', 'oproj'}
RAW_WIN = 16


ENG_LIST = ("sp", "pool", "pe", "dve", "act")


class Res:
    __slots__ = ("name", "last_writer", "readers")

    def __init__(self, name=""):
        self.name = name
        self.last_writer = None
        self.readers = []


class Op:
    __slots__ = ("eng", "fn", "deps", "dma", "need_inc", "inc_val",
                 "dma_idx", "waits", "tag", "selfsync", "raw_same", "eseq")

    def __init__(self, eng, fn, dma, tag):
        self.selfsync = False
        self.raw_same = None
        self.eseq = 0
        self.eng = eng
        self.fn = fn
        self.dma = dma
        self.deps = set()
        self.need_inc = False
        self.inc_val = 0
        self.dma_idx = 0
        self.waits = []
        self.tag = tag


class P:
    def __init__(self, nc):
        self.nc = nc
        self.ops = []
        self.csem = {}
        self.dsem = {}
        self._es = contextlib.ExitStack()

    def sbuf(self, name, shape, dtype):
        return self.nc.alloc_sbuf_tensor(name, shape, dtype)

    def psum(self, name, shape, dtype=F32):
        return self.nc.alloc_psum_tensor(name, shape, dtype)

    def emit(self, eng, fn, r=(), w=(), deps=(), dma=False, tag="",
             selfsync=False):
        op = Op(eng, fn, dma, tag)
        op.selfsync = selfsync
        dd = op.deps
        for res in r:
            lw = res.last_writer
            if lw is not None:
                dd.add(lw)
                if lw.eng == eng and not lw.dma and eng in ("dve", "act", "pool"):
                    if op.raw_same is None:
                        op.raw_same = set()
                    op.raw_same.add(lw)
        for res in w:
            if res.last_writer is not None:
                dd.add(res.last_writer)
            dd.update(res.readers)
        for d in deps:
            if d is not None:
                dd.add(d)
        dd.discard(op)
        for res in r:
            res.readers.append(op)
        for res in w:
            res.last_writer = op
            res.readers = []
        self.ops.append(op)
        return op

    def barrier(self, eng, deps, tag="barrier"):
        return self.emit(eng, None, deps=deps, tag=tag)

    def finalize_and_emit(self):
        nc = self.nc
        for e in ENG_LIST:
            self.csem[e] = self._es.enter_context(nc.semaphore(f"cs_{e}"))
            self.dsem[e] = self._es.enter_context(nc.semaphore(f"ds_{e}"))
        eseq = {e: 0 for e in ENG_LIST}
        for op in self.ops:
            eseq[op.eng] += 1
            op.eseq = eseq[op.eng]
        global RAW_WIN
        for op in self.ops:
            for d in op.deps:
                near_raw = (op.raw_same is not None and d in op.raw_same
                            and op.eseq - d.eseq <= RAW_WIN)
                if not d.dma and (d.eng != op.eng or op.selfsync or near_raw):
                    if d.fn is None:
                        raise RuntimeError(f"cross-engine dep on barrier {d.tag}")
                    d.need_inc = True
        cnt = {e: 0 for e in ENG_LIST}
        dcnt = {e: 0 for e in ENG_LIST}
        for op in self.ops:
            if op.dma:
                dcnt[op.eng] += 1
                op.dma_idx = dcnt[op.eng]
            elif op.need_inc:
                cnt[op.eng] += 1
            op.inc_val = cnt[op.eng]
        wm = {e: {} for e in ENG_LIST}
        for op in self.ops:
            mywm = wm[op.eng]
            wl = {}
            for d in op.deps:
                if d.dma:
                    sem, v = self.dsem[d.eng], 16 * d.dma_idx
                elif d.eng == op.eng:
                    near_raw = (op.raw_same is not None and d in op.raw_same
                                and op.eseq - d.eseq <= RAW_WIN)
                    if not (op.selfsync or near_raw):
                        continue
                    sem, v = self.csem[d.eng], d.inc_val
                else:
                    sem, v = self.csem[d.eng], d.inc_val
                key = id(sem)
                if mywm.get(key, 0) >= v:
                    continue
                if key not in wl or wl[key][1] < v:
                    wl[key] = (sem, v)
            op.waits = list(wl.values())
            for key, (sem, v) in wl.items():
                mywm[key] = v
        per = {e: [] for e in ENG_LIST}
        for op in self.ops:
            per[op.eng].append(op)
        blk = self._es.enter_context(nc.Block())
        regmap = {"sp": blk.sync, "pool": blk.gpsimd, "pe": blk.tensor,
                  "dve": blk.vector, "act": blk.scalar}

        def mk(e):
            ops = per[e]

            def runner(engine):
                for op in ops:
                    for sem, v in op.waits:
                        engine.wait_ge(sem, v)
                    if op.fn is None:
                        continue
                    ins = op.fn(engine)
                    if op.dma:
                        ins.then_inc(self.dsem[op.eng], 16)
                    elif op.need_inc:
                        ins.then_inc(self.csem[op.eng], 1)
            return runner
        for e in ENG_LIST:
            regmap[e](mk(e))
        self._es.close()


class Arena:

    def __init__(self, p, name, cols, dtype):
        self.sb = p.sbuf(name, [128, cols], dtype)
        self.cols = cols
        self.name = name
        self.cursor = 0
        self.carry = []
        self.live = []

    def reset(self):
        carry = []
        for r in self.live:
            if r.last_writer is not None:
                carry.append(r.last_writer)
            carry.extend(r.readers)
        self.carry = carry
        self.live = []
        self.cursor = 0

    def take(self, ncols, name):
        off = self.cursor
        self.cursor += ncols
        if self.cursor > self.cols:
            raise RuntimeError(f"arena {self.name} overflow at {name}: "
                               f"{self.cursor} > {self.cols}")
        r = Res(name)
        r.readers = list(self.carry)
        self.live.append(r)
        return off, r

    def sl(self, off, ncols):
        return self.sb[:, off:off + ncols]

    def ap(self, off, dims):
        return bass.AP(self.sb, off, [[self.cols, 128]] + dims)



class Registry:
    def __init__(self, p):
        self.p = p
        self.host = {}
        self.handles = {}
        self._const_cache = {}

    def dram_in(self, name, shape, fn):
        if name in self.handles:
            return self.handles[name]
        t = self.p.nc.dram_tensor(name, list(shape), F32, kind="ExternalInput")
        self.host[name] = fn
        self.handles[name] = t
        return t

    def const_sb(self, name, arr, dtype):
        key = (name, dtype)
        if key in self._const_cache:
            return self._const_cache[key]
        arr = np.asarray(arr, dtype=np.float32)
        if arr.ndim == 1:
            arr = arr[None, :]
        t = self.dram_in("c_" + name, arr.shape, arr)
        sfx = "_bf" if dtype == BF16 else "_f"
        sb = self.p.sbuf("sbc_" + name + sfx, list(arr.shape), dtype)
        res = Res("c_" + name)
        self.p.emit("pool",
                    lambda e, sb=sb, t=t: e.dma_start(out=sb[:], in_=t.ap()),
                    w=[res], dma=True, tag="const_" + name)
        out = (sb, res)
        self._const_cache[key] = out
        return out


class Ctx:
    def __init__(self, p, reg):
        self.p = p
        self.reg = reg
        self.xT = p.sbuf("xT", [128, 8 * T], F32)
        self.xT_r = [Res(f"xT{c}") for c in range(8)]
        self.xn = p.sbuf("xn", [128, 8 * T], BF16)
        self.xn_r = [Res(f"xn{c}") for c in range(8)]
        self.heads = p.sbuf("heads", [128, 8 * T], BF16)
        self.heads_r = [Res(f"hd{c}") for c in range(8)]
        self.pb = [p.psum(f"pb{i}", [128, 512], F32) for i in range(8)]
        self.pb_r = [Res(f"pb{i}") for i in range(8)]
        self.af = Arena(p, "arenaF", 5824, F32)
        self.am = Arena(p, "arenaM", 4608, BF16)
        self.ab = Arena(p, "arenaB", 22224, BF16)
        r = reg.const_sb
        self.id_f, self.id_f_r = r("ident", np.eye(128), F32)
        self.id_b, self.id_b_r = r("ident", np.eye(128), BF16)
        self.ones_col_f, self.ones_col_f_r = r("ones_col", np.ones((128, 1)), F32)
        self.ones_col_b, self.ones_col_b_r = r("ones_col", np.ones((128, 1)), BF16)
        self.ones_row_f, self.ones_row_f_r = r("ones_row", np.ones((1, 128)), F32)
        self.ones_row_b, self.ones_row_b_r = r("ones_row", np.ones((1, 128)), BF16)
        self.ones_512_b, self.ones_512_b_r = r("ones_512", np.ones((1, 512)), BF16)
        self.ones_tile_b, self.ones_tile_b_r = r("ones_tile", np.ones((128, 128)), BF16)
        a = np.arange(128)
        m_blta = np.where(a[None, :] < a[:, None], NEG, 0.0)
        self.mk_blta, self.mk_blta_r = r("mask_blta_T", m_blta.T, BF16)
        self.mk_bgta, self.mk_bgta_r = r("mask_bgta_T", m_blta, BF16)
        blk = (a[:, None] // 64) != (a[None, :] // 64)
        self.mk_l1, self.mk_l1_r = r("mask_l1_T", np.where(blk, NEG, 0.0), BF16)



def emit_load_transpose_in(ctx, x_ap):
    p = ctx.p
    o_tok, r_tok0 = ctx.af.take(1024, "xtok0")
    o_tok1, r_tok1 = ctx.af.take(1024, "xtok1")
    offs = [o_tok, o_tok1]
    rs = [r_tok0, r_tok1]
    for tt in range(16):
        bi = tt % 2
        sl = ctx.af.sl(offs[bi], 1024)
        p.emit("sp", lambda e, sl=sl, tt=tt: e.dma_start(
            out=sl, in_=x_ap[tt * 128:(tt + 1) * 128, :]),
            w=[rs[bi]], dma=True, tag=f"ldx{tt}")
        for g in range(2):
            pb, pr = ctx.pb[g], ctx.pb_r[g]
            for j in range(4):
                c = g * 4 + j
                p.emit("pe", lambda e, sl=sl, c=c, j=j, pb=pb: e.transpose(
                    pb[:, j * 128:(j + 1) * 128],
                    sl[:, c * 128:(c + 1) * 128], ctx.id_f[:]),
                    r=[rs[bi], ctx.id_f_r], w=[pr], tag=f"ti{tt}_{c}")

            def ev(e, g=g, tt=tt, pb=pb):
                dst = bass.AP(ctx.xT, (g * 4) * T + tt * 128,
                              [[8 * T, 128], [T, 4], [1, 128]])
                return e.tensor_copy(dst, pb[:])
            p.emit("dve", ev, r=[pr],
                   w=[ctx.xT_r[g * 4 + j] for j in range(4)], tag=f"ei{tt}_{g}")


def emit_transpose_out_store(ctx, out_ap):
    p = ctx.p
    ctx.af.reset()
    o0, r0 = ctx.af.take(1024, "otok0")
    o1, r1 = ctx.af.take(1024, "otok1")
    offs, rs = [o0, o1], [r0, r1]
    last = None
    for tt in range(16):
        bi = tt % 2
        sl = ctx.af.sl(offs[bi], 1024)
        for g in range(2):
            pb, pr = ctx.pb[g], ctx.pb_r[g]
            for j in range(4):
                c = g * 4 + j
                p.emit("pe", lambda e, c=c, j=j, tt=tt, pb=pb: e.transpose(
                    pb[:, j * 128:(j + 1) * 128],
                    ctx.xT[:, c * T + tt * 128: c * T + (tt + 1) * 128],
                    ctx.id_f[:]),
                    r=[ctx.xT_r[c], ctx.id_f_r], w=[pr], tag=f"to{tt}_{c}")
            p.emit("dve", lambda e, g=g, sl=sl, pb=pb: e.tensor_copy(
                sl[:, g * 512:(g + 1) * 512], pb[:]),
                r=[pr], w=[rs[bi]], tag=f"eo{tt}_{g}")
        last = p.emit("sp", lambda e, sl=sl, tt=tt: e.dma_start(
            out=out_ap[tt * 128:(tt + 1) * 128, :], in_=sl),
            r=[rs[bi]], dma=True, tag=f"stx{tt}")
    p.barrier("sp", deps=[last], tag="final_wait")


def emit_ln(ctx, tag):
    p = ctx.p
    ctx.af.reset()
    ctx.ab.reset()
    ctx.am.reset()
    o_sq, r_sq = ctx.ab.take(T, "sq")
    o_m, r_m = ctx.ab.take(T, "mrep")
    o_r, r_r = ctx.ab.take(T, "rrep")
    o_v, r_v = ctx.af.take(T, "var")
    o_rw, r_rows = ctx.af.take(1024, "lnrows")
    rows = ctx.af.sb[0:1, o_rw:o_rw + 1024]
    sq = ctx.ab.sl(o_sq, T)
    mrep = ctx.ab.sl(o_m, T)
    rrep = ctx.ab.sl(o_r, T)
    var = ctx.af.sl(o_v, T)
    for tb in range(4):
        pbx, pbq = ctx.pb[0], ctx.pb[1]
        for c in range(8):
            p.emit("act", lambda e, c=c, tb=tb: e.activation(
                sq[:, tb * 512:(tb + 1) * 512],
                ctx.xT[:, c * T + tb * 512: c * T + (tb + 1) * 512], AF.Square),
                r=[ctx.xT_r[c]], w=[r_sq], tag=f"{tag}sq{tb}_{c}")
            p.emit("pe", lambda e, c=c, tb=tb, pbx=pbx: e.matmul(
                pbx[0:1, :], ctx.ones_col_f[:],
                ctx.xT[:, c * T + tb * 512: c * T + (tb + 1) * 512],
                start=(c == 0), stop=(c == 7)),
                r=[ctx.xT_r[c], ctx.ones_col_f_r], w=[ctx.pb_r[0]],
                tag=f"{tag}sx{tb}_{c}")
            p.emit("pe", lambda e, c=c, tb=tb, pbq=pbq: e.matmul(
                pbq[0:1, :], ctx.ones_col_b[:],
                sq[:, tb * 512:(tb + 1) * 512],
                start=(c == 0), stop=(c == 7)),
                r=[r_sq, ctx.ones_col_b_r], w=[ctx.pb_r[1]],
                tag=f"{tag}sq2{tb}_{c}")
        p.emit("dve", lambda e, pbx=pbx: e.tensor_copy(
            rows[0:1, 0:512], pbx[0:1, :]),
            r=[ctx.pb_r[0]], w=[r_rows], tag=f"{tag}es{tb}")
        p.emit("dve", lambda e, pbq=pbq: e.tensor_copy(
            rows[0:1, 512:1024], pbq[0:1, :]),
            r=[ctx.pb_r[1]], w=[r_rows], tag=f"{tag}eq{tb}")
        pm, pr2 = ctx.pb[2], ctx.pb[3]
        p.emit("pe", lambda e, pm=pm: e.matmul(
            pm[:], ctx.ones_row_f[:], rows[0:1, 0:512],
            start=True, stop=True),
            r=[r_rows, ctx.ones_row_f_r], w=[ctx.pb_r[2]], tag=f"{tag}rm{tb}")
        p.emit("pe", lambda e, pr2=pr2: e.matmul(
            pr2[:], ctx.ones_row_f[:], rows[0:1, 512:1024],
            start=True, stop=True),
            r=[r_rows, ctx.ones_row_f_r], w=[ctx.pb_r[3]], tag=f"{tag}rr{tb}")
        p.emit("dve", lambda e, tb=tb, pm=pm: e.tensor_scalar_mul(
            mrep[:, tb * 512:(tb + 1) * 512], pm[:], 1.0 / D),
            r=[ctx.pb_r[2]], w=[r_m], tag=f"{tag}mn{tb}")
        p.emit("dve", lambda e, tb=tb: e.tensor_mul(
            var[:, tb * 512:(tb + 1) * 512],
            mrep[:, tb * 512:(tb + 1) * 512],
            mrep[:, tb * 512:(tb + 1) * 512]),
            r=[r_m], w=[r_v], tag=f"{tag}m2{tb}")
        p.emit("dve", lambda e, tb=tb, pr2=pr2: e.scalar_tensor_tensor(
            var[:, tb * 512:(tb + 1) * 512], pr2[:], 1.0 / D,
            var[:, tb * 512:(tb + 1) * 512], OP.mult, OP.subtract),
            r=[ctx.pb_r[3], r_v], w=[r_v], tag=f"{tag}vr{tb}")
        p.emit("dve", lambda e, tb=tb: e.tensor_scalar_add(
            var[:, tb * 512:(tb + 1) * 512],
            var[:, tb * 512:(tb + 1) * 512], 1e-5),
            r=[r_v], w=[r_v], tag=f"{tag}ve{tb}")
        p.emit("act", lambda e, tb=tb: e.activation(
            var[:, tb * 512:(tb + 1) * 512],
            var[:, tb * 512:(tb + 1) * 512], AF.Ln),
            r=[r_v], w=[r_v], tag=f"{tag}ln{tb}")
        p.emit("act", lambda e, tb=tb: e.activation(
            rrep[:, tb * 512:(tb + 1) * 512],
            var[:, tb * 512:(tb + 1) * 512], AF.Exp, scale=-0.5),
            r=[r_v], w=[r_r], tag=f"{tag}ex{tb}")
    for c in range(8):
        p.emit("dve", lambda e, c=c: e.tensor_sub(
            sq[:], ctx.xT[:, c * T:(c + 1) * T], mrep[:]),
            r=[ctx.xT_r[c], r_m], w=[r_sq], tag=f"{tag}xm{c}")
        p.emit("dve", lambda e, c=c: e.tensor_mul(
            ctx.xn[:, c * T:(c + 1) * T], sq[:], rrep[:]),
            r=[r_sq, r_r], w=[ctx.xn_r[c]], tag=f"{tag}xr{c}")


def load_w_dc_slice(ctx, tgt_off, tgt_r, wt, col0, ncols, arena, tag):
    p = ctx.p
    if isinstance(wt, tuple):
        wt, base = wt
        col0 = col0 + base
    nin = int(wt.shape[0])
    wcols = int(wt.shape[1])
    ndc = nin // 128

    def fn(e):
        dst = bass.AP(arena.sb, tgt_off, [[arena.cols, 128], [ncols, ndc], [1, ncols]])
        src = bass.AP(wt, col0, [[wcols, 128], [wcols * 128, ndc],
                                 [1, ncols]])
        return e.dma_start(out=dst, in_=src)
    return p.emit("pool", fn, w=[tgt_r], dma=True, tag=tag)


def rope_tables():
    inv = 1.0 / (10000.0 ** (np.arange(0, DH, 2) / DH))
    pos = np.arange(T)
    ang = pos[None, :] * inv[:, None]
    cos64 = np.concatenate([np.cos(ang), np.cos(ang)], 0)
    sinS = np.concatenate([-np.sin(ang), np.sin(ang)], 0)
    return np.concatenate([cos64, cos64], 0), np.concatenate([sinS, sinS], 0)


def rot_perm_cols(w_t):
    idx = np.arange(w_t.shape[1])
    src = (idx // DH) * DH + ((idx % DH) + 32) % DH
    return np.ascontiguousarray(w_t[:, src])


def attn_kt_plan(kind, g):
    out = []
    if kind == "full":
        for kt in range(16):
            out.append((kt, 0, 512, []))
    elif kind == "causal":
        for kt in range(4 * g + 4):
            c0 = max(0, (kt - 4 * g) * 128)
            masks = [("blta", (kt - 4 * g) * 128)] if kt >= 4 * g else []
            out.append((kt, c0, 512, masks))
    else:
        for kt in range(max(0, 4 * g - 2), min(16, 4 * g + 6)):
            qlo = max(4 * g, kt - 2)
            qhi = min(4 * g + 3, kt + 2)
            c0 = (qlo - 4 * g) * 128
            c1 = (qhi - 4 * g + 1) * 128
            masks = []
            if 4 * g <= kt - 2 <= 4 * g + 3:
                masks.append(("blta", (kt - 2 - 4 * g) * 128))
            if 4 * g <= kt + 2 <= 4 * g + 3:
                masks.append(("bgta", (kt + 2 - 4 * g) * 128))
            out.append((kt, c0, c1, masks))
    return out


def emit_attn_common(ctx, tag, cfg):
    p = ctx.p
    s = cfg["scale"]
    SH = 30.0
    ctx.ab.reset()
    ctx.af.reset()
    o_vt, _ = ctx.ab.take(16 * 8 * 65, "v_tok")
    v_tok_r = [[Res(f"{tag}vt{half}_{kt}") for kt in range(16)] for half in range(2)]
    for half in range(2):
        for kt in range(16):
            v_tok_r[half][kt].readers = list(ctx.ab.carry)
            ctx.ab.live.append(v_tok_r[half][kt])
    aug = []
    for nm in ("k65a", "k65b", "q65a", "q65b"):
        aug.append(ctx.ab.take(T, nm))
    (o_ka, r_ka), (o_kb, r_kb), (o_qa, r_qa), (o_qb, r_qb) = aug
    o_e, _ = ctx.ab.take(1024, "expb")
    expb_r = [Res(f"{tag}ex{i}") for i in range(2)]
    for r_ in expb_r:
        r_.readers = list(ctx.ab.carry)
        ctx.ab.live.append(r_)
    o_w, r_w = ctx.ab.take(2048, "wslice")
    o_wv, r_wv = ctx.ab.take(2048, "wv")
    o_sqs, r_sqs = ctx.ab.take(512, "sqs")
    o_rec, r_rec = ctx.af.take(512, "recb")
    o_rt, r_rt = ctx.af.take(512, "ropet")
    o_rr, r_rr = ctx.af.take(640, "rrow")
    o_nqa, r_nqa = ctx.af.take(T, "nqrowA")
    o_nqb, r_nqb = ctx.af.take(T, "nqrowB")
    recb = ctx.af.sl(o_rec, 512)
    ropet = ctx.af.sl(o_rt, 512)
    rrow = ctx.af.sl(o_rr, 640)
    if cfg["rope"]:
        cos128, sin128 = rope_tables()
        tcos = ctx.reg.dram_in("c_rope_cos", [128, T], cos128)
        tsin = ctx.reg.dram_in("c_rope_sin", [128, T], sin128)
        o_cos, cosb_r = ctx.am.take(T, "rope_cos")
        o_sin, sinb_r = ctx.am.take(T, "rope_sin")
        cosb = ctx.am.sb[:, o_cos:o_cos + T]
        sinb = ctx.am.sb[:, o_sin:o_sin + T]
        p.emit("pool", lambda e: e.dma_start(out=cosb, in_=tcos.ap()),
               w=[cosb_r], dma=True, tag=f"{tag}ldcos")
        p.emit("pool", lambda e: e.dma_start(out=sinb, in_=tsin.ap()),
               w=[sinb_r], dma=True, tag=f"{tag}ldsin")

    VT = 8 * 65

    for o_k, r_k in ((o_ka, r_ka), (o_kb, r_kb)):
        p.emit("dve", lambda e, o_k=o_k: e.memset(
            ctx.ab.sb[64:65, o_k:o_k + T], 1.0),
            w=[r_k], tag=f"{tag}kones")

    def emit_v_quarter(quarter):
        load_w_dc_slice(ctx, o_wv, r_wv, cfg["wv_t"], quarter * 256, 256,
                        ctx.ab, f"{tag}ldv{quarter}")
        half = quarter // 2
        for kt in range(16):
            pb, pbr = ctx.pb[kt % 2], ctx.pb_r[kt % 2]
            for dc in range(8):
                p.emit("pe", lambda e, kt=kt, dc=dc, pb=pb: e.matmul(
                    pb[:, 0:256],
                    ctx.xn[:, dc * T + kt * 128: dc * T + (kt + 1) * 128],
                    ctx.ab.sl(o_wv + dc * 256, 256),
                    start=(dc == 0), stop=(dc == 7 and cfg.get("bv") is None)),
                    r=[ctx.xn_r[dc], r_wv], w=[pbr], tag=f"{tag}v{quarter}_{kt}_{dc}")
            if cfg.get("bv") is not None:
                brow, brr, boff = cfg["bv"]
                p.emit("pe", lambda e, pb=pb, quarter=quarter, brow=brow,
                       boff=boff: e.matmul(
                    pb[:, 0:256], ctx.ones_row_b[:],
                    brow[0:1, boff + quarter * 256: boff + (quarter + 1) * 256],
                    start=False, stop=True),
                    r=[brr, ctx.ones_row_b_r], w=[pbr], tag=f"{tag}vb{quarter}_{kt}")

            def evv(e, kt=kt, quarter=quarter, pb=pb):
                dst = bass.AP(ctx.ab.sb, o_vt + kt * VT + (quarter % 2) * 4 * 65,
                              [[ctx.ab.cols, 128], [65, 4], [1, 64]])
                return e.tensor_copy(dst, pb[:, 0:256])
            p.emit("dve", evv, r=[pbr], w=[v_tok_r[half][kt]],
                   tag=f"{tag}ve{quarter}_{kt}")
        if quarter % 2 == 1:
            def setones(e, half=half):
                dst = bass.AP(ctx.ab.sb, o_vt + 64,
                              [[ctx.ab.cols, 128], [VT, 16], [65, 8]])
                return e.memset(dst, 1.0)
            p.emit("dve", setones, r=[], w=v_tok_r[half], tag=f"{tag}vo{half}")

    def proj(c, what, dsts, bias):
        wi_main, wi_rot = 0, 1
        names = [(cfg["w" + what + "_t"], wi_main)] + (
            [(cfg["w" + what + "_rot_t"], wi_rot)] if cfg["rope"] else [])
        for wt, wi in names:
            load_w_dc_slice(ctx, o_w + wi * 1024, r_w, wt, c * 128, 128,
                            ctx.ab, f"{tag}lw{what}{c}_{wi}")
        (o_da, r_da), (o_db, r_db) = dsts
        for tb in range(4):
            pbq, pbrot = ctx.pb[4], ctx.pb[5]
            for dc in range(8):
                p.emit("pe", lambda e, tb=tb, dc=dc, pbq=pbq: e.matmul(
                    pbq[:],
                    ctx.ab.sl(o_w + 0 * 1024 + dc * 128, 128),
                    ctx.xn[:, dc * T + tb * 512: dc * T + (tb + 1) * 512],
                    start=(dc == 0), stop=(dc == 7 and bias is None)),
                    r=[r_w, ctx.xn_r[dc]], w=[ctx.pb_r[4]],
                    tag=f"{tag}{what}mm{c}_{tb}_{dc}")
            if bias is not None:
                brow, brr, boff = bias
                p.emit("pe", lambda e, pbq=pbq, brow=brow, boff=boff, c=c: e.matmul(
                    pbq[:], brow[0:1, boff + c * 128: boff + (c + 1) * 128],
                    ctx.ones_512_b[:], start=False, stop=True),
                    r=[brr, ctx.ones_512_b_r], w=[ctx.pb_r[4]],
                    tag=f"{tag}{what}b{c}_{tb}")
            p.emit("act", lambda e, pbq=pbq: e.activation(
                ctx.ab.sl(o_sqs, 512), pbq[:], AF.Square),
                r=[ctx.pb_r[4]], w=[r_sqs], tag=f"{tag}{what}sq{c}_{tb}")
            for hh in range(2):
                pbn = ctx.pb[6 + hh]
                p.emit("pe", lambda e, hh=hh, pbn=pbn: e.matmul(
                    pbn[0:1, :], ctx.ones_col_b[hh * 64:(hh + 1) * 64, :],
                    ctx.ab.sb[hh * 64:(hh + 1) * 64, o_sqs:o_sqs + 512],
                    start=True, stop=True),
                    r=[r_sqs, ctx.ones_col_b_r], w=[ctx.pb_r[6 + hh]],
                    tag=f"{tag}{what}nm{c}_{tb}_{hh}")
                if what == "q":
                    o_nq = (o_nqa, o_nqb)[hh]
                    r_nq = (r_nqa, r_nqb)[hh]
                    p.emit("dve", lambda e, tb=tb, hh=hh, pbn=pbn, o_nq=o_nq:
                           e.tensor_copy(
                        ctx.af.sb[0:1, o_nq + tb * 512: o_nq + (tb + 1) * 512],
                        pbn[0:1, :]),
                        r=[ctx.pb_r[6 + hh]], w=[r_nq],
                        tag=f"{tag}nq{c}_{tb}_{hh}")
                else:
                    p.emit("dve", lambda e, tb=tb, hh=hh, pbn=pbn: e.reduce_max(
                        rrow[0:1, 32 + hh * 4 + tb: 33 + hh * 4 + tb],
                        pbn[0:1, :], AX.X),
                        r=[ctx.pb_r[6 + hh]], w=[r_rr],
                        tag=f"{tag}nk{c}_{tb}_{hh}")
            if cfg["rope"]:
                for dc in range(8):
                    p.emit("pe", lambda e, tb=tb, dc=dc, pbrot=pbrot: e.matmul(
                        pbrot[:],
                        ctx.ab.sl(o_w + 1 * 1024 + dc * 128, 128),
                        ctx.xn[:, dc * T + tb * 512: dc * T + (tb + 1) * 512],
                        start=(dc == 0), stop=(dc == 7)),
                        r=[r_w, ctx.xn_r[dc]], w=[ctx.pb_r[5]],
                        tag=f"{tag}{what}rm{c}_{tb}_{dc}")
                p.emit("dve", lambda e, tb=tb, pbq=pbq: e.tensor_mul(
                    ropet[:], pbq[:], cosb[:, tb * 512:(tb + 1) * 512]),
                    r=[ctx.pb_r[4], cosb_r], w=[r_rt], tag=f"{tag}{what}rc{c}_{tb}")
                p.emit("dve", lambda e, tb=tb, pbrot=pbrot: e.tensor_mul(
                    recb[:], pbrot[:], sinb[:, tb * 512:(tb + 1) * 512]),
                    r=[ctx.pb_r[5], sinb_r], w=[r_rec], tag=f"{tag}{what}rs{c}_{tb}")
                for hh, (o_d, r_d) in enumerate(((o_da, r_da), (o_db, r_db))):
                    p.emit("dve", lambda e, tb=tb, hh=hh, o_d=o_d: e.tensor_add(
                        ctx.ab.sb[0:64, o_d + tb * 512: o_d + (tb + 1) * 512],
                        ropet[hh * 64:(hh + 1) * 64, :],
                        recb[hh * 64:(hh + 1) * 64, :]),
                        r=[r_rt, r_rec], w=[r_d], tag=f"{tag}{what}ra{c}_{tb}_{hh}")
            else:
                for hh, (o_d, r_d) in enumerate(((o_da, r_da), (o_db, r_db))):
                    p.emit("dve", lambda e, tb=tb, hh=hh, o_d=o_d, pbq=pbq:
                           e.tensor_copy(
                        ctx.ab.sb[0:64, o_d + tb * 512: o_d + (tb + 1) * 512],
                        pbq[hh * 64:(hh + 1) * 64, :]),
                        r=[ctx.pb_r[4]], w=[r_d], tag=f"{tag}{what}ev{c}_{tb}_{hh}")

    for half in range(2):
        emit_v_quarter(2 * half)
        emit_v_quarter(2 * half + 1)
        for c in range(4 * half, 4 * half + 4):
            proj(c, "k", ((o_ka, r_ka), (o_kb, r_kb)), cfg.get("bk"))
            proj(c, "q", ((o_qa, r_qa), (o_qb, r_qb)), cfg.get("bq"))
            for hh in range(2):
                o_q65 = (o_qa, o_qb)[hh]
                r_q65 = (r_qa, r_qb)[hh]
                o_nq = (o_nqa, o_nqb)[hh]
                r_nq = (r_nqa, r_nqb)[hh]
                p.emit("dve", lambda e, hh=hh: e.reduce_max(
                    rrow[0:1, 40 + hh:41 + hh],
                    rrow[0:1, 32 + hh * 4:36 + hh * 4], AX.X),
                    r=[r_rr], w=[r_rr], tag=f"{tag}mk{c}_{hh}")
                p.emit("dve", lambda e, hh=hh, o_nq=o_nq: e.tensor_scalar_mul(
                    ctx.af.sb[0:1, o_nq:o_nq + T],
                    ctx.af.sb[0:1, o_nq:o_nq + T],
                    rrow[0:1, 40 + hh:41 + hh]),
                    r=[r_nq, r_rr], w=[r_nq], tag=f"{tag}bm{c}_{hh}",
                    selfsync=True)
                p.emit("act", lambda e, o_nq=o_nq: e.activation(
                    ctx.af.sb[0:1, o_nq:o_nq + T],
                    ctx.af.sb[0:1, o_nq:o_nq + T], AF.Sqrt),
                    r=[r_nq], w=[r_nq], tag=f"{tag}bq{c}_{hh}")
                p.emit("dve", lambda e, o_nq=o_nq, o_q65=o_q65: e.tensor_scalar(
                    ctx.ab.sb[64:65, o_q65:o_q65 + T],
                    ctx.af.sb[0:1, o_nq:o_nq + T], -1.0, SH / s,
                    OP.mult, OP.add),
                    r=[r_nq], w=[r_q65], tag=f"{tag}bw{c}_{hh}")

            if ATTN_DEBUG is not None and c == 0:
                ATTN_DEBUG(ctx, tag + "p0", {
                    "k65a": (ctx.ab.sb[0:128, o_ka:o_ka + T], [r_ka]),
                    "q65a": (ctx.ab.sb[0:128, o_qa:o_qa + T], [r_qa]),
                    "nqa": (ctx.af.sb[0:1, o_nqa:o_nqa + T], [r_nqa]),
                    "rrow": (rrow[:], [r_rr])})
            for hh in range(2):
                h = 2 * c + hh
                o_k65 = (o_ka, o_kb)[hh]
                r_k65 = (r_ka, r_kb)[hh]
                o_q65 = (o_qa, o_qb)[hh]
                r_q65 = (r_qa, r_qb)[hh]
                hloc = 2 * (c - 4 * half) + hh
                for g in range(4):
                    kts = attn_kt_plan(cfg["kind"], g)
                    av_bank, av_r = ctx.pb[6], ctx.pb_r[6]
                    n_kts = len(kts)

                    def emit_av(i, kt, c0, c1, hloc=hloc, h=h, g=g,
                                n_kts=n_kts, av_bank=av_bank, av_r=av_r,
                                half=half):
                        p.emit("pe", lambda e, i=i, kt=kt, c0=c0, c1=c1: e.matmul(
                            av_bank[0:65, c0:c1],
                            bass.AP(ctx.ab.sb, o_vt + kt * VT + hloc * 65,
                                    [[ctx.ab.cols, 128], [1, 65]]),
                            ctx.ab.sb[:, o_e + (i % 2) * 512 + c0:
                                      o_e + (i % 2) * 512 + c1],
                            start=(i == 0), stop=(i == n_kts - 1)),
                            r=[v_tok_r[half][kt], expb_r[i % 2]], w=[av_r],
                            tag=f"{tag}av{h}_{g}_{kt}")

                    prev = None
                    for i, (kt, c0, c1, masks) in enumerate(kts):
                        sb, sbr = ctx.pb[i % 2], ctx.pb_r[i % 2]
                        p.emit("pe", lambda e, kt=kt, c0=c0, c1=c1, sb=sb, g=g,
                               o_k65=o_k65, o_q65=o_q65,
                               nm=len(masks): e.matmul(
                            sb[:, c0:c1],
                            ctx.ab.sb[0:65, o_k65 + kt * 128: o_k65 + (kt + 1) * 128],
                            ctx.ab.sb[0:65, o_q65 + g * 512 + c0: o_q65 + g * 512 + c1],
                            start=True, stop=(nm == 0)),
                            r=[r_k65, r_q65], w=[sbr], tag=f"{tag}sc{h}_{g}_{kt}")
                        for mi, (mt, mc0) in enumerate(masks):
                            mco = ctx.mk_blta if mt == "blta" else ctx.mk_bgta
                            mcr = ctx.mk_blta_r if mt == "blta" else ctx.mk_bgta_r
                            p.emit("pe", lambda e, sb=sb, mc0=mc0, mco=mco: e.matmul(
                                sb[:, mc0:mc0 + 128], mco[:], ctx.id_b[:],
                                start=False, stop=(mi == len(masks) - 1)),
                                r=[mcr, ctx.id_b_r], w=[sbr],
                                tag=f"{tag}m{h}_{g}_{kt}_{mi}")
                        p.emit("act", lambda e, i=i, c0=c0, c1=c1, sb=sb: e.activation(
                            ctx.ab.sb[:, o_e + (i % 2) * 512 + c0:
                                      o_e + (i % 2) * 512 + c1],
                            sb[:, c0:c1], AF.Exp, scale=s),
                            r=[sbr], w=[expb_r[i % 2]],
                            tag=f"{tag}e{h}_{g}_{kt}")
                        if prev is not None:
                            emit_av(*prev)
                        prev = (i, kt, c0, c1)
                    emit_av(*prev)
                    p.emit("dve", lambda e: e.tensor_copy(
                        rrow[0:1, 128:640], av_bank[64:65, :]),
                        r=[av_r], w=[r_rr], tag=f"{tag}s{h}_{g}")
                    if ATTN_DEBUG is not None and h == 0 and g == 0:
                        ATTN_DEBUG(ctx, tag + "se", {
                            "sumexp": (rrow[0:1, 128:640], [r_rr])})
                    p.emit("act", lambda e: e.activation(
                        rrow[0:1, 128:640], rrow[0:1, 128:640], AF.Ln),
                        r=[r_rr], w=[r_rr], tag=f"{tag}sl{h}_{g}")
                    p.emit("act", lambda e: e.activation(
                        rrow[0:1, 128:640], rrow[0:1, 128:640], AF.Exp,
                        scale=-1.0),
                        r=[r_rr], w=[r_rr], tag=f"{tag}sx{h}_{g}")
                    if ATTN_DEBUG is not None and h == 0 and g == 0:
                        ATTN_DEBUG(ctx, tag + "sx", {
                            "recrow": (rrow[0:1, 128:640], [r_rr])})
                    pb7b = ctx.pb[7]
                    p.emit("pe", lambda e, pb7b=pb7b: e.matmul(
                        pb7b[:, :], ctx.ones_row_f[:],
                        rrow[0:1, 128:640], start=True, stop=True),
                        r=[r_rr, ctx.ones_row_f_r], w=[ctx.pb_r[7]],
                        tag=f"{tag}sr{h}_{g}")
                    p.emit("dve", lambda e, pb7b=pb7b: e.tensor_copy(
                        recb[0:64, :], pb7b[0:64, :]),
                        r=[ctx.pb_r[7]], w=[r_rec], tag=f"{tag}se{h}_{g}")

                    def hev(e, hb=hh * 64, g=g, c=c):
                        return e.tensor_mul(
                            ctx.heads[hb:hb + 64,
                                      c * T + g * 512: c * T + (g + 1) * 512],
                            av_bank[0:64, :], recb[0:64, :])
                    p.emit("dve", hev, r=[av_r, r_rec], w=[ctx.heads_r[c]],
                           tag=f"{tag}h{h}_{g}")
                    if ATTN_DEBUG is not None and h == 0 and g == 0:
                        ATTN_DEBUG(ctx, tag + "rc", {
                            "recb": (recb[0:64, :], [r_rec])})

    ctx.ab.reset()
    o_wo, r_wo = ctx.ab.take(8192, "wob")
    for ohalf in range(2):
        load_w_dc_slice(ctx, o_wo, r_wo, cfg["wo_t"], ohalf * 512, 512,
                        ctx.ab, f"{tag}lwo{ohalf}")
        for oc in range(4):
            occ = ohalf * 4 + oc
            for tb in range(4):
                pb, pbr = ctx.pb[2 + (tb % 2)], ctx.pb_r[2 + (tb % 2)]
                for dc in range(8):
                    p.emit("pe", lambda e, oc=oc, tb=tb, dc=dc, pb=pb: e.matmul(
                        pb[:],
                        ctx.ab.sl(o_wo + dc * 512 + oc * 128, 128),
                        ctx.heads[:, dc * T + tb * 512: dc * T + (tb + 1) * 512],
                        start=(dc == 0),
                        stop=(dc == 7 and cfg.get("bo") is None)),
                        r=[r_wo, ctx.heads_r[dc]], w=[pbr],
                        tag=f"{tag}o{occ}_{tb}_{dc}")
                if cfg.get("bo") is not None:
                    brow, brr, boff = cfg["bo"]
                    p.emit("pe", lambda e, occ=occ, pb=pb, brow=brow, boff=boff: e.matmul(
                        pb[:], brow[0:1, boff + occ * 128: boff + (occ + 1) * 128],
                        ctx.ones_512_b[:], start=False, stop=True),
                        r=[brr, ctx.ones_512_b_r], w=[pbr],
                        tag=f"{tag}ob{occ}_{tb}")
                p.emit("dve", lambda e, occ=occ, tb=tb, pb=pb: e.tensor_add(
                    ctx.xT[:, occ * T + tb * 512: occ * T + (tb + 1) * 512],
                    ctx.xT[:, occ * T + tb * 512: occ * T + (tb + 1) * 512],
                    pb[:]),
                    r=[pbr, ctx.xT_r[occ]], w=[ctx.xT_r[occ]],
                    tag=f"{tag}oe{occ}_{tb}")



def emit_l3(ctx, reg):
    p = ctx.p
    emit_ln(ctx, "l3n_")
    tr = lambda k: (lambda inp, k=k: np.ascontiguousarray(inp[k].T))
    cfg = dict(
        wq_t=reg.dram_in("L3wq_t", [D, D], tr("L3_attn_Wq")),
        wk_t=reg.dram_in("L3wk_t", [D, D], tr("L3_attn_Wk")),
        wv_t=reg.dram_in("L3wv_t", [D, D], tr("L3_attn_Wv")),
        wo_t=reg.dram_in("L3wo_t", [D, D], tr("L3_attn_Wo")),
        wq_rot_t=reg.dram_in("L3wqr_t", [D, D],
                             lambda inp: rot_perm_cols(np.ascontiguousarray(inp["L3_attn_Wq"].T))),
        wk_rot_t=reg.dram_in("L3wkr_t", [D, D],
                             lambda inp: rot_perm_cols(np.ascontiguousarray(inp["L3_attn_Wk"].T))),
        rope=True, kind="causal", scale=1.0 / 8.0)
    emit_attn_common(ctx, "l3a_", cfg)

    w_t = reg.dram_in("L3w_t", [D, D], tr("L3_ffn_weights"))
    sc = reg.dram_in("L3sc", [128, 8],
                     lambda inp: np.ascontiguousarray(
                         inp["L3_ffn_scale"].reshape(8, 128).T))
    bc = reg.dram_in("L3bc", [128, 8],
                     lambda inp: np.ascontiguousarray(
                         inp["L3_ffn_bias"].reshape(8, 128).T))
    ctx.ab.reset()
    ctx.af.reset()
    o_w, r_w = ctx.ab.take(8192, "l3fw")
    o_t, r_t = ctx.af.take(512, "l3ft")
    o_sc, r_sc = ctx.af.take(16, "l3fsc")
    tmp = ctx.af.sl(o_t, 512)
    scb = ctx.af.sl(o_sc, 16)
    p.emit("sp", lambda e: e.dma_start(out=scb[:, 0:8], in_=sc.ap()),
           w=[r_sc], dma=True, tag="l3f_lsc")
    p.emit("sp", lambda e: e.dma_start(out=scb[:, 8:16], in_=bc.ap()),
           w=[r_sc], dma=True, tag="l3f_lbc")
    for ohalf in range(2):
        load_w_dc_slice(ctx, o_w, r_w, w_t, ohalf * 512, 512, ctx.ab,
                        f"l3f_lw{ohalf}")
        p.emit("dve", lambda e: e.tensor_scalar(
            ctx.ab.sl(o_w, 8192), ctx.ab.sl(o_w, 8192), 0.0, 2.0,
            OP.is_ge, OP.mult),
            r=[r_w], w=[r_w], tag=f"l3f_s1{ohalf}")
        p.emit("dve", lambda e: e.tensor_scalar_add(
            ctx.ab.sl(o_w, 8192), ctx.ab.sl(o_w, 8192), -1.0),
            r=[r_w], w=[r_w], tag=f"l3f_s2{ohalf}")
        for oc in range(4):
            occ = ohalf * 4 + oc
            for tb in range(4):
                pb, pbr = ctx.pb[4 + (tb % 2)], ctx.pb_r[4 + (tb % 2)]
                for dc in range(8):
                    p.emit("pe", lambda e, oc=oc, tb=tb, dc=dc, pb=pb: e.matmul(
                        pb[:],
                        ctx.ab.sl(o_w + dc * 512 + oc * 128, 128),
                        ctx.xn[:, dc * T + tb * 512: dc * T + (tb + 1) * 512],
                        start=(dc == 0), stop=(dc == 7)),
                        r=[r_w, ctx.xn_r[dc]], w=[pbr],
                        tag=f"l3f_m{occ}_{tb}_{dc}")
                p.emit("dve", lambda e, occ=occ, pb=pb: e.tensor_scalar(
                    tmp[:], pb[:], scb[:, occ:occ + 1], scb[:, 8 + occ:9 + occ],
                    OP.mult, OP.add),
                    r=[pbr, r_sc], w=[r_t], tag=f"l3f_sb{occ}_{tb}")
                p.emit("dve", lambda e: e.tensor_scalar(
                    tmp[:], tmp[:], 100.0, -100.0, OP.min, OP.max),
                    r=[r_t], w=[r_t], tag=f"l3f_cl{occ}_{tb}")
                p.emit("dve", lambda e, occ=occ, tb=tb: e.tensor_add(
                    ctx.xT[:, occ * T + tb * 512: occ * T + (tb + 1) * 512],
                    ctx.xT[:, occ * T + tb * 512: occ * T + (tb + 1) * 512],
                    tmp[:]),
                    r=[r_t, ctx.xT_r[occ]], w=[ctx.xT_r[occ]],
                    tag=f"l3f_x{occ}_{tb}")


def bias_row(ctx, reg, name, key, n=D):
    t = reg.dram_in(name, [1, n],
                    lambda inp, key=key: np.asarray(inp[key]).reshape(1, -1))
    off, res = ctx.am.take(n, name)
    sb = ctx.am.sb[0:1, off:off + n]
    ctx.p.emit("pool", lambda e: e.dma_start(out=sb, in_=t.ap()),
               w=[res], dma=True, tag="ld_" + name)
    return (sb, res, 0)


def emit_router(ctx, reg, tag, w_t, nE, brow, renorm, logits_carve=None):
    p = ctx.p
    o_rw, r_rw = ctx.ab.take(8 * nE, "rw")
    o_gt, r_gt = ctx.af.take(16 * nE, "gates_tok")
    if logits_carve is None:
        o_lg, r_lg = ctx.af.take(T, "logits")
    else:
        o_lg, r_lg = logits_carve
    o_sc, r_sc = ctx.af.take(4 * nE + 8, "rscratch")
    load_w_dc_slice(ctx, o_rw, r_rw, w_t, 0, nE, ctx.ab, f"{tag}ldw")
    lg = ctx.af.sl(o_lg, T)
    for tb in range(4):
        pb, pbr = ctx.pb[6], ctx.pb_r[6]
        for dc in range(8):
            p.emit("pe", lambda e, dc=dc, tb=tb, pb=pb: e.matmul(
                pb[0:nE, 0:512], ctx.ab.sl(o_rw + dc * nE, nE),
                ctx.xn[:, dc * T + tb * 512: dc * T + (tb + 1) * 512],
                start=(dc == 0), stop=(dc == 7 and brow is None)),
                r=[r_rw, ctx.xn_r[dc]], w=[pbr], tag=f"{tag}mm{tb}_{dc}")
        if brow is not None:
            bsb, bres, boff = brow
            p.emit("pe", lambda e, pb=pb, bsb=bsb, boff=boff: e.matmul(
                pb[0:nE, 0:512], bsb[0:1, boff:boff + nE],
                ctx.ones_512_b[:], start=False, stop=True),
                r=[bres, ctx.ones_512_b_r], w=[pbr], tag=f"{tag}b{tb}")
        p.emit("dve", lambda e, tb=tb, pb=pb: e.tensor_copy(
            lg[0:nE, tb * 512:(tb + 1) * 512], pb[0:nE, 0:512]),
            r=[pbr], w=[r_lg], tag=f"{tag}ev{tb}")
    sc = ctx.af.sl(o_sc, 4 * nE + 8)
    for tt in range(16):
        pb, pbr = ctx.pb[7], ctx.pb_r[7]
        p.emit("pe", lambda e, tt=tt, pb=pb: e.transpose(
            pb[:, 0:nE], lg[0:nE, tt * 128:(tt + 1) * 128],
            ctx.id_f[0:nE, 0:nE]),
            r=[r_lg, ctx.id_f_r], w=[pbr], tag=f"{tag}tr{tt}")
        C = 3 * nE
        p.emit("dve", lambda e, pb=pb: e.reduce_max(
            sc[:, C:C + 1], pb[:, 0:nE], AX.X, None, None, True),
            r=[pbr], w=[r_sc], tag=f"{tag}rm{tt}")
        p.emit("act", lambda e, pb=pb: e.activation(
            sc[:, 0:nE], pb[:, 0:nE], AF.Exp, bias=sc[:, C:C + 1],
            accum_out=sc[:, C + 1:C + 2]),
            r=[pbr, r_sc], w=[r_sc], tag=f"{tag}ex{tt}")
        p.emit("dve", lambda e: e.reciprocal(
            sc[:, C + 2:C + 3], sc[:, C + 1:C + 2]),
            r=[r_sc], w=[r_sc], tag=f"{tag}rc{tt}")
        p.emit("dve", lambda e: e.tensor_scalar_mul(
            sc[:, 0:nE], sc[:, 0:nE], sc[:, C + 2:C + 3]),
            r=[r_sc], w=[r_sc], tag=f"{tag}pr{tt}", selfsync=True)
        p.emit("dve", lambda e: e.reduce_max(
            sc[:, C + 3:C + 4], sc[:, 0:nE], AX.X),
            r=[r_sc], w=[r_sc], tag=f"{tag}m1{tt}")
        p.emit("dve", lambda e: e.tensor_scalar(
            sc[:, nE:2 * nE], sc[:, 0:nE], sc[:, C + 3:C + 4], NEG,
            OP.is_equal, OP.mult),
            r=[r_sc], w=[r_sc], tag=f"{tag}mk{tt}", selfsync=True)
        p.emit("dve", lambda e: e.tensor_add(
            sc[:, nE:2 * nE], sc[:, nE:2 * nE], sc[:, 0:nE]),
            r=[r_sc], w=[r_sc], tag=f"{tag}ms{tt}")
        p.emit("dve", lambda e: e.reduce_max(
            sc[:, C + 4:C + 5], sc[:, nE:2 * nE], AX.X),
            r=[r_sc], w=[r_sc], tag=f"{tag}m2{tt}")
        p.emit("dve", lambda e: e.tensor_scalar(
            sc[:, 2 * nE:3 * nE], sc[:, 0:nE], sc[:, C + 4:C + 5], None,
            OP.is_ge),
            r=[r_sc], w=[r_sc], tag=f"{tag}ge{tt}", selfsync=True)
        if renorm:
            p.emit("dve", lambda e: e.tensor_add(
                sc[:, C + 5:C + 6], sc[:, C + 3:C + 4], sc[:, C + 4:C + 5]),
                r=[r_sc], w=[r_sc], tag=f"{tag}s12{tt}")
            p.emit("dve", lambda e: e.reciprocal(
                sc[:, C + 6:C + 7], sc[:, C + 5:C + 6]),
                r=[r_sc], w=[r_sc], tag=f"{tag}rs{tt}")
            p.emit("dve", lambda e: e.tensor_scalar_mul(
                sc[:, 0:nE], sc[:, 0:nE], sc[:, C + 6:C + 7]),
                r=[r_sc], w=[r_sc], tag=f"{tag}rn{tt}", selfsync=True)
        p.emit("dve", lambda e, tt=tt: e.tensor_mul(
            ctx.af.sl(o_gt + tt * nE, nE), sc[:, 2 * nE:3 * nE], sc[:, 0:nE]),
            r=[r_sc], w=[r_gt], tag=f"{tag}g{tt}")
    if ATTN_DEBUG is not None:
        ATTN_DEBUG(ctx, tag + "gt", {"gates": (ctx.af.sl(o_gt, 16 * nE), [r_gt]),
                                     "sc": (sc[:], [r_sc]),
                                     "lg": (lg[:], [r_lg])})
    return o_gt, r_gt


def emit_g_one(ctx, tag, o_gt, r_gt, nE, e, tb, tbw, o_g, r_g, o_bc, r_bc,
               pbi=6):
    p = ctx.p
    pb, pbr = ctx.pb[pbi], ctx.pb_r[pbi]
    for half in range(tbw // 128):
        tt = (tb * tbw) // 128 + half
        p.emit("dve", lambda e_, e=e, tt=tt: e_.tensor_scalar_mul(
            ctx.ab.sl(o_bc, 128), ctx.ones_tile_b[:],
            bass.AP(ctx.af.sb, o_gt + tt * nE + e,
                    [[ctx.af.cols, 128], [1, 1]])),
            r=[r_gt, ctx.ones_tile_b_r], w=[r_bc], tag=f"{tag}bc{tb}_{half}_{e}",
            selfsync=True)
        p.emit("pe", lambda e_, pb=pb: e_.matmul(
            pb[:, 0:128], ctx.ab.sl(o_bc, 128), ctx.id_b[:],
            start=True, stop=True),
            r=[r_bc, ctx.id_b_r], w=[pbr], tag=f"{tag}gr{tb}_{half}_{e}")
        p.emit("dve", lambda e_, half=half, pb=pb: e_.tensor_copy(
            ctx.ab.sl(o_g + half * 128, 128), pb[:, 0:128]),
            r=[pbr], w=[r_g], tag=f"{tag}ge{tb}_{half}_{e}")


def emit_g_block(ctx, tag, o_gt, r_gt, nE, tb, o_g, r_g, o_bc, r_bc):
    p = ctx.p
    for half in range(2):
        tt = tb * 2 + half
        pb, pbr = ctx.pb[6], ctx.pb_r[6]
        for e in range(nE):
            p.emit("dve", lambda e_, e=e, tt=tt: e_.tensor_scalar_mul(
                ctx.ab.sl(o_bc, 128), ctx.ones_tile_b[:],
                bass.AP(ctx.af.sb, o_gt + tt * nE + e,
                        [[ctx.af.cols, 128], [1, 1]])),
                r=[r_gt, ctx.ones_tile_b_r], w=[r_bc],
                tag=f"{tag}bc{tb}_{half}_{e}", selfsync=True)
            p.emit("pe", lambda e_, pb=pb: e_.matmul(
                pb[:, 0:128], ctx.ab.sl(o_bc, 128), ctx.id_b[:],
                start=True, stop=True),
                r=[r_bc, ctx.id_b_r], w=[pbr], tag=f"{tag}gr{tb}_{half}_{e}")
            p.emit("dve", lambda e_, e=e, half=half, pb=pb: e_.tensor_copy(
                ctx.ab.sl(o_g + e * 256 + half * 128, 128), pb[:, 0:128]),
                r=[pbr], w=[r_g], tag=f"{tag}ge{tb}_{half}_{e}")


def emit_l0_ffn(ctx, reg):
    p = ctx.p
    w1 = reg.dram_in("L0w1", [8 * D, 4096], lambda inp: np.ascontiguousarray(
        inp["L0_ffn_w1"].reshape(8 * D, 4096)))
    w2 = reg.dram_in("L0w2", [8 * 4096, D], lambda inp: np.ascontiguousarray(
        inp["L0_ffn_w2"].reshape(8 * 4096, D)))
    rw_t = reg.dram_in("L0rw", [D, 9], lambda inp: inp["L0_ffn_router_w"])
    rb = bias_row(ctx, reg, "L0rb", "L0_ffn_router_b", 9)
    ctx.ab.reset()
    ctx.af.reset()
    nE = 9
    o_big, r_big = ctx.af.take(4096, "y_logits")
    o_gt, r_gt = emit_router(ctx, reg, "l0r_", rw_t, nE, rb, False,
                             logits_carve=(o_big, r_big))
    r_y = Res("l0f_y")
    r_y.readers = list(ctx.af.carry) + [op for op in (r_big.last_writer,)
                                        if op is not None] + list(r_big.readers)
    ctx.af.live.append(r_y)
    o_h, r_h = ctx.ab.take(32 * 512, "h")
    o_wA, r_wA = ctx.ab.take(2048, "wA")
    o_wB, r_wB = ctx.ab.take(2048, "wB")
    o_g, r_g = ctx.ab.take(512, "G")
    o_bc, r_bc = ctx.ab.take(128, "bcast")
    wcar = [(o_wA, r_wA), (o_wB, r_wB)]
    for tb in range(4):
        for e in range(8):
            emit_g_one(ctx, "l0g_", o_gt, r_gt, nE, e, tb, 512, o_g, r_g,
                       o_bc, r_bc)
            for fbh in range(16):
                o_w, r_w = wcar[fbh % 2]

                def ldw1(eng, e=e, fbh=fbh, o_w=o_w):
                    dst = bass.AP(ctx.ab.sb, o_w,
                                  [[ctx.ab.cols, 128], [256, 8], [1, 256]])
                    src = bass.AP(w1, e * D * 4096 + fbh * 256,
                                  [[4096, 128], [4096 * 128, 8], [1, 256]])
                    return eng.dma_start(out=dst, in_=src)
                p.emit("pool", ldw1, w=[r_w], dma=True,
                       tag=f"l0f_lw1_{tb}_{e}_{fbh}")
                for fc in range(2):
                    pb, pbr = ctx.pb[4 + fc], ctx.pb_r[4 + fc]
                    for dc in range(8):
                        p.emit("pe", lambda eng, dc=dc, fc=fc, tb=tb, pb=pb,
                               o_w=o_w: eng.matmul(
                            pb[:],
                            ctx.ab.sl(o_w + dc * 256 + fc * 128, 128),
                            ctx.xn[:, dc * T + tb * 512: dc * T + (tb + 1) * 512],
                            start=(dc == 0), stop=(dc == 7)),
                            r=[r_w, ctx.xn_r[dc]], w=[pbr],
                            tag=f"l0f_h{tb}_{e}_{fbh}_{fc}_{dc}")
                    fci = fbh * 2 + fc
                    p.emit("act", lambda eng, fci=fci, pb=pb: eng.activation(
                        ctx.ab.sl(o_h + fci * 512, 512), pb[:], AF.Gelu),
                        r=[pbr], w=[r_h], tag=f"l0f_g{tb}_{e}_{fci}")
                    p.emit("dve", lambda eng, fci=fci: eng.tensor_mul(
                        ctx.ab.sl(o_h + fci * 512, 512),
                        ctx.ab.sl(o_h + fci * 512, 512),
                        ctx.ab.sl(o_g, 512)),
                        r=[r_h, r_g], w=[r_h], tag=f"l0f_gm{tb}_{e}_{fci}")
            for fbh in range(16):
                o_w, r_w = wcar[fbh % 2]

                def ldw2(eng, e=e, fbh=fbh, o_w=o_w):
                    dst = bass.AP(ctx.ab.sb, o_w,
                                  [[ctx.ab.cols, 128], [1024, 2], [1, 1024]])
                    src = bass.AP(w2, (e * 4096 + fbh * 256) * D,
                                  [[D, 128], [D * 128, 2], [1, 1024]])
                    return eng.dma_start(out=dst, in_=src)
                p.emit("pool", ldw2, w=[r_w], dma=True,
                       tag=f"l0f_lw2_{tb}_{e}_{fbh}")
                for dc in range(8):
                    ybank, ybr = ctx.pb[dc], ctx.pb_r[dc]
                    for fcl in range(2):
                        fci = fbh * 2 + fcl
                        p.emit("pe", lambda eng, dc=dc, fcl=fcl, fci=fci,
                               ybank=ybank, o_w=o_w,
                               st=(fbh == 0 and fcl == 0),
                               lst=(fbh == 15 and fcl == 1): eng.matmul(
                            ybank[:],
                            ctx.ab.sl(o_w + fcl * 1024 + dc * 128, 128),
                            ctx.ab.sl(o_h + fci * 512, 512),
                            start=st, stop=lst),
                            r=[r_w, r_h], w=[ybr],
                            tag=f"l0f_y{tb}_{e}_{fbh}_{dc}_{fcl}")
            for dc in range(8):
                ybank, ybr = ctx.pb[dc], ctx.pb_r[dc]
                if e == 0:
                    p.emit("dve", lambda eng, dc=dc, ybank=ybank: eng.tensor_copy(
                        ctx.af.sl(o_big + dc * 512, 512), ybank[:]),
                        r=[ybr], w=[r_y], tag=f"l0f_yc{tb}_{e}_{dc}")
                else:
                    p.emit("dve", lambda eng, dc=dc, ybank=ybank: eng.tensor_add(
                        ctx.af.sl(o_big + dc * 512, 512),
                        ctx.af.sl(o_big + dc * 512, 512), ybank[:]),
                        r=[ybr, r_y], w=[r_y], tag=f"l0f_ya{tb}_{e}_{dc}")
        emit_g_one(ctx, "l0gi_", o_gt, r_gt, nE, 8, tb, 512, o_g, r_g,
                   o_bc, r_bc)
        for dc in range(8):
            p.emit("dve", lambda eng, dc=dc, tb=tb: eng.tensor_mul(
                ctx.ab.sl(o_h + dc * 512, 512),
                ctx.xn[:, dc * T + tb * 512: dc * T + (tb + 1) * 512],
                ctx.ab.sl(o_g, 512)),
                r=[ctx.xn_r[dc], r_g], w=[r_h], tag=f"l0f_i1{tb}_{dc}")
            p.emit("dve", lambda eng, dc=dc, tb=tb: eng.tensor_add(
                ctx.af.sl(o_big + dc * 512, 512),
                ctx.af.sl(o_big + dc * 512, 512),
                ctx.ab.sl(o_h + dc * 512, 512)),
                r=[r_h, r_y], w=[r_y], tag=f"l0f_i2{tb}_{dc}")
            p.emit("dve", lambda eng, dc=dc, tb=tb: eng.tensor_add(
                ctx.xT[:, dc * T + tb * 512: dc * T + (tb + 1) * 512],
                ctx.xT[:, dc * T + tb * 512: dc * T + (tb + 1) * 512],
                ctx.af.sl(o_big + dc * 512, 512)),
                r=[r_y, ctx.xT_r[dc]], w=[ctx.xT_r[dc]], tag=f"l0f_x{tb}_{dc}")


def emit_l0(ctx, reg):
    emit_ln(ctx, "l0n_")
    tr = lambda k: (lambda inp, k=k: np.ascontiguousarray(np.asarray(inp[k]).T))
    cfg = dict(
        wq_t=reg.dram_in("L0wq_t", [D, D], tr("L0_attn_wq")),
        wk_t=reg.dram_in("L0wk_t", [D, D], tr("L0_attn_wk")),
        wv_t=reg.dram_in("L0wv_t", [D, D], tr("L0_attn_wv")),
        wo_t=reg.dram_in("L0wo_t", [D, D], tr("L0_attn_wo")),
        bq=bias_row(ctx, reg, "L0bq", "L0_attn_bq"),
        bk=bias_row(ctx, reg, "L0bk", "L0_attn_bk"),
        bv=bias_row(ctx, reg, "L0bv", "L0_attn_bv"),
        bo=bias_row(ctx, reg, "L0bo", "L0_attn_bo"),
        rope=False, kind="band", scale=1.0 / 8.0)
    emit_attn_common(ctx, "l0a_", cfg)
    emit_l0_ffn(ctx, reg)


def emit_l2_ffn(ctx, reg):
    p = ctx.p
    w1 = reg.dram_in("L2w1", [8 * D, D], lambda inp: np.ascontiguousarray(
        inp["L2_ffn_w1"].reshape(8 * D, D)))
    w2 = reg.dram_in("L2w2", [8 * D, D], lambda inp: np.ascontiguousarray(
        inp["L2_ffn_w2"].reshape(8 * D, D)))
    rw_t = reg.dram_in("L2rw", [D, 8], lambda inp: inp["L2_ffn_router_w"])
    sgu = reg.dram_in("L2sgu", [D, 2048],
                      lambda inp: inp["L2_ffn_shared_gate_up"])
    sdn = reg.dram_in("L2sdn", [D, D], lambda inp: inp["L2_ffn_shared_down"])
    sgw = reg.dram_in("L2sgw", [128, 8], lambda inp: np.ascontiguousarray(
        np.asarray(inp["L2_ffn_shared_gate_w"]).reshape(8, 128).T))
    ctx.ab.reset()
    ctx.af.reset()
    nE = 8
    o_big, r_big = ctx.af.take(4096, "y_logits")
    o_gt, r_gt = emit_router(ctx, reg, "l2r_", rw_t, nE, None, False,
                             logits_carve=(o_big, r_big))
    r_y = Res("l2f_y")
    r_y.readers = list(ctx.af.carry) + [op for op in (r_big.last_writer,)
                                        if op is not None] + list(r_big.readers)
    ctx.af.live.append(r_y)
    o_h, r_h = ctx.ab.take(8 * 512, "h")
    o_wA, r_wA = ctx.ab.take(2048, "wA")
    o_wB, r_wB = ctx.ab.take(2048, "wB")
    o_g, r_g = ctx.ab.take(512, "G")
    o_bc, r_bc = ctx.ab.take(128, "bcast")
    o_gs, r_gs = ctx.ab.take(8 * 512, "gsh")
    o_us, r_us = ctx.ab.take(8 * 512, "ush")
    o_sr, r_sr = ctx.ab.take(512, "sigrep")
    o_sw, r_sw = ctx.ab.take(8, "sgw")
    o_row, r_row = ctx.af.take(512, "sigrow")
    wcar = [(o_wA, r_wA), (o_wB, r_wB)]
    p.emit("pool", lambda e: e.dma_start(
        out=bass.AP(ctx.ab.sb, o_sw, [[ctx.ab.cols, 128], [1, 8]]),
        in_=sgw.ap()), w=[r_sw], dma=True, tag="l2f_lsgw")
    for tb in range(4):
        for blk in range(8):
            o_w, r_w = wcar[blk % 2]

            def ldgu(eng, blk=blk, o_w=o_w):
                dst = bass.AP(ctx.ab.sb, o_w,
                              [[ctx.ab.cols, 128], [256, 8], [1, 256]])
                src = bass.AP(sgu, blk * 256,
                              [[2048, 128], [2048 * 128, 8], [1, 256]])
                return eng.dma_start(out=dst, in_=src)
            p.emit("pool", ldgu, w=[r_w], dma=True, tag=f"l2f_lgu{tb}_{blk}")
            for j in range(2):
                oc = blk * 2 + j
                pb, pbr = ctx.pb[4 + j], ctx.pb_r[4 + j]
                for dc in range(8):
                    p.emit("pe", lambda e, dc=dc, j=j, tb=tb, pb=pb, o_w=o_w: e.matmul(
                        pb[:],
                        ctx.ab.sl(o_w + dc * 256 + j * 128, 128),
                        ctx.xn[:, dc * T + tb * 512: dc * T + (tb + 1) * 512],
                        start=(dc == 0), stop=(dc == 7)),
                        r=[r_w, ctx.xn_r[dc]], w=[pbr],
                        tag=f"l2f_gu{tb}_{oc}_{dc}")
                if oc < 8:
                    p.emit("act", lambda e, oc=oc, pb=pb: e.activation(
                        ctx.ab.sl(o_gs + oc * 512, 512), pb[:], AF.Silu),
                        r=[pbr], w=[r_gs], tag=f"l2f_sl{tb}_{oc}")
                else:
                    p.emit("dve", lambda e, oc=oc, pb=pb: e.tensor_copy(
                        ctx.ab.sl(o_us + (oc - 8) * 512, 512), pb[:]),
                        r=[pbr], w=[r_us], tag=f"l2f_up{tb}_{oc}")
        pb6, pbr6 = ctx.pb[6], ctx.pb_r[6]
        for dc in range(8):
            p.emit("pe", lambda e, dc=dc, tb=tb, pb6=pb6: e.matmul(
                pb6[0:1, :], ctx.ab.sl(o_sw + dc, 1),
                ctx.xn[:, dc * T + tb * 512: dc * T + (tb + 1) * 512],
                start=(dc == 0), stop=(dc == 7)),
                r=[r_sw, ctx.xn_r[dc]], w=[pbr6], tag=f"l2f_sg{tb}_{dc}")
        p.emit("act", lambda e, pb6=pb6: e.activation(
            ctx.af.sl(o_row, 512)[0:1, :], pb6[0:1, :], AF.Sigmoid),
            r=[pbr6], w=[r_row], tag=f"l2f_si{tb}")
        p.emit("pe", lambda e, pb6=pb6: e.matmul(
            pb6[:], ctx.ones_row_f[:], ctx.af.sl(o_row, 512)[0:1, :],
            start=True, stop=True),
            r=[r_row, ctx.ones_row_f_r], w=[pbr6], tag=f"l2f_sr{tb}")
        p.emit("dve", lambda e, pb6=pb6: e.tensor_copy(
            ctx.ab.sl(o_sr, 512), pb6[:]),
            r=[pbr6], w=[r_sr], tag=f"l2f_se{tb}")
        for oc in range(8):
            p.emit("dve", lambda e, oc=oc: e.tensor_mul(
                ctx.ab.sl(o_gs + oc * 512, 512),
                ctx.ab.sl(o_gs + oc * 512, 512),
                ctx.ab.sl(o_us + oc * 512, 512)),
                r=[r_gs, r_us], w=[r_gs], tag=f"l2f_m1{tb}_{oc}")
            p.emit("dve", lambda e, oc=oc: e.tensor_mul(
                ctx.ab.sl(o_gs + oc * 512, 512),
                ctx.ab.sl(o_gs + oc * 512, 512),
                ctx.ab.sl(o_sr, 512)),
                r=[r_gs, r_sr], w=[r_gs], tag=f"l2f_m2{tb}_{oc}")
        for fbh in range(2):
            for dbh in range(2):
                o_w, r_w = wcar[(fbh * 2 + dbh) % 2]

                def ldsd2(eng, fbh=fbh, dbh=dbh, o_w=o_w):
                    dst = bass.AP(ctx.ab.sb, o_w,
                                  [[ctx.ab.cols, 128], [512, 4], [1, 512]])
                    src = bass.AP(sdn, fbh * 512 * D + dbh * 512,
                                  [[D, 128], [D * 128, 4], [1, 512]])
                    return eng.dma_start(out=dst, in_=src)
                p.emit("pool", ldsd2, w=[r_w], dma=True,
                       tag=f"l2f_lsd2{tb}_{fbh}_{dbh}")
                for dc in range(4):
                    dcc = dbh * 4 + dc
                    ybank, ybr = ctx.pb[dcc], ctx.pb_r[dcc]
                    for fc in range(4):
                        fci = fbh * 4 + fc
                        p.emit("pe", lambda e, dc=dc, fc=fc, fci=fci,
                               ybank=ybank, o_w=o_w,
                               st=(fbh == 0 and fc == 0),
                               lst=(fbh == 1 and fc == 3): e.matmul(
                            ybank[:],
                            ctx.ab.sl(o_w + fc * 512 + dc * 128, 128),
                            ctx.ab.sl(o_gs + fci * 512, 512),
                            start=st, stop=lst),
                            r=[r_w, r_gs], w=[ybr],
                            tag=f"l2f_sd{tb}_{dcc}_{fci}")
        for dc in range(8):
            ybank, ybr = ctx.pb[dc], ctx.pb_r[dc]
            p.emit("dve", lambda e, dc=dc, ybank=ybank: e.tensor_copy(
                ctx.af.sl(o_big + dc * 512, 512), ybank[:]),
                r=[ybr], w=[r_y], tag=f"l2f_yc{tb}_{dc}")
        for e_ in range(8):
            emit_g_one(ctx, "l2g_", o_gt, r_gt, nE, e_, tb, 512, o_g, r_g,
                       o_bc, r_bc, pbi=6)
            for blk in range(4):
                o_w, r_w = wcar[blk % 2]

                def ldw1(eng, e_=e_, blk=blk, o_w=o_w):
                    dst = bass.AP(ctx.ab.sb, o_w,
                                  [[ctx.ab.cols, 128], [256, 8], [1, 256]])
                    src = bass.AP(w1, e_ * D * D + blk * 256,
                                  [[D, 128], [D * 128, 8], [1, 256]])
                    return eng.dma_start(out=dst, in_=src)
                p.emit("pool", ldw1, w=[r_w], dma=True,
                       tag=f"l2f_lw1_{tb}_{e_}_{blk}")
                for fc in range(2):
                    fci = blk * 2 + fc
                    pb, pbr = ctx.pb[4 + fc], ctx.pb_r[4 + fc]
                    for dc in range(8):
                        p.emit("pe", lambda eng, dc=dc, fc=fc, tb=tb, pb=pb,
                               o_w=o_w: eng.matmul(
                            pb[:],
                            ctx.ab.sl(o_w + dc * 256 + fc * 128, 128),
                            ctx.xn[:, dc * T + tb * 512: dc * T + (tb + 1) * 512],
                            start=(dc == 0), stop=(dc == 7)),
                            r=[r_w, ctx.xn_r[dc]], w=[pbr],
                            tag=f"l2f_h{tb}_{e_}_{fci}_{dc}")
                    p.emit("act", lambda eng, fci=fci, pb=pb: eng.activation(
                        ctx.ab.sl(o_h + fci * 512, 512), pb[:], AF.Silu),
                        r=[pbr], w=[r_h], tag=f"l2f_sh{tb}_{e_}_{fci}")
                    p.emit("dve", lambda eng, fci=fci: eng.tensor_mul(
                        ctx.ab.sl(o_h + fci * 512, 512),
                        ctx.ab.sl(o_h + fci * 512, 512),
                        ctx.ab.sl(o_g, 512)),
                        r=[r_h, r_g], w=[r_h], tag=f"l2f_gm{tb}_{e_}_{fci}")
            for blk in range(4):
                o_w, r_w = wcar[blk % 2]

                def ldw2(eng, e_=e_, blk=blk, o_w=o_w):
                    dst = bass.AP(ctx.ab.sb, o_w,
                                  [[ctx.ab.cols, 128], [1024, 2], [1, 1024]])
                    src = bass.AP(w2, (e_ * D + blk * 256) * D,
                                  [[D, 128], [D * 128, 2], [1, 1024]])
                    return eng.dma_start(out=dst, in_=src)
                p.emit("pool", ldw2, w=[r_w], dma=True,
                       tag=f"l2f_lw2_{tb}_{e_}_{blk}")
                for dc in range(8):
                    ybank, ybr = ctx.pb[dc], ctx.pb_r[dc]
                    for fcl in range(2):
                        fci = blk * 2 + fcl
                        p.emit("pe", lambda eng, dc=dc, fcl=fcl, fci=fci,
                               ybank=ybank, o_w=o_w,
                               st=(blk == 0 and fcl == 0),
                               lst=(blk == 3 and fcl == 1): eng.matmul(
                            ybank[:],
                            ctx.ab.sl(o_w + fcl * 1024 + dc * 128, 128),
                            ctx.ab.sl(o_h + fci * 512, 512),
                            start=st, stop=lst),
                            r=[r_w, r_h], w=[ybr],
                            tag=f"l2f_y{tb}_{e_}_{blk}_{dc}_{fcl}")
            for dc in range(8):
                ybank, ybr = ctx.pb[dc], ctx.pb_r[dc]
                p.emit("dve", lambda eng, dc=dc, ybank=ybank: eng.tensor_add(
                    ctx.af.sl(o_big + dc * 512, 512),
                    ctx.af.sl(o_big + dc * 512, 512), ybank[:]),
                    r=[ybr, r_y], w=[r_y], tag=f"l2f_ya{tb}_{e_}_{dc}")
        for dc in range(8):
            p.emit("dve", lambda e, dc=dc, tb=tb: e.tensor_add(
                ctx.xT[:, dc * T + tb * 512: dc * T + (tb + 1) * 512],
                ctx.xT[:, dc * T + tb * 512: dc * T + (tb + 1) * 512],
                ctx.af.sl(o_big + dc * 512, 512)),
                r=[r_y, ctx.xT_r[dc]], w=[ctx.xT_r[dc]], tag=f"l2f_x{tb}_{dc}")


def emit_l2(ctx, reg):
    emit_ln(ctx, "l2n_")
    qkv_t = reg.dram_in("L2qkv_t", [D, 3 * D],
                        lambda inp: np.ascontiguousarray(inp["L2_attn_w_qkv"].T))
    cfg = dict(
        wq_t=(qkv_t, 0), wk_t=(qkv_t, D), wv_t=(qkv_t, 2 * D),
        wo_t=reg.dram_in("L2wp_t", [D, D],
                         lambda inp: np.ascontiguousarray(inp["L2_attn_w_proj"].T)),
        bo=bias_row(ctx, reg, "L2bp", "L2_attn_b_proj"),
        rope=False, kind="full", scale=1.0 / 8.0)
    emit_attn_common(ctx, "l2a_", cfg)
    emit_l2_ffn(ctx, reg)


def l1_ranges(u):
    s = (64 + 512 * u) % T
    if s + 512 <= T:
        return [(s, 512)]
    return [(s, T - s), (0, 512 - (T - s))]


def emit_l1_attn(ctx, reg):
    p = ctx.p

    def perm_cols():
        o = np.arange(3 * D)
        j, rest = o // D, o % D
        h, dk = rest // DH, rest % DH
        return h * 192 + j * DH + dk

    qkv = reg.dram_in("L1qkv", [D, 3 * D],
                      lambda inp: np.ascontiguousarray(
                          np.asarray(inp["L1_attn_w_qkv"])[:, perm_cols()]))
    bias = reg.dram_in("L1bqkv", [1, 3 * D],
                       lambda inp: np.ascontiguousarray(
                           np.asarray(inp["L1_attn_b_qkv"])[perm_cols()]).reshape(1, -1))
    wo = reg.dram_in("L1wo", [D, D], lambda inp: inp["L1_attn_w_o"])
    o_bq, bres = ctx.am.take(3 * D, "l1_bqkv")
    bsb = ctx.am.sb[0:1, o_bq:o_bq + 3 * D]
    p.emit("pool", lambda e: e.dma_start(out=bsb, in_=bias.ap()),
           w=[bres], dma=True, tag="l1a_ldb")
    bo = bias_row(ctx, reg, "L1bo", "L1_attn_b_o")
    s = 1.0 / 8.0
    SH = 30.0
    ctx.ab.reset()
    ctx.af.reset()
    aug = [ctx.ab.take(512, nm) for nm in ("k65a", "k65b", "q65a", "q65b")]
    (o_ka, r_ka), (o_kb, r_kb), (o_qa, r_qa), (o_qb, r_qb) = aug
    o_vt, r_vt = ctx.ab.take(4 * 16 * 65, "v_tok")
    o_w, r_w = ctx.ab.take(4096, "wslice")
    o_wo, r_wo = ctx.ab.take(8192, "wo")
    o_e, r_e = ctx.ab.take(512, "expb")
    o_sqs, r_sqs = ctx.ab.take(512, "sqs")
    o_rec, r_rec = ctx.af.take(512, "recb")
    o_rr, r_rr = ctx.af.take(640, "rrow")
    o_nqa, r_nqa = ctx.af.take(512, "nqa")
    o_nqb, r_nqb = ctx.af.take(512, "nqb")
    rrow = ctx.af.sl(o_rr, 640)
    recb = ctx.af.sl(o_rec, 512)
    VT = 16 * 65
    load_w_dc_slice(ctx, o_wo, r_wo, wo, 0, 1024, ctx.ab, "l1a_lwo")
    for o_k, r_k in ((o_ka, r_ka), (o_kb, r_kb)):
        p.emit("dve", lambda e, o_k=o_k: e.memset(
            ctx.ab.sb[64:65, o_k:o_k + 512], 1.0), w=[r_k], tag="l1a_kones")

    for u in range(4):
        rngs = l1_ranges(u)
        for half in range(2) if 'v' in L1A_PARTS else []:
            load_w_dc_slice(ctx, o_w, r_w, qkv, 2 * D + half * 512, 512,
                            ctx.ab, f"l1a_lv{u}_{half}")
            for wi in range(4):
                pb, pbr = ctx.pb[2 + wi % 2], ctx.pb_r[2 + wi % 2]
                mb = 0
                for (rs, rl) in rngs:
                    t0 = wi * 128
                    a = max(t0, mb)
                    b = min(t0 + 128, mb + rl)
                    if a < b:
                        for dc in range(8):
                            p.emit("pe", lambda e, dc=dc, pb=pb, a=a, b=b,
                                   rs=rs, mb=mb, t0=t0: e.matmul(
                                pb[a - t0:b - t0, 0:512],
                                ctx.xn[:, dc * T + rs + (a - mb):
                                       dc * T + rs + (b - mb)],
                                ctx.ab.sl(o_w + dc * 512, 512),
                                start=(dc == 0), stop=False),
                                r=[ctx.xn_r[dc], r_w], w=[pbr],
                                tag=f"l1a_v{u}_{half}_{wi}_{a}_{dc}")
                        p.emit("pe", lambda e, pb=pb, a=a, b=b, t0=t0,
                               half=half: e.matmul(
                            pb[a - t0:b - t0, 0:512],
                            ctx.ones_row_b[0:1, 0:b - a],
                            bsb[0:1, 2 * D + half * 512: 2 * D + (half + 1) * 512],
                            start=False, stop=True),
                            r=[bres, ctx.ones_row_b_r], w=[pbr],
                            tag=f"l1a_vb{u}_{half}_{wi}_{a}")
                    mb += rl

                def evv(e, wi=wi, half=half, pb=pb):
                    dst = bass.AP(ctx.ab.sb, o_vt + wi * VT + half * 8 * 65,
                                  [[ctx.ab.cols, 128], [65, 8], [1, 64]])
                    return e.tensor_copy(dst, pb[:, 0:512])
                p.emit("dve", evv, r=[pbr], w=[r_vt], tag=f"l1a_ve{u}_{half}_{wi}")

        def setvones(e):
            dst = bass.AP(ctx.ab.sb, o_vt + 64,
                          [[ctx.ab.cols, 128], [VT, 4], [65, 16]])
            return e.memset(dst, 1.0)
        p.emit("dve", setvones, r=[], w=[r_vt], tag=f"l1a_vo{u}")

        for c in range(8) if 'proj' in L1A_PARTS else []:
            def proj(what, base, dsts, nq_mode, c=c, u=u, rngs=rngs):
                load_w_dc_slice(ctx, o_w, r_w, qkv, base + c * 128, 128,
                                ctx.ab, f"l1a_lw{what}{u}_{c}")
                pbq = ctx.pb[4]
                mb = 0
                for (rs, rl) in rngs:
                    for dc in range(8):
                        p.emit("pe", lambda e, dc=dc, rs=rs, rl=rl, mb=mb,
                               pbq=pbq: e.matmul(
                            pbq[:, mb:mb + rl],
                            ctx.ab.sl(o_w + dc * 128, 128),
                            ctx.xn[:, dc * T + rs: dc * T + rs + rl],
                            start=(dc == 0), stop=False),
                            r=[r_w, ctx.xn_r[dc]], w=[ctx.pb_r[4]],
                            tag=f"l1a_{what}mm{u}_{c}_{mb}_{dc}")
                    p.emit("pe", lambda e, rl=rl, mb=mb, pbq=pbq, base=base: e.matmul(
                        pbq[:, mb:mb + rl],
                        bsb[0:1, base + c * 128: base + (c + 1) * 128],
                        ctx.ones_512_b[0:1, 0:rl], start=False, stop=True),
                        r=[bres, ctx.ones_512_b_r], w=[ctx.pb_r[4]],
                        tag=f"l1a_{what}b{u}_{c}_{mb}")
                    mb += rl
                p.emit("act", lambda e, pbq=pbq: e.activation(
                    ctx.ab.sl(o_sqs, 512), pbq[:], AF.Square),
                    r=[ctx.pb_r[4]], w=[r_sqs], tag=f"l1a_{what}sq{u}_{c}")
                for hh in range(2):
                    pbn = ctx.pb[6 + hh]
                    p.emit("pe", lambda e, hh=hh, pbn=pbn: e.matmul(
                        pbn[0:1, :], ctx.ones_col_b[hh * 64:(hh + 1) * 64, :],
                        ctx.ab.sb[hh * 64:(hh + 1) * 64, o_sqs:o_sqs + 512],
                        start=True, stop=True),
                        r=[r_sqs, ctx.ones_col_b_r], w=[ctx.pb_r[6 + hh]],
                        tag=f"l1a_{what}nm{u}_{c}_{hh}")
                    if nq_mode:
                        o_nq = (o_nqa, o_nqb)[hh]
                        r_nq = (r_nqa, r_nqb)[hh]
                        p.emit("dve", lambda e, pbn=pbn, o_nq=o_nq: e.tensor_copy(
                            ctx.af.sb[0:1, o_nq:o_nq + 512], pbn[0:1, :]),
                            r=[ctx.pb_r[6 + hh]], w=[r_nq],
                            tag=f"l1a_nq{u}_{c}_{hh}")
                    else:
                        p.emit("dve", lambda e, hh=hh, pbn=pbn: e.reduce_max(
                            rrow[0:1, 40 + hh:41 + hh], pbn[0:1, :], AX.X),
                            r=[ctx.pb_r[6 + hh]], w=[r_rr],
                            tag=f"l1a_nk{u}_{c}_{hh}")
                for hh, (o_d, r_d) in enumerate(dsts):
                    p.emit("dve", lambda e, hh=hh, o_d=o_d, pbq=pbq: e.tensor_copy(
                        ctx.ab.sb[0:64, o_d:o_d + 512],
                        pbq[hh * 64:(hh + 1) * 64, :]),
                        r=[ctx.pb_r[4]], w=[r_d], tag=f"l1a_{what}ev{u}_{c}_{hh}")

            proj("k", D, ((o_ka, r_ka), (o_kb, r_kb)), False)
            proj("q", 0, ((o_qa, r_qa), (o_qb, r_qb)), True)
            for hh in range(2):
                o_q65 = (o_qa, o_qb)[hh]
                r_q65 = (r_qa, r_qb)[hh]
                o_nq = (o_nqa, o_nqb)[hh]
                r_nq = (r_nqa, r_nqb)[hh]
                p.emit("dve", lambda e, hh=hh, o_nq=o_nq: e.tensor_scalar_mul(
                    ctx.af.sb[0:1, o_nq:o_nq + 512],
                    ctx.af.sb[0:1, o_nq:o_nq + 512],
                    rrow[0:1, 40 + hh:41 + hh]),
                    r=[r_nq, r_rr], w=[r_nq], tag=f"l1a_bm{u}_{c}_{hh}",
                    selfsync=True)
                p.emit("act", lambda e, o_nq=o_nq: e.activation(
                    ctx.af.sb[0:1, o_nq:o_nq + 512],
                    ctx.af.sb[0:1, o_nq:o_nq + 512], AF.Sqrt),
                    r=[r_nq], w=[r_nq], tag=f"l1a_bq{u}_{c}_{hh}")
                p.emit("dve", lambda e, o_nq=o_nq, o_q65=o_q65: e.tensor_scalar(
                    ctx.ab.sb[64:65, o_q65:o_q65 + 512],
                    ctx.af.sb[0:1, o_nq:o_nq + 512], -1.0, SH / s,
                    OP.mult, OP.add),
                    r=[r_nq], w=[r_q65], tag=f"l1a_bw{u}_{c}_{hh}")

            for hh in range(2) if 'scores' in L1A_PARTS else []:
                h = 2 * c + hh
                o_k65 = (o_ka, o_kb)[hh]
                r_k65 = (r_ka, r_kb)[hh]
                o_q65 = (o_qa, o_qb)[hh]
                r_q65 = (r_qa, r_qb)[hh]
                av_bank, av_r = ctx.pb[5], ctx.pb_r[5]
                for wi in range(4):
                    sb, sbr = ctx.pb[wi % 2], ctx.pb_r[wi % 2]
                    p.emit("pe", lambda e, wi=wi, sb=sb, o_k65=o_k65,
                           o_q65=o_q65: e.matmul(
                        sb[:, 0:128],
                        ctx.ab.sb[0:65, o_k65 + wi * 128: o_k65 + (wi + 1) * 128],
                        ctx.ab.sb[0:65, o_q65 + wi * 128: o_q65 + (wi + 1) * 128],
                        start=True, stop=False),
                        r=[r_k65, r_q65], w=[sbr], tag=f"l1a_sc{u}_{h}_{wi}")
                    p.emit("pe", lambda e, sb=sb: e.matmul(
                        sb[:, 0:128], ctx.mk_l1[:], ctx.id_b[:],
                        start=False, stop=True),
                        r=[ctx.mk_l1_r, ctx.id_b_r], w=[sbr],
                        tag=f"l1a_mk{u}_{h}_{wi}")
                    p.emit("act", lambda e, sb=sb: e.activation(
                        ctx.ab.sb[:, o_e:o_e + 128], sb[:, 0:128],
                        AF.Exp, scale=s),
                        r=[sbr], w=[r_e], tag=f"l1a_e{u}_{h}_{wi}")
                    if 'av' in L1A_PARTS:
                        p.emit("pe", lambda e, wi=wi, av_bank=av_bank, h=h: e.matmul(
                            av_bank[0:65, wi * 128: wi * 128 + 128],
                            ctx.ab.sb[0:128,
                                      o_vt + wi * VT + h * 65:
                                      o_vt + wi * VT + h * 65 + 65],
                            ctx.ab.sb[:, o_e:o_e + 128],
                            start=(wi == 0), stop=(wi == 3)),
                            r=[r_vt, r_e], w=[av_r],
                            tag=f"l1a_av{u}_{h}_{wi}")
                if 'recip' not in L1A_PARTS:
                    continue
                p.emit("dve", lambda e, av_bank=av_bank: e.tensor_copy(
                    rrow[0:1, 128:640], av_bank[64:65, :]),
                    r=[av_r], w=[r_rr], tag=f"l1a_s{u}_{h}")
                p.emit("act", lambda e: e.activation(
                    rrow[0:1, 128:640], rrow[0:1, 128:640], AF.Ln),
                    r=[r_rr], w=[r_rr], tag=f"l1a_sl{u}_{h}")
                p.emit("act", lambda e: e.activation(
                    rrow[0:1, 128:640], rrow[0:1, 128:640], AF.Exp, scale=-1.0),
                    r=[r_rr], w=[r_rr], tag=f"l1a_sx{u}_{h}")
                pb7b = ctx.pb[7]
                p.emit("pe", lambda e, pb7b=pb7b: e.matmul(
                    pb7b[:, :], ctx.ones_row_f[:], rrow[0:1, 128:640],
                    start=True, stop=True),
                    r=[r_rr, ctx.ones_row_f_r], w=[ctx.pb_r[7]],
                    tag=f"l1a_sr{u}_{h}")
                p.emit("dve", lambda e, pb7b=pb7b: e.tensor_copy(
                    recb[0:64, :], pb7b[0:64, :]),
                    r=[ctx.pb_r[7]], w=[r_rec], tag=f"l1a_se{u}_{h}")

                def hev(e, hb=hh * 64, u=u, c=c, av_bank=av_bank):
                    return e.tensor_mul(
                        ctx.heads[hb:hb + 64,
                                  c * T + u * 512: c * T + (u + 1) * 512],
                        av_bank[0:64, :], recb[0:64, :])
                p.emit("dve", hev, r=[av_r, r_rec], w=[ctx.heads_r[c]],
                       tag=f"l1a_h{u}_{h}")

        for oc in range(8) if 'oproj' in L1A_PARTS else []:
            pb, pbr = ctx.pb[2 + oc % 2], ctx.pb_r[2 + oc % 2]
            for dc in range(8):
                p.emit("pe", lambda e, oc=oc, dc=dc, u=u, pb=pb: e.matmul(
                    pb[:],
                    ctx.ab.sl(o_wo + dc * 1024 + oc * 128, 128),
                    ctx.heads[:, dc * T + u * 512: dc * T + (u + 1) * 512],
                    start=(dc == 0), stop=False),
                    r=[r_wo, ctx.heads_r[dc]], w=[pbr],
                    tag=f"l1a_o{u}_{oc}_{dc}")
            brow, brr, boff = bo
            p.emit("pe", lambda e, oc=oc, pb=pb, brow=brow, boff=boff: e.matmul(
                pb[:], brow[0:1, boff + oc * 128: boff + (oc + 1) * 128],
                ctx.ones_512_b[:], start=False, stop=True),
                r=[brr, ctx.ones_512_b_r], w=[pbr], tag=f"l1a_ob{u}_{oc}")
            mb = 0
            for (rs, rl) in rngs:
                p.emit("dve", lambda e, oc=oc, rs=rs, rl=rl, mb=mb, pb=pb: e.tensor_add(
                    ctx.xT[:, oc * T + rs: oc * T + rs + rl],
                    ctx.xT[:, oc * T + rs: oc * T + rs + rl],
                    pb[:, mb:mb + rl]),
                    r=[pbr, ctx.xT_r[oc]], w=[ctx.xT_r[oc]],
                    tag=f"l1a_oe{u}_{oc}_{mb}")
                mb += rl


def emit_l1_ffn(ctx, reg):
    p = ctx.p
    gw_t = reg.dram_in("L1gw", [D, 8],
                       lambda inp: np.ascontiguousarray(np.asarray(inp["L1_ffn_gate_w"]).T))
    wg_t = reg.dram_in("L1wg", [D, 4096],
                       lambda inp: np.ascontiguousarray(np.asarray(inp["L1_ffn_w_gate"]).T))
    wu_t = reg.dram_in("L1wu", [D, 4096],
                       lambda inp: np.ascontiguousarray(np.asarray(inp["L1_ffn_w_up"]).T))
    wd_t = reg.dram_in("L1wd", [4096, D],
                       lambda inp: np.ascontiguousarray(np.asarray(inp["L1_ffn_w_down"]).T))
    a1c = reg.dram_in("L1a1", [D, 128],
                      lambda inp: np.ascontiguousarray(
                          np.asarray(inp["L1_ffn_A1"]).transpose(2, 0, 1).reshape(D, 128)))
    a3c = reg.dram_in("L1a3", [D, 128],
                      lambda inp: np.ascontiguousarray(
                          np.asarray(inp["L1_ffn_A3"]).transpose(2, 0, 1).reshape(D, 128)))
    b1c = reg.dram_in("L1b1", [16, 8 * 4096],
                      lambda inp: np.ascontiguousarray(
                          np.asarray(inp["L1_ffn_B1"]).transpose(2, 0, 1).reshape(16, 8 * 4096)))
    b3c = reg.dram_in("L1b3", [16, 8 * 4096],
                      lambda inp: np.ascontiguousarray(
                          np.asarray(inp["L1_ffn_B3"]).transpose(2, 0, 1).reshape(16, 8 * 4096)))
    a2c = reg.dram_in("L1a2", [4096, 128],
                      lambda inp: np.ascontiguousarray(
                          np.asarray(inp["L1_ffn_A2"]).transpose(2, 0, 1).reshape(4096, 128)))
    b2c = reg.dram_in("L1b2", [16, 8 * D],
                      lambda inp: np.ascontiguousarray(
                          np.asarray(inp["L1_ffn_B2"]).transpose(2, 0, 1).reshape(16, 8 * D)))
    ctx.ab.reset()
    ctx.af.reset()
    nE = 8
    o_z, r_z = ctx.af.take(2048, "z")
    o_gt, r_gt = emit_router(ctx, reg, "l1r_", gw_t, nE, None, True,
                             logits_carve=(o_z, r_z))
    o_us, _r_us_d = ctx.af.take(1024, "us")
    r_us2 = [Res("l1f_us0"), Res("l1f_us1")]
    for r_ in r_us2:
        r_.readers = list(ctx.af.carry)
        ctx.af.live.append(r_)
    o_wgu, r_wgu = ctx.ab.take(8192, "wgu_b2")
    o_wd, r_wd = ctx.ab.take(1024, "wdblk")
    o_b13, _r_b13_d = ctx.ab.take(2048, "b13blk")
    r_b1 = [Res("l1f_b1_0"), Res("l1f_b1_1")]
    r_b3 = [Res("l1f_b3_0"), Res("l1f_b3_1")]
    o_ag, _r_ag_d = ctx.ab.take(1024, "ag")
    r_ag2 = [Res("l1f_ag0"), Res("l1f_ag1")]
    o_mx, _r_mx_d = ctx.ab.take(1024, "mix")
    r_mx2 = [Res("l1f_mx0"), Res("l1f_mx1")]
    for r_ in r_b1 + r_b3 + r_ag2 + r_mx2:
        r_.readers = list(ctx.ab.carry)
        ctx.ab.live.append(r_)
    o_a2, r_a2 = ctx.ab.take(512, "a2blk")
    o_g, r_g = ctx.ab.take(nE * 256, "G")
    o_bc, r_bc = ctx.ab.take(128, "bcast")
    o_t13, r_t13 = ctx.ab.take(4096, "t13_zg")
    o_c1, r_c1 = ctx.ab.take(1024, "c1")
    o_c3, r_c3 = ctx.ab.take(1024, "c3")

    def t_slot(si, e_):
        return o_t13 + (si * 8 + e_) * 256

    for tb in range(8):
        emit_g_block(ctx, "l1g_", o_gt, r_gt, nE, tb, o_g, r_g, o_bc, r_bc)
        load_w_dc_slice(ctx, o_wgu, r_wgu, a1c, 0, 128, ctx.ab, f"l1f_la1{tb}")
        load_w_dc_slice(ctx, o_wgu + 1024, r_wgu, a3c, 0, 128, ctx.ab,
                        f"l1f_la3{tb}")
        for si in range(2):
            for e_ in range(8):
                pbi = 6 + (si * 8 + e_) % 2
                pb, pbr = ctx.pb[pbi], ctx.pb_r[pbi]
                for dc in range(8):
                    p.emit("pe", lambda e, dc=dc, tb=tb, si=si, e_=e_, pb=pb: e.matmul(
                        pb[0:16, 0:256],
                        bass.AP(ctx.ab.sb, o_wgu + si * 1024 + dc * 128 + e_ * 16,
                                [[ctx.ab.cols, 128], [1, 16]]),
                        ctx.xn[:, dc * T + tb * 256: dc * T + (tb + 1) * 256],
                        start=(dc == 0), stop=(dc == 7)),
                        r=[r_wgu, ctx.xn_r[dc]], w=[pbr],
                        tag=f"l1f_t{tb}_{si}_{e_}_{dc}")
                p.emit("dve", lambda e, si=si, e_=e_, pb=pb: e.tensor_scalar_mul(
                    ctx.ab.sb[0:16, t_slot(si, e_): t_slot(si, e_) + 256],
                    pb[0:16, 0:256], 2.0),
                    r=[pbr], w=[r_t13], tag=f"l1f_te{tb}_{si}_{e_}")
        p.emit("dve", lambda e: e.memset(
            ctx.af.sb[0:16, o_z:o_z + 2048], 0.0), w=[r_z], tag=f"l1f_z0{tb}")
        for fb in range(8):
            load_w_dc_slice(ctx, o_wgu, r_wgu, wg_t, fb * 512, 512, ctx.ab,
                            f"l1f_lg{tb}_{fb}")
            load_w_dc_slice(ctx, o_wgu + 4096, r_wgu, wu_t, fb * 512, 512,
                            ctx.ab, f"l1f_lu{tb}_{fb}")

            def lda2(e, fb=fb):
                dst = bass.AP(ctx.ab.sb, o_a2,
                              [[ctx.ab.cols, 128], [128, 4], [1, 128]])
                src = bass.AP(a2c, fb * 512 * 128,
                              [[128, 128], [128 * 128, 4], [1, 128]])
                return e.dma_start(out=dst, in_=src)
            p.emit("pool", lda2, w=[r_a2], dma=True, tag=f"l1f_la2{tb}_{fb}")
            for si, (oc_, rc_) in enumerate(((o_c1, r_c1), (o_c3, r_c3))):
                for fc in range(4):
                    pb, pbr = ctx.pb[4 + si], ctx.pb_r[4 + si]
                    for dc in range(8):
                        p.emit("pe", lambda e, dc=dc, fc=fc, si=si, tb=tb,
                               pb=pb: e.matmul(
                            pb[:, (fc % 2) * 256:(fc % 2) * 256 + 256],
                            ctx.ab.sl(o_wgu + si * 4096 + dc * 512 + fc * 128, 128),
                            ctx.xn[:, dc * T + tb * 256: dc * T + (tb + 1) * 256],
                            start=(dc == 0 and fc % 2 == 0),
                            stop=(dc == 7 and fc % 2 == 1)),
                            r=[r_wgu, ctx.xn_r[dc]], w=[pbr],
                            tag=f"l1f_c{tb}_{fb}_{si}_{fc}_{dc}")
                    if fc % 2 == 1:
                        p.emit("dve", lambda e, fc=fc, oc_=oc_, rc_=rc_, pb=pb:
                               e.tensor_copy(
                            ctx.ab.sl(oc_ + (fc - 1) * 256, 512), pb[:, 0:512]),
                            r=[pbr], w=[rc_], tag=f"l1f_ce{tb}_{fb}_{si}_{fc}")
            for e_ in range(8):
                par = e_ % 2

                def ldb1(eng, e_=e_, fb=fb, par=par):
                    dst = bass.AP(ctx.ab.sb, o_b13 + par * 1024,
                                  [[ctx.ab.cols, 16], [1, 512]])
                    src = bass.AP(b1c, e_ * 4096 + fb * 512,
                                  [[8 * 4096, 16], [1, 512]])
                    return eng.dma_start(out=dst, in_=src)
                p.emit("pool", ldb1, w=[r_b1[par]], dma=True,
                       tag=f"l1f_lb1{tb}_{fb}_{e_}")

                def ldb3(eng, e_=e_, fb=fb, par=par):
                    dst = bass.AP(ctx.ab.sb, o_b13 + par * 1024 + 512,
                                  [[ctx.ab.cols, 16], [1, 512]])
                    src = bass.AP(b3c, e_ * 4096 + fb * 512,
                                  [[8 * 4096, 16], [1, 512]])
                    return eng.dma_start(out=dst, in_=src)
                p.emit("pool", ldb3, w=[r_b3[par]], dma=True,
                       tag=f"l1f_lb3{tb}_{fb}_{e_}")
                for fp in range(2):
                    pd1, pd1r = ctx.pb[4 + 2 * fp], ctx.pb_r[4 + 2 * fp]
                    pd3, pd3r = ctx.pb[5 + 2 * fp], ctx.pb_r[5 + 2 * fp]
                    p.emit("pe", lambda e, fp=fp, pd1=pd1: e.matmul(
                        pd1[:, 0:512], ctx.id_b[:],
                        ctx.ab.sl(o_c1 + fp * 512, 512),
                        start=True, stop=False),
                        r=[ctx.id_b_r, r_c1], w=[pd1r],
                        tag=f"l1f_ic1_{tb}_{fb}_{e_}_{fp}")
                    for fc in (2 * fp, 2 * fp + 1):
                        p.emit("pe", lambda e, fc=fc, e_=e_, pd1=pd1, par=par: e.matmul(
                            pd1[:, (fc % 2) * 256:(fc % 2) * 256 + 256],
                            ctx.ab.sb[0:16, o_b13 + par * 1024 + fc * 128:
                                      o_b13 + par * 1024 + (fc + 1) * 128],
                            ctx.ab.sb[0:16, t_slot(0, e_): t_slot(0, e_) + 256],
                            start=False, stop=(fc % 2 == 1)),
                            r=[r_b1[par], r_t13], w=[pd1r],
                            tag=f"l1f_d1_{tb}_{fb}_{e_}_{fc}")
                    p.emit("pe", lambda e, fp=fp, pd3=pd3: e.matmul(
                        pd3[:, 0:512], ctx.id_b[:],
                        ctx.ab.sl(o_c3 + fp * 512, 512),
                        start=True, stop=False),
                        r=[ctx.id_b_r, r_c3], w=[pd3r],
                        tag=f"l1f_ic3_{tb}_{fb}_{e_}_{fp}")
                    for fc in (2 * fp, 2 * fp + 1):
                        p.emit("pe", lambda e, fc=fc, e_=e_, pd3=pd3, par=par: e.matmul(
                            pd3[:, (fc % 2) * 256:(fc % 2) * 256 + 256],
                            ctx.ab.sb[0:16, o_b13 + par * 1024 + 512 + fc * 128:
                                      o_b13 + par * 1024 + 512 + (fc + 1) * 128],
                            ctx.ab.sb[0:16, t_slot(1, e_): t_slot(1, e_) + 256],
                            start=False, stop=(fc % 2 == 1)),
                            r=[r_b3[par], r_t13], w=[pd3r],
                            tag=f"l1f_d3_{tb}_{fb}_{e_}_{fc}")
                for fp in range(2):
                    pd1, pd1r = ctx.pb[4 + 2 * fp], ctx.pb_r[4 + 2 * fp]
                    pd3, pd3r = ctx.pb[5 + 2 * fp], ctx.pb_r[5 + 2 * fp]
                    p.emit("act", lambda e, fp=fp, pd1=pd1: e.activation(
                        ctx.af.sl(o_us + fp * 512, 512), pd1[:, 0:512], AF.Silu),
                        r=[pd1r], w=[r_us2[fp]], tag=f"l1f_s{tb}_{fb}_{e_}_{fp}")
                    p.emit("dve", lambda e, fp=fp, pd3=pd3: e.tensor_mul(
                        ctx.ab.sl(o_ag + fp * 512, 512),
                        ctx.af.sl(o_us + fp * 512, 512), pd3[:, 0:512]),
                        r=[r_us2[fp], pd3r], w=[r_ag2[fp]],
                        tag=f"l1f_ag{tb}_{fb}_{e_}_{fp}")
                    for fci in range(2):
                        p.emit("dve", lambda e, fci=fci, e_=e_, fp=fp: e.tensor_mul(
                            ctx.ab.sl(o_ag + fp * 512 + fci * 256, 256),
                            ctx.ab.sl(o_ag + fp * 512 + fci * 256, 256),
                            ctx.ab.sl(o_g + e_ * 256, 256)),
                            r=[r_ag2[fp], r_g], w=[r_ag2[fp]],
                            tag=f"l1f_vg{tb}_{fb}_{e_}_{fp}_{fci}")
                    if e_ == 0:
                        p.emit("dve", lambda e, fp=fp: e.tensor_copy(
                            ctx.ab.sl(o_mx + fp * 512, 512),
                            ctx.ab.sl(o_ag + fp * 512, 512)),
                            r=[r_ag2[fp]], w=[r_mx2[fp]],
                            tag=f"l1f_mx{tb}_{fb}_{e_}_{fp}")
                    else:
                        p.emit("dve", lambda e, fp=fp: e.tensor_add(
                            ctx.ab.sl(o_mx + fp * 512, 512),
                            ctx.ab.sl(o_mx + fp * 512, 512),
                            ctx.ab.sl(o_ag + fp * 512, 512)),
                            r=[r_ag2[fp], r_mx2[fp]], w=[r_mx2[fp]],
                            tag=f"l1f_mx{tb}_{fb}_{e_}_{fp}")
                    for fc in (2 * fp, 2 * fp + 1):
                        p.emit("pe", lambda e, fc=fc, e_=e_, fp=fp, pd3=pd3: e.matmul(
                            pd3[0:16, 0:256],
                            bass.AP(ctx.ab.sb, o_a2 + fc * 128 + e_ * 16,
                                    [[ctx.ab.cols, 128], [1, 16]]),
                            ctx.ab.sl(o_ag + fp * 512 + (fc % 2) * 256, 256),
                            start=(fc % 2 == 0), stop=(fc % 2 == 1)),
                            r=[r_a2, r_ag2[fp]], w=[pd3r],
                            tag=f"l1f_zm{tb}_{fb}_{e_}_{fc}")
                    p.emit("dve", lambda e, e_=e_, pd3=pd3: e.tensor_add(
                        ctx.af.sb[0:16, o_z + e_ * 256: o_z + (e_ + 1) * 256],
                        ctx.af.sb[0:16, o_z + e_ * 256: o_z + (e_ + 1) * 256],
                        pd3[0:16, 0:256]),
                        r=[pd3r, r_z], w=[r_z], tag=f"l1f_ze{tb}_{fb}_{e_}_{fp}")
            for dhalf in range(2):
                for q in range(2):
                    def ldwd(e, fb=fb, dhalf=dhalf, q=q):
                        dst = bass.AP(ctx.ab.sb, o_wd,
                                      [[ctx.ab.cols, 128], [512, 2], [1, 512]])
                        src = bass.AP(wd_t, (fb * 512 + q * 256) * D + dhalf * 512,
                                      [[D, 128], [D * 128, 2], [1, 512]])
                        return e.dma_start(out=dst, in_=src)
                    p.emit("pool", ldwd, w=[r_wd], dma=True,
                           tag=f"l1f_lwd{tb}_{fb}_{dhalf}_{q}")
                    for dc in range(4):
                        dcc = dhalf * 4 + dc
                        ybank, ybr = ctx.pb[dcc // 2], ctx.pb_r[dcc // 2]
                        ycol = (dcc % 2) * 256
                        for fc in (2 * q, 2 * q + 1):
                            p.emit("pe", lambda e, dc=dc, fc=fc, ybank=ybank,
                                   ycol=ycol,
                                   st=(fb == 0 and fc == 0 and dcc % 2 == 0): e.matmul(
                                ybank[:, ycol:ycol + 256],
                                ctx.ab.sl(o_wd + (fc % 2) * 512 + dc * 128, 128),
                                ctx.ab.sl(o_mx + fc * 256, 256),
                                start=st, stop=False),
                                r=[r_wd, r_mx2[fc // 2]], w=[ybr],
                                tag=f"l1f_y{tb}_{fb}_{dcc}_{fc}")
        def ldb2(e):
            dst = bass.AP(ctx.ab.sb, o_wgu, [[ctx.ab.cols, 16], [1, 8192]])
            src = bass.AP(b2c, 0, [[8 * D, 16], [1, 8192]])
            return e.dma_start(out=dst, in_=src)
        p.emit("pool", ldb2, w=[r_wgu], dma=True, tag=f"l1f_lb2{tb}")
        for e_ in range(8):
            p.emit("dve", lambda e, e_=e_: e.tensor_scalar_mul(
                ctx.ab.sb[0:16, o_t13 + e_ * 256: o_t13 + (e_ + 1) * 256],
                ctx.af.sb[0:16, o_z + e_ * 256: o_z + (e_ + 1) * 256], 2.0),
                r=[r_z], w=[r_t13], tag=f"l1f_zg{tb}_{e_}")
        for dc in range(8):
            ybank, ybr = ctx.pb[dc // 2], ctx.pb_r[dc // 2]
            ycol = (dc % 2) * 256
            for e_ in range(8):
                p.emit("pe", lambda e, dc=dc, e_=e_, ybank=ybank, ycol=ycol,
                       lst=(e_ == 7 and dc % 2 == 1): e.matmul(
                    ybank[:, ycol:ycol + 256],
                    bass.AP(ctx.ab.sb, o_wgu + e_ * D + dc * 128,
                            [[ctx.ab.cols, 16], [1, 128]]),
                    ctx.ab.sb[0:16, o_t13 + e_ * 256: o_t13 + (e_ + 1) * 256],
                    start=False, stop=lst),
                    r=[r_wgu, r_t13], w=[ybr], tag=f"l1f_d2_{tb}_{dc}_{e_}")
        for dc in range(8):
            ybank, ybr = ctx.pb[dc // 2], ctx.pb_r[dc // 2]
            ycol = (dc % 2) * 256
            p.emit("dve", lambda e, dc=dc, tb=tb, ybank=ybank, ycol=ycol: e.tensor_add(
                ctx.xT[:, dc * T + tb * 256: dc * T + (tb + 1) * 256],
                ctx.xT[:, dc * T + tb * 256: dc * T + (tb + 1) * 256],
                ybank[:, ycol:ycol + 256]),
                r=[ybr, ctx.xT_r[dc]], w=[ctx.xT_r[dc]], tag=f"l1f_ye{tb}_{dc}")


def emit_l1(ctx, reg):
    emit_ln(ctx, "l1n_")
    emit_l1_attn(ctx, reg)
    emit_l1_ffn(ctx, reg)


LAYER_EMITTERS = {0: emit_l0, 1: emit_l1, 2: emit_l2, 3: emit_l3}


def build_program(layers, hook=None):
    nc = bass.Bass("TRN2", target_bir_lowering=False)
    p = P(nc)
    reg = Registry(p)
    x_ap = nc.dram_tensor("x", [T, D], F32, kind="ExternalInput").ap()
    out_ap = nc.dram_tensor("out", [T, D], F32, kind="ExternalOutput").ap()
    ctx = Ctx(p, reg)
    emit_load_transpose_in(ctx, x_ap)
    if hook:
        hook(ctx, reg, "after_load")
    for li in layers:
        LAYER_EMITTERS[li](ctx, reg)
        if hook:
            hook(ctx, reg, f"after_l{li}")
    emit_transpose_out_store(ctx, out_ap)
    if hook:
        hook(ctx, reg, "end")
    p.finalize_and_emit()
    return nc, reg, ctx


def kernel(**inputs):
    layers_groups = [[0, 1, 2, 3]]
    x = np.ascontiguousarray(inputs["L0_attn_x"], dtype=np.float32)
    cur = [np.ascontiguousarray(x[c]) for c in range(NCORE)]
    for group in layers_groups:
        nc, reg, _ = build_program(group)
        host = {}
        for name, fn in reg.host.items():
            arr = fn(inputs) if callable(fn) else fn
            host[name] = np.ascontiguousarray(arr, dtype=np.float32)
        in_maps = [dict(host, x=cur[c]) for c in range(NCORE)]
        res = run_bass_kernel_spmd(nc, in_maps, core_ids=list(range(NCORE)))
        cur = [np.asarray(res.results[c]["out"]) for c in range(NCORE)]
    return np.stack(cur, 0)
